# version 21
# baseline (speedup 1.0000x reference)
import math
from contextlib import ExitStack

import numpy as np

import concourse.bass as bass
import concourse.mybir as mybir
from concourse.bass_utils import run_bass_kernel_spmd

F32 = mybir.dt.float32
BF16 = mybir.dt.bfloat16
I32 = mybir.dt.int32
AF = mybir.ActivationFunctionType
ALU = mybir.AluOpType

B, T, D = 4, 8192, 2048
H, E = 16, 128
HID = 5632
CH = 128
TPC = T // 2
NT = 8
TT = TPC // NT
NCH = TT // CH
NKC = D // 128
NJG = HID // 128
NJGP = 48
NFB = D // 512
NKD = NJG
NKDP = 48
PI = float(np.pi)
TWO_PI = float(2.0 * np.pi)
RND_M = 12582912.0

LQ = 16 * NKC * 128
LV = 4 * NKC * 512
LP = NFB * H * 512
LG = NJGP * NKC * 128
LDN = NKDP * NFB * 512

ENGS = ("sync", "scalar", "vector", "tensor", "gpsimd")


def _slopes():
    start = 2.0 ** (-(2.0 ** (-(math.log2(H) - 3))))
    return np.array([start ** (i + 1) for i in range(H)], dtype=np.float32)


class Sem:
    def __init__(self, es, nc, name):
        self.h = es.enter_context(nc.semaphore(name))
        self.n = 0


class Prog:

    TRACE = []

    def __init__(self):
        self.q = {e: [] for e in ENGS}

    def wait(self, eng, sem, n=None):
        n = sem.n if n is None else n
        if n > 0:
            Prog.TRACE.append((eng, "wait", sem.h.name, n))
            self.q[eng].append(lambda e, h=sem.h, n=n: e.wait_ge(h, n))

    def dma(self, eng, sem, out, in_, **kw):
        def f(e, out=out, in_=in_, kw=kw, h=sem.h):
            e.dma_start(out=out, in_=in_, **kw).then_inc(h, 16)
        Prog.TRACE.append((eng, "inc", sem.h.name, 16))
        self.q[eng].append(f)
        sem.n += 16
        return sem.n

    def op(self, eng, sem, fn):
        if sem is None:
            Prog.TRACE.append((eng, "op", None, 0))
            self.q[eng].append(fn)
            return None
        Prog.TRACE.append((eng, "inc", sem.h.name, 1))
        self.q[eng].append(lambda e, fn=fn, h=sem.h: fn(e).then_inc(h, 1))
        sem.n += 1
        return sem.n

    def run(self, nc, name):
        with nc.Block(name) as blk:
            for eng in ENGS:
                fns = self.q[eng]
                if fns:
                    getattr(blk, eng)(lambda e, fns=fns: [f(e) for f in fns])


def build_program(debug=(), phases=("w", "p1", "p2a", "p2b")):
    nc = bass.Bass("TRN2", target_bir_lowering=False)
    es = ExitStack()

    dbg_pairs = []

    def dram(name, shape, dt, kind="Internal"):
        th = nc.dram_tensor(name, shape, dt, kind=kind)
        if kind == "Internal" and name in debug:
            dbg_pairs.append(
                (nc.dram_tensor("dbg_" + name, shape, dt,
                                kind="ExternalOutput"), th))
        return th

    def sbt(name, shape, dt, stack=None):
        return (stack or es).enter_context(nc.sbuf_tensor(name, shape, dt))

    def pst(name, shape, dt, stack=None):
        return (stack or es).enter_context(nc.psum_tensor(name, shape, dt))

    def inp(name, shape, dt):
        return nc.dram_tensor(name, shape, dt, kind="ExternalInput")

    x_in = inp("x", [TPC, D], F32)
    wq_sh = inp("wq_sh", [256, D], F32)
    wk_sh = inp("wk_sh", [256, D], F32)
    wv_sh = inp("wv_sh", [512, 1024], F32)
    wkt_sh = inp("wkt_sh", [512, 1024], F32)
    wp_sh = inp("wp_sh", [512, 1024], F32)
    wg_sh = inp("wg_sh", [768, D], I32)
    wu_sh = inp("wu_sh", [768, D], I32)
    wd_sh = inp("wd_sh", [D, 768], I32)
    sg_t = inp("sg_t", [128, NJG], F32)
    su_t = inp("su_t", [128, NJG], F32)
    sd_row = inp("sd_row", [1, D], F32)
    th_tm_in = inp("theta_tm", [1, D], F32)
    th_t_in = inp("theta_t", [128, H], F32)
    ident_in = inp("ident", [128, 128], BF16)
    ones_col_in = inp("ones_col", [128, 1], BF16)
    ones_row_in = inp("ones_row", [1, 128], F32)
    iota_fm_in = inp("iota_fm", [128, D], F32)
    iota_col_in = inp("iota_col", [128, 1], F32)
    pos_col_in = inp("pos_col", [128, 1], F32)
    mask_col_in = inp("mask_col", [128, 1], F32)
    qd_fm_in = inp("qd_fm", [128, D], F32)
    kd_tm_in = inp("kd_tm", [128, D], F32)
    ddt_in = inp("ddt", [128, D], F32)
    out_t = nc.dram_tensor("out", [TPC, D], F32, kind="ExternalOutput")

    pre_loc, pre_all = {}, {}
    for nm, L in (("sq", LQ), ("sk", LQ), ("sv", LV), ("skt", LV), ("sp", LP),
                  ("sg", LG), ("su", LG), ("sd", LDN)):
        pre_loc[nm] = dram("pre_%s_loc" % nm, [128, L // 8], BF16)
        pre_all[nm] = dram("pre_%s_all" % nm, [1024, L // 8], BF16)
    x_bf = dram("x_bf", [TPC, D], BF16)
    xnfm_sp = dram("xnfm_sp", [128, NT * NKC * TT], BF16)
    kfm_sp = dram("kfm_sp", [128, 32 * D], BF16)
    vtm_sp = dram("vtm_sp", [128, 32 * D], BF16)
    st_sp = dram("st_sp", [128, 32 * H * 256], BF16)
    ktc_sp = dram("ktc_sp", [128, 32 * D], BF16)
    o_sp = dram("o_sp", [128, NT * H * TT], BF16)
    sT_sp = dram("sT_sp", [128, 32 * D], BF16)
    qp_sp = dram("qp_sp", [128, 32 * D], BF16)
    kp_sp = dram("kp_sp", [128, 32 * D], BF16)
    trc_sp = dram("trc_sp", [128, 32 * D], BF16)
    trs_sp = dram("trs_sp", [128, 32 * D], BF16)
    st_loc = dram("st_loc", [128, H * 256], F32)
    st_pair = dram("st_pair", [256, H * 256], F32)
    partial = dram("partial", [TPC, D], BF16)

    ident = sbt("c_ident", [128, 128], BF16)
    ones_col = sbt("c_ones_col", [128, 1], BF16)
    ones_row = sbt("c_ones_row", [1, 128], F32)
    iota_fm = sbt("c_iota_fm", [128, D], F32)
    iota_col = sbt("c_iota_col", [128, 1], F32)
    pos_col = sbt("c_pos_col", [128, 1], F32)
    mask_col = sbt("c_mask_col", [128, 1], F32)
    theta_tm = sbt("c_theta_tm", [128, D], F32)
    theta_t = sbt("c_theta_t", [128, H], F32)
    half_pi = sbt("c_half_pi", [128, 1], F32)
    c_n1 = sbt("c_n1", [128, 1], F32)

    sSY = Sem(es, nc, "sSY")
    sSC = Sem(es, nc, "sSC")
    sGQ = Sem(es, nc, "sGQ")
    sPE = Sem(es, nc, "sPE")
    sDV = Sem(es, nc, "sDV")
    sAC = Sem(es, nc, "sAC")
    sPO = Sem(es, nc, "sPO")
    sAG = {k: Sem(es, nc, "sAG" + k) for k in
           ("sk", "sv", "skt", "sq", "sp", "sg", "su", "sd", "st")}
    ring_sems = [(Sem(es, nc, "sR%da" % i), Sem(es, nc, "sR%db" % i),
                  Sem(es, nc, "sR%dc" % i)) for i in range(3)]
    sXB = Sem(es, nc, "sXB")
    sXC = Sem(es, nc, "sXC")
    sXN = Sem(es, nc, "sXN")
    sST = Sem(es, nc, "sST")
    sSPx = Sem(es, nc, "sSPx")
    sSPk = Sem(es, nc, "sSPk")
    sSPv = Sem(es, nc, "sSPv")
    sSPs = Sem(es, nc, "sSPs")
    sSTP = Sem(es, nc, "sSTP")
    sPW = Sem(es, nc, "sPW")
    sLK = [Sem(es, nc, "sLK%d" % i) for i in range(2)]
    sLV = [Sem(es, nc, "sLV%d" % i) for i in range(2)]
    sLO = [Sem(es, nc, "sLO%d" % i) for i in range(2)]
    sRP = [Sem(es, nc, "sRP%d" % i) for i in range(2)]
    sRX = [Sem(es, nc, "sRX%d" % i) for i in range(2)]
    sOU = [Sem(es, nc, "sOU%d" % i) for i in range(2)]

    slopes = _slopes()
    bdec = [float(np.exp(np.float64(-s) * CH))
            for s in slopes.astype(np.float64)]

    def stream_runs(nm, c0, c1):
        Lc = pre_all[nm].shape[1]
        runs, c = [], c0
        while c < c1:
            e = min(c1, (c // Lc + 1) * Lc)
            runs.append((c, e))
            c = e
        return runs

    def stream_ap(nm, c0, c1):
        Lc = pre_all[nm].shape[1]
        s = c0 // Lc
        return pre_all[nm].ap()[s * 128:(s + 1) * 128, c0 - s * Lc:c1 - s * Lc]

    class Ring:
        def __init__(self, name, nm, ncols, stack, sems, nbuf=2):
            self.nm, self.ncols, self.nbuf = nm, ncols, nbuf
            self.sems = sems
            self.bufs = [sbt("%s%d" % (name, i), [128, ncols], BF16, stack)
                         for i in range(nbuf)]
            self.loaded = []
            self.freed = []
            self.issue_i = 0
            self.cons_i = 0
            self.take_i = 0

        def issue(self, P, c0):
            i = self.issue_i % self.nbuf
            if self.issue_i >= self.nbuf:
                sem, n = self.freed[self.issue_i - self.nbuf]
                P.wait("sync", sem, n)
            for (a, b) in stream_runs(self.nm, c0, c0 + self.ncols):
                P.dma("sync", self.sems[i], self.bufs[i][:, a - c0:b - c0],
                      stream_ap(self.nm, a, b))
            self.loaded.append((i, self.sems[i].n))
            self.issue_i += 1

        def use(self, P, eng="tensor"):
            i, n = self.loaded[self.take_i]
            self.take_i += 1
            P.wait(eng, self.sems[i], n)
            return self.bufs[i]

        def release(self, sem):
            self.freed.append((sem, sem.n))
            self.cons_i += 1

        def start(self, P, seq):
            self.seq = list(seq)
            self.si = 0
            for i in range(min(self.nbuf, len(self.seq))):
                self.issue(P, self.seq[i])

        def take(self, P, eng="tensor"):
            return self.use(P, eng)

        def done(self, P, sem):
            self.release(sem)
            nxt = self.si + self.nbuf
            if nxt < len(self.seq):
                self.issue(P, self.seq[nxt])
            self.si += 1

    ctx = {"n_xbf": 0}

    def _phase_w():
      with ExitStack() as stw:
        stag = sbt("w_stag", [128, 12288], BF16, stw)
        rowf = [sbt("w_rowf%d" % i, [128, D], F32, stw) for i in range(2)]
        rowi = [sbt("w_rowi%d" % i, [128, D], I32, stw) for i in range(2)]
        rowb = [sbt("w_rowb%d" % i, [128, D], BF16, stw) for i in range(2)]
        ptr = pst("w_ptr", [128, 8192], BF16, stw)
        P = Prog()
        for dst, src in ((ident, ident_in), (ones_col, ones_col_in),
                         (ones_row, ones_row_in), (iota_fm, iota_fm_in),
                         (iota_col, iota_col_in), (pos_col, pos_col_in),
                         (mask_col, mask_col_in), (theta_t, th_t_in)):
            P.dma("sync", sSY, dst[:], src.ap())
        P.dma("sync", sSY, theta_tm[:], th_tm_in.ap().broadcast_to([128, D]))
        P.op("vector", sDV, lambda v: v.memset(half_pi[:], PI / 2.0))
        P.op("vector", sDV, lambda v: v.memset(c_n1[:], -1.0))
        nc.const_aps.register(F32, -1.0, c_n1[:, 0:1])
        P.wait("tensor", sSY)

        for t in range(NT):
            P.dma("gpsimd", sXC, x_bf.ap()[t * TT:(t + 1) * TT, :],
                  x_in.ap()[t * TT:(t + 1) * TT, :])
        ctx["n_xbf"] = sXC.n

        jobs = [
            ("sk", [(wk_sh.ap()[j*128:(j+1)*128, :], False, NKC,
                     (lambda kc, j=j: j*NKC + kc)) for j in range(2)], 2*NKC),
            ("sv", [(wv_sh.ap()[j*128:(j+1)*128, :], False, 8,
                     (lambda kc, j=j: kc*4 + j)) for j in range(4)], 32),
            ("skt", [(wkt_sh.ap()[j*128:(j+1)*128, :], False, 8,
                      (lambda kc, j=j: kc*4 + j)) for j in range(4)], 32),
            ("sq", [(wq_sh.ap()[j*128:(j+1)*128, :], False, NKC,
                     (lambda kc, j=j: j*NKC + kc)) for j in range(2)], 2*NKC),
            ("sp", [(wp_sh.ap()[j*128:(j+1)*128, :], False, 8,
                     (lambda kc, j=j: kc*4 + j)) for j in range(4)], 32),
            ("sg", [(wg_sh.ap()[j*128:(j+1)*128, :], True, NKC,
                     (lambda kc, j=j: j*NKC + kc)) for j in range(6)], 6*NKC),
            ("su", [(wu_sh.ap()[j*128:(j+1)*128, :], True, NKC,
                     (lambda kc, j=j: j*NKC + kc)) for j in range(6)], 6*NKC),
            ("sd", [(wd_sh.ap()[j*128:(j+1)*128, :], True, 6,
                     (lambda kc, j=j: kc*16 + j)) for j in range(16)], 96),
        ]
        ag_ready = {}
        pi = 0
        cv_done = {}
        tr_done = {}
        for nm, parts, nblk in jobs:
            n_store_prev = sST.n
            for srcap, is_int, nkc, scol in parts:
                cols = srcap.shape[1]
                par = pi % 2
                rbuf = rowi[par] if is_int else rowf[par]
                rb = rowb[par]
                P.wait("scalar", sDV, cv_done.get(pi - 2, 0))
                nld = P.dma("scalar", sRX[par], rbuf[:, 0:cols], srcap)
                P.wait("vector", sRX[par], nld)
                P.wait("vector", sPE, tr_done.get(pi - 2, 0))
                P.op("vector", sDV, lambda v, rbuf=rbuf, rb=rb, cols=cols,
                     a=(-128.0 if is_int else 0.0):
                     v.tensor_scalar(out=rb[:, 0:cols], in0=rbuf[:, 0:cols],
                                     scalar1=a, scalar2=None, op0=ALU.add))
                cv_done[pi] = sDV.n
                P.wait("tensor", sDV, cv_done[pi])
                for k0 in range(0, nkc, 8):
                    nb = min(8, nkc - k0)
                    P.wait("tensor", sDV, sDV.n)
                    for i in range(nb):
                        P.op("tensor", sPE if i == nb - 1 else None,
                             lambda t_, kc=k0 + i, i=i, rb=rb: t_.transpose(
                                 ptr[:, i*1024:i*1024 + 128],
                                 rb[:, kc*128:(kc+1)*128], ident[:]))
                    P.wait("vector", sPE, sPE.n)
                    P.wait("vector", sST, n_store_prev)

                    def ev(v, k0=k0, nb=nb, scol=scol):
                        r = None
                        for i in range(nb):
                            r = v.tensor_copy(
                                out=stag[:, scol(k0+i)*128:(scol(k0+i)+1)*128],
                                in_=bass.AP(ptr, i*1024, [[8192, 128], [1, 128]]))
                        return r
                    P.op("vector", sDV, ev)
                tr_done[pi] = sPE.n
                pi += 1
            P.wait("sync", sDV, sDV.n)
            P.dma("sync", sST, pre_loc[nm].ap()[:, 0:nblk*128],
                  stag[:, 0:nblk*128])
            ag_ready[nm] = sST.n
            P.wait("gpsimd", sST, ag_ready[nm])
            Prog.TRACE.append(("gpsimd", "inc", "sAG" + nm, 1))
            P.op("gpsimd", None, lambda g, nm=nm: g.collective_compute(
                "AllGather", ALU.bypass,
                replica_groups=[[0, 1, 2, 3, 4, 5, 6, 7]],
                ins=[pre_loc[nm].ap()], outs=[pre_all[nm].ap()],
            ).then_inc(sAG[nm].h, 1))
            sAG[nm].n = 1
        P.run(nc, "phW")

    if "w" in phases:
        _phase_w()

    def _phase_p1():
      with ExitStack() as st1:
        xf = [sbt("p1_xf%d" % i, [128, NKC * TT], BF16, st1)
              for i in range(2)]
        sq_sc = [sbt("p1_sq%d" % i, [128, 512], BF16, st1) for i in range(2)]
        row_s = sbt("p1_rows", [1, 4 * TT], F32, st1)
        k_fm = sbt("p1_kfm", [128, NCH * D], BF16, st1)
        v_tm = sbt("p1_vtm", [128, NCH * D], BF16, st1)
        k_tm = sbt("p1_ktm", [128, NCH * D], BF16, st1)
        kkd = sbt("p1_kkd", [128, D], BF16, st1)
        ktc = [sbt("p1_ktc%d" % i, [128, D], BF16, st1) for i in range(2)]
        kts = [sbt("p1_kts%d" % i, [128, D], BF16, st1) for i in range(2)]
        trc = sbt("p1_trc", [128, D], BF16, st1)
        trs = sbt("p1_trs", [128, D], BF16, st1)
        ph_b = sbt("p1_ph", [128, D], F32, st1)
        rr_b = sbt("p1_rr", [128, D], F32, st1)
        kd_tm = sbt("p1_kdtm", [128, D], BF16, st1)
        states = sbt("p1_st", [128, H * 256], F32, st1)
        st_bf = sbt("p1_stbf", [128, H * 256], BF16, st1)
        tcol = sbt("p1_tcol", [128, 1], F32, st1)
        c2pi = sbt("p1_c2pi", [128, 1], F32, st1)
        p_row = pst("p1_prow", [1, 1024], F32, st1)
        p_bc = pst("p1_pbc", [128, 1024], F32, st1)
        p_a = pst("p1_pa", [128, 2048], F32, st1)
        rk = Ring("p1_rk", "sk", 2048, st1, ring_sems[0], nbuf=3)
        rv = Ring("p1_rv", "sv", 2048, st1, ring_sems[1])
        rkt = Ring("p1_rkt", "skt", 2048, st1, ring_sems[2])

        P = Prog()
        P.dma("gpsimd", sGQ, kd_tm[:], kd_tm_in.ap())
        n_kdtm = sGQ.n
        P.op("vector", sDV, lambda v: v.memset(states[:], 0.0))
        P.op("vector", sDV, lambda v: v.memset(c2pi[:], -TWO_PI))
        P.wait("sync", sAG["sk"], 1)
        P.wait("sync", sAG["sv"], 1)
        P.wait("sync", sAG["skt"], 1)
        rk.start(P, [j * 2048 for _ in range(NT) for j in range(H)])
        seq_v = [r * 2048 for _ in range(NT) for r in range(16)]
        rv.start(P, seq_v)
        rkt.start(P, seq_v)

        st_c = {"spill": 0, "upd": 0, "mm_done": {}}
        sq_done = {}

        def xbar(t, bf):
            P.wait("sync", sXC, ctx["n_xbf"])
            for kc in range(NKC):
                P.dma("sync", sXB, bf[:, kc*TT:(kc+1)*TT],
                      x_bf.ap()[t*TT:(t+1)*TT, kc*128:(kc+1)*128],
                      transpose=True)
            return sXB.n

        def layer_norm(t, bf, n_xfm, n_spill_xn):
            P.wait("tensor", sXB, n_xfm)
            P.wait("tensor", sDV, sDV.n)
            for kc in range(NKC):
                P.wait("scalar", sXB, n_xfm)
                if kc >= 2:
                    P.wait("scalar", sPE, sq_done[kc - 2])
                elif sq_done.get(NKC - 2 + kc):
                    P.wait("scalar", sPE, sq_done[NKC - 2 + kc])
                P.op("scalar", sAC, lambda s, kc=kc, bf=bf: s.activation(
                    out=sq_sc[kc % 2][:], in_=bf[:, kc*TT:(kc+1)*TT],
                    func=AF.Square))
                P.op("tensor", None, lambda t_, kc=kc, bf=bf: t_.matmul(
                    p_row[0:1, 0:TT], ones_col[:], bf[:, kc*TT:(kc+1)*TT],
                    start=(kc == 0), stop=(kc == NKC - 1)))
                P.wait("tensor", sAC, sAC.n)
                sq_done[kc] = P.op("tensor", sPE, lambda t_, kc=kc: t_.matmul(
                    p_row[0:1, 512:512 + TT], ones_col[:], sq_sc[kc % 2][:],
                    start=(kc == 0), stop=(kc == NKC - 1)))
            P.wait("vector", sPE, sPE.n)
            P.op("vector", sDV, lambda v: v.tensor_scalar(
                out=row_s[0:1, 0:TT], in0=p_row[0:1, 0:TT],
                scalar1=-1.0 / D, scalar2=None, op0=ALU.mult))
            P.op("vector", sDV, lambda v: v.tensor_tensor(
                out=row_s[0:1, 1024:1024 + TT], in0=row_s[0:1, 0:TT],
                in1=row_s[0:1, 0:TT], op=ALU.mult))
            P.op("vector", sDV, lambda v: v.tensor_scalar(
                out=row_s[0:1, 1536:1536 + TT], in0=p_row[0:1, 512:512 + TT],
                scalar1=1.0 / D, scalar2=None, op0=ALU.mult))
            P.op("vector", sDV, lambda v: v.tensor_tensor(
                out=row_s[0:1, 1024:1024 + TT], in0=row_s[0:1, 1536:1536 + TT],
                in1=row_s[0:1, 1024:1024 + TT], op=ALU.subtract))
            P.op("vector", sDV, lambda v: v.tensor_scalar(
                out=row_s[0:1, 1024:1024 + TT], in0=row_s[0:1, 1024:1024 + TT],
                scalar1=1e-5, scalar2=None, op0=ALU.add))
            P.wait("scalar", sDV, sDV.n)
            P.op("scalar", sAC, lambda s: s.activation(
                out=row_s[0:1, 1536:1536 + TT], in_=row_s[0:1, 1024:1024 + TT],
                func=AF.Sqrt))
            P.wait("vector", sAC, sAC.n)
            P.op("vector", sDV, lambda v: v.reciprocal(
                out=row_s[0:1, 512:512 + TT], in_=row_s[0:1, 1536:1536 + TT]))
            n_rows = sDV.n
            P.wait("tensor", sDV, n_rows)
            P.op("tensor", None, lambda t_: t_.matmul(
                p_bc[:, 0:512], ones_row[:], row_s[0:1, 0:TT],
                start=True, stop=True))
            P.op("tensor", sPE, lambda t_: t_.matmul(
                p_bc[:, 512:1024], ones_row[:], row_s[0:1, 512:512 + TT],
                start=True, stop=True))
            P.wait("vector", sPE, sPE.n)
            P.wait("vector", sSPx, n_spill_xn)
            nm_rep = bass.AP(p_bc, 0, [[1024, 128], [0, NKC], [1, TT]])
            rs_rep = bass.AP(p_bc, 512, [[1024, 128], [0, NKC], [1, TT]])
            xn3 = bf[:].rearrange("p (kc t) -> p kc t", kc=NKC)
            P.op("vector", sDV, lambda v: v.tensor_tensor(
                out=xn3, in0=xn3, in1=nm_rep, op=ALU.add))
            P.wait("vector", sDV, sDV.n)
            P.op("vector", sDV, lambda v: v.tensor_tensor(
                out=xn3, in0=xn3, in1=rs_rep, op=ALU.mult))
            n_xn = sDV.n
            P.wait("sync", sDV, n_xn)
            return n_xn, P.dma("sync", sSPx,
                               xnfm_sp.ap()[:, t*NKC*TT:(t+1)*NKC*TT], bf[:])

        def chain(t, c, pe_done_before):
            gt0 = t * TT + c * CH
            par = c % 2
            P.wait("vector", sPO, st_c.get("ph_rd", 0))
            P.op("vector", sDV, lambda v, gt0=gt0: v.tensor_scalar(
                out=tcol[:], in0=iota_col[:], scalar1=pos_col[:, 0:1],
                scalar2=float(gt0), op0=ALU.add, op1=ALU.add))
            P.wait("vector", sDV, sDV.n)
            P.wait("gpsimd", sDV, sDV.n)
            P.wait("gpsimd", sAC, st_c.get("tr_rd", 0))
            P.wait("gpsimd", sGQ, n_kdtm)
            P.op("gpsimd", sPO, lambda g: g.tensor_tensor(
                out=ph_b[:], in0=theta_tm[:],
                in1=tcol[:].broadcast_to([128, D]), op=ALU.mult))
            n_ph = sPO.n
            P.wait("scalar", sPO, n_ph)
            P.op("scalar", sAC, lambda s: s.activation(
                out=rr_b[:], in_=ph_b[:], func=AF.Copy,
                scale=1.0 / TWO_PI, bias=RND_M))
            P.op("scalar", sAC, lambda s: s.activation(
                out=rr_b[:], in_=rr_b[:], func=AF.Copy, bias=-RND_M))
            n_rr = sAC.n
            P.wait("gpsimd", sAC, n_rr)
            P.op("gpsimd", sPO, lambda g: g.tensor_tensor(
                out=rr_b[:], in0=rr_b[:],
                in1=c2pi[:].broadcast_to([128, D]), op=ALU.mult))
            P.wait("gpsimd", sPO, sPO.n)
            P.op("gpsimd", sPO, lambda g: g.tensor_tensor(
                out=rr_b[:], in0=rr_b[:], in1=ph_b[:], op=ALU.add))
            st_c["ph_rd"] = sPO.n
            P.wait("scalar", sPO, sPO.n)
            P.op("scalar", sAC, lambda s: s.activation(
                out=trs[:], in_=rr_b[:], func=AF.Sin))
            P.op("scalar", sAC, lambda s: s.activation(
                out=rr_b[:], in_=rr_b[:], func=AF.Abs))
            P.op("scalar", sAC, lambda s: s.activation(
                out=trc[:], in_=rr_b[:], func=AF.Sin,
                scale=-1.0, bias=half_pi[:, 0:1]))
            st_c["tr_rd"] = 0
            n_cos = sAC.n
            P.wait("gpsimd", sAC, n_cos)
            P.wait("gpsimd", sPE, pe_done_before)
            P.op("gpsimd", sPO, lambda g, c=c: g.tensor_tensor(
                out=kkd[:], in0=k_tm[:, c*D:(c+1)*D], in1=kd_tm[:],
                op=ALU.mult))
            P.wait("gpsimd", sPO, sPO.n)
            P.op("gpsimd", sPO, lambda g, par=par: g.tensor_tensor(
                out=ktc[par][:], in0=kkd[:], in1=trc[:], op=ALU.mult))
            P.op("gpsimd", sPO, lambda g, par=par: g.tensor_tensor(
                out=kts[par][:], in0=kkd[:], in1=trs[:], op=ALU.mult))
            st_c["tr_rd"] = sPO.n
            return sPO.n

        def scan_mms(t, c, n_pool, n_vev):
            par = c % 2
            P.wait("vector", sSPs, st_c["spill"])
            P.op("vector", sDV,
                 lambda v: v.tensor_copy(out=st_bf[:], in_=states[:]))
            P.wait("sync", sDV, sDV.n)
            st_c["spill"] = P.dma(
                "sync", sSPs,
                st_sp.ap()[:, (t*NCH + c)*H*256:(t*NCH + c + 1)*H*256],
                st_bf[:])
            for grp in range(4):
                P.wait("tensor", sPO, n_pool)
                P.wait("tensor", sDV, st_c["upd"])
                P.wait("tensor", sDV, n_vev)
                for hh in range(4):
                    h = grp * 4 + hh
                    P.op("tensor", None,
                         lambda t_, h=h, hh=hh, c=c, par=par:
                         t_.matmul(
                             p_a[:, 1024 + hh*256:1024 + hh*256 + 128],
                             ktc[par][:, h*128:(h+1)*128],
                             v_tm[:, c*D + h*128:c*D + (h+1)*128],
                             start=(hh % 2 == 0), stop=False))
                    P.op("tensor", sPE if hh == 3 else None,
                         lambda t_, h=h, hh=hh, c=c, par=par:
                         t_.matmul(
                             p_a[:, 1024 + hh*256 + 128:1024 + (hh+1)*256],
                             kts[par][:, h*128:(h+1)*128],
                             v_tm[:, c*D + h*128:c*D + (h+1)*128],
                             start=False, stop=(hh % 2 == 1)))
                P.wait("vector", sPE, sPE.n)
                for hh in range(4):
                    h = grp * 4 + hh
                    P.op("vector", sDV if hh == 3 else None,
                         lambda v, h=h, hh=hh:
                         v.scalar_tensor_tensor(
                             out=states[:, h*256:(h+1)*256],
                             in0=states[:, h*256:(h+1)*256],
                             scalar=bdec[h],
                             in1=p_a[:, 1024 + hh*256:1024 + (hh+1)*256],
                             op0=ALU.mult, op1=ALU.add))
                st_c["upd"] = sDV.n
            st_c["mm_done"][t * NCH + c] = sPE.n

        n_spill_k = n_spill_v = 0
        spx = {0: 0, 1: 0}
        prod_end = {-1: 0, -2: 0}
        n_xn_t = {}
        n_xfm = xbar(0, xf[0])
        n_xn_t[0], spx[0] = layer_norm(0, xf[0], n_xfm, 0)
        for t in range(NT):
            cur = xf[t % 2]
            n_xn = n_xn_t[t]
            if t + 1 < NT:
                nb = xf[(t + 1) % 2]
                P.wait("sync", sPE, prod_end[t - 1])
                P.wait("sync", sSPx, spx[(t + 1) % 2])
                n_xfm2 = xbar(t + 1, nb)
            P.wait("tensor", sDV, n_xn)
            P.wait("vector", sSPv, n_spill_v)
            P.wait("vector", sPO, sPO.n)
            for ring, dst in ((rkt, k_tm), (rv, v_tm)):
                for fv in range(4):
                    P.wait("tensor", sDV, sDV.n)
                    for qtr in range(4):
                        buf = ring.take(P)
                        for kci in range(4):
                            kc = qtr * 4 + kci
                            for c in range(NCH):
                                P.op("tensor",
                                     sPE if (kci == 3 and c == NCH - 1) else None,
                                     lambda t_, kc=kc, kci=kci, c=c, buf=buf,
                                     cur=cur:
                                     t_.matmul(
                                         p_a[:, c*512:(c+1)*512],
                                         cur[:, kc*TT + c*128:kc*TT + (c+1)*128],
                                         buf[:, kci*512:(kci+1)*512],
                                         start=(kc == 0), stop=(kc == NKC - 1)))
                        ring.done(P, sPE)
                    P.wait("vector", sPE, sPE.n)
                    P.op("vector", sDV, lambda v, fv=fv, dst=dst: v.tensor_copy(
                        out=bass.AP(dst, fv * 512,
                                    [[NCH * D, 128], [D, NCH], [1, 512]]),
                        in_=bass.AP(p_a, 0, [[2048, 128], [512, NCH], [1, 512]])))
            n_vev = sDV.n
            P.wait("sync", sDV, n_vev)
            n_spill_v = P.dma("sync", sSPv,
                              vtm_sp.ap()[:, t*NCH*D:(t+1)*NCH*D], v_tm[:])
            if t + 1 < NT:
                n_xn_t[t + 1], spx[(t + 1) % 2] = layer_norm(
                    t + 1, nb, n_xfm2, spx[(t + 1) % 2])
            P.wait("vector", sSPk, n_spill_k)
            chains = {}
            evk = {}
            for j in range(H):
                buf = rk.take(P)
                P.wait("tensor", sDV, evk.get(j - 2, n_vev))
                for kc in range(NKC):
                    P.op("tensor", sPE if kc == NKC - 1 else None,
                         lambda t_, j=j, kc=kc, buf=buf, cur=cur: t_.matmul(
                             p_a[:, (j % 2)*512:(j % 2)*512 + 512],
                             buf[:, kc*128:(kc+1)*128],
                             cur[:, kc*TT:(kc+1)*TT],
                             start=(kc == 0), stop=(kc == NKC - 1)))
                rk.done(P, sPE)
                P.wait("vector", sPE, sPE.n)
                P.op("vector", sDV, lambda v, j=j: v.tensor_copy(
                    out=bass.AP(k_fm, j * 128,
                                [[NCH * D, 128], [D, NCH], [1, 128]]),
                    in_=bass.AP(p_a, (j % 2)*512,
                                [[2048, 128], [128, NCH], [1, 128]])))
                evk[j] = sDV.n
                gci0 = t * NCH
                if j == 0:
                    chains[0] = chain(t, 0, st_c["mm_done"].get(gci0 - 2, 0))
                    chains[1] = chain(t, 1, st_c["mm_done"].get(gci0 - 1, 0))
                if j == 7:
                    scan_mms(t, 0, chains[0], n_vev)
                    chains[2] = chain(t, 2, st_c["mm_done"][gci0])
                if j == 11:
                    scan_mms(t, 1, chains[1], n_vev)
                    chains[3] = chain(t, 3, st_c["mm_done"][gci0 + 1])
                if j == 15:
                    scan_mms(t, 2, chains[2], n_vev)
            P.wait("sync", sDV, sDV.n)
            n_spill_k = P.dma("sync", sSPk,
                              kfm_sp.ap()[:, t*NCH*D:(t+1)*NCH*D], k_fm[:])
            prod_end[t] = sPE.n
            scan_mms(t, 3, chains[3], n_vev)
        P.wait("sync", sDV, sDV.n)
        n_stl = P.dma("sync", sSPs, st_loc.ap(), states[:])
        P.wait("gpsimd", sSPs, n_stl)
        Prog.TRACE.append(("gpsimd", "inc", "sAGst", 1))
        P.op("gpsimd", None, lambda g: g.collective_compute(
            "AllGather", ALU.bypass,
            replica_groups=[[0, 1], [2, 3], [4, 5], [6, 7]],
            ins=[st_loc.ap()], outs=[st_pair.ap()],
        ).then_inc(sAG["st"].h, 1))
        sAG["st"].n = 1
        P.run(nc, "phP1")

    if "p1" in phases:
        _phase_p1()

    def _phase_p2a():
      with ExitStack() as st2:
        xn_fm = sbt("p2_xnfm", [128, NKC * TT], BF16, st2)
        q_fm = sbt("p2_qfm", [128, NCH * D], BF16, st2)
        k_ch = [sbt("p2_kch%d" % i, [128, D], BF16, st2) for i in range(2)]
        v_ch = [sbt("p2_vch%d" % i, [128, D], BF16, st2) for i in range(2)]
        qpc = [sbt("p2_qpc%d" % i, [128, D], BF16, st2) for i in range(2)]
        qps = [sbt("p2_qps%d" % i, [128, D], BF16, st2) for i in range(2)]
        kpc = [sbt("p2_kpc%d" % i, [128, D], BF16, st2) for i in range(2)]
        kps = [sbt("p2_kps%d" % i, [128, D], BF16, st2) for i in range(2)]
        trc = [sbt("p2_trc%d" % i, [128, D], BF16, st2) for i in range(2)]
        trs = [sbt("p2_trs%d" % i, [128, D], BF16, st2) for i in range(2)]
        sttr = [sbt("p2_sttr%d" % i, [128, H * 256], BF16, st2)
                for i in range(2)]
        sTb = [sbt("p2_sT%d" % i, [128, 512], BF16, st2) for i in range(2)]
        ph_b = sbt("p2_ph", [128, D], F32, st2)
        rr_b = sbt("p2_rr", [128, D], F32, st2)
        qd_fm = sbt("p2_qdfm", [128, D], BF16, st2)
        ddt = sbt("p2_ddt", [128, D], BF16, st2)
        stin = sbt("p2_stin", [128, H * 256], BF16, st2)
        o_t = sbt("p2_ot", [128, H * TT], BF16, st2)
        pout = sbt("p2_pout", [128, 512], BF16, st2)
        tmp_o = sbt("p2_tmpo", [128, 512], BF16, st2)
        cM = sbt("p2_cM", [128, 1], F32, st2)
        cMn = sbt("p2_cMn", [128, 1], F32, st2)
        cI2P = sbt("p2_cI2P", [128, 1], F32, st2)
        cM4 = sbt("p2_cM4", [128, 1], F32, st2)
        p_q = pst("p2_pq", [128, 1024], F32, st2)
        p_s = pst("p2_ps", [128, 1024], F32, st2)
        p_o = pst("p2_po", [128, 1024], F32, st2)
        p_p = pst("p2_pp", [128, 1024], F32, st2)
        rq = Ring("p2_rq", "sq", 2048, st2, ring_sems[0])
        rp = Ring("p2_rp", "sp", 4096, st2, ring_sems[1])

        P = Prog()
        P.dma("gpsimd", sGQ, qd_fm[:], qd_fm_in.ap())
        P.dma("gpsimd", sGQ, ddt[:], ddt_in.ap())
        n_c2 = sGQ.n
        P.op("vector", sDV, lambda v: v.memset(cM[:], RND_M))
        P.op("vector", sDV, lambda v: v.memset(cMn[:], -RND_M))
        P.op("vector", sDV, lambda v: v.memset(cI2P[:], 1.0 / TWO_PI))
        P.op("vector", sDV, lambda v: v.memset(cM4[:], RND_M + 0.25))
        nc.const_aps.register(F32, RND_M, cM[:, 0:1])
        nc.const_aps.register(F32, -RND_M, cMn[:, 0:1])
        nc.const_aps.register(F32, 1.0 / TWO_PI, cI2P[:, 0:1])
        nc.const_aps.register(F32, RND_M + 0.25, cM4[:, 0:1])
        P.wait("gpsimd", sAG["st"], 1)
        nstin = P.dma("gpsimd", sSTP, stin[:], st_pair.ap()[0:128, :])
        P.wait("vector", sSTP, nstin)
        P.op("vector", sDV, lambda v: v.tensor_scalar(
            out=stin[:], in0=stin[:], scalar1=mask_col[:, 0:1], scalar2=None,
            op0=ALU.mult))
        P.wait("sync", sAG["sq"], 1)
        P.wait("sync", sAG["sp"], 1)
        rq.start(P, [j * 2048 for _ in range(NT) for j in range(H)])
        rp.start(P, [r * 4096 for _ in range(NT) for r in range(8)])

        def chain(ci, pe_done_before):
            t, c = divmod(ci, NCH)
            par = ci % 2
            gt0 = t * TT + c * CH
            P.wait("sync", sPE, pe_done_before)
            P.wait("sync", sDV, sDV.n)
            n_k = P.dma("sync", sLK[par], k_ch[par][:],
                        kfm_sp.ap()[:, ci*D:(ci+1)*D])
            n_v = P.dma("sync", sLV[par], v_ch[par][:],
                        vtm_sp.ap()[:, ci*D:(ci+1)*D])
            n_lo = P.dma("sync", sLO[par], sttr[par][:],
                         st_sp.ap()[:, ci*H*256:(ci+1)*H*256])
            P.wait("vector", sAC, sAC.n)
            P.wait("vector", sPO, sPO.n)
            P.op("vector", sDV, lambda v, gt0=gt0: v.tensor_scalar(
                out=rr_b[:], in0=iota_fm[:], scalar1=pos_col[:, 0:1],
                scalar2=float(gt0), op0=ALU.add, op1=ALU.add))
            P.wait("vector", sDV, sDV.n)
            th_rep = theta_t[:].unsqueeze(2).broadcast_to([128, H, 128])
            P.op("vector", sDV, lambda v, th_rep=th_rep: v.tensor_tensor(
                out=ph_b[:].rearrange("p (h t) -> p h t", h=H),
                in0=rr_b[:].rearrange("p (h t) -> p h t", h=H),
                in1=th_rep, op=ALU.mult))
            n_ph = sDV.n
            P.wait("scalar", sDV, n_ph)
            P.op("scalar", sAC, lambda s: s.activation(
                out=rr_b[:], in_=ph_b[:], func=AF.Copy,
                scale=1.0 / TWO_PI, bias=RND_M))
            P.op("scalar", sAC, lambda s: s.activation(
                out=rr_b[:], in_=rr_b[:], func=AF.Copy, bias=-RND_M))
            n_rr = sAC.n
            P.wait("vector", sAC, n_rr)
            P.op("vector", sDV, lambda v: v.scalar_tensor_tensor(
                out=rr_b[:], in0=rr_b[:], scalar=-TWO_PI, in1=ph_b[:],
                op0=ALU.mult, op1=ALU.add))
            P.wait("scalar", sDV, sDV.n)
            P.wait("scalar", sPE, pe_done_before)
            P.op("scalar", sAC, lambda s, par=par: s.activation(
                out=trs[par][:], in_=rr_b[:], func=AF.Sin))
            n_sin = sAC.n
            P.op("scalar", sAC, lambda s: s.activation(
                out=rr_b[:], in_=ph_b[:], func=AF.Copy,
                scale=1.0 / TWO_PI, bias=RND_M + 0.25))
            P.op("scalar", sAC, lambda s: s.activation(
                out=rr_b[:], in_=rr_b[:], func=AF.Copy, bias=-RND_M))
            n_rr2 = sAC.n
            P.wait("vector", sAC, n_rr2)
            P.op("vector", sDV, lambda v: v.scalar_tensor_tensor(
                out=rr_b[:], in0=rr_b[:], scalar=-TWO_PI, in1=ph_b[:],
                op0=ALU.mult, op1=ALU.add))
            P.wait("scalar", sDV, sDV.n)
            P.op("scalar", sAC, lambda s, par=par: s.activation(
                out=trc[par][:], in_=rr_b[:], func=AF.Sin,
                bias=half_pi[:, 0:1]))
            n_cos = sAC.n
            P.wait("gpsimd", sAC, n_cos)
            P.wait("gpsimd", sLK[par], n_k)
            P.wait("gpsimd", sPE, pe_done_before)
            P.op("gpsimd", sPO, lambda g, par=par: g.tensor_tensor(
                out=kpc[par][:], in0=k_ch[par][:], in1=trc[par][:],
                op=ALU.mult))
            P.op("gpsimd", sPO, lambda g, par=par: g.tensor_tensor(
                out=kps[par][:], in0=k_ch[par][:], in1=trs[par][:],
                op=ALU.mult))
            P.wait("vector", sLO[par], n_lo)
            P.wait("vector", sGQ, n_c2)
            P.wait("vector", sPE, pe_done_before)
            for h in range(H):
                P.op("vector", sDV if h == H - 1 else None,
                     lambda v, h=h, m=ci, par=par: v.scalar_tensor_tensor(
                         out=sttr[par][:, h*256:(h+1)*256],
                         in0=stin[:, h*256:(h+1)*256],
                         scalar=float(bdec[h] ** m),
                         in1=sttr[par][:, h*256:(h+1)*256],
                         op0=ALU.mult, op1=ALU.add))
            return [sPO.n, sDV.n, n_v]

        def chain_b(ci, deps, n_qev):
            t, c = divmod(ci, NCH)
            par = ci % 2
            P.wait("gpsimd", sDV, n_qev)
            qsl = bass.AP(q_fm, c * D, [[NCH * D, 128], [1, D]])
            P.op("gpsimd", sPO, lambda g, qsl=qsl, par=par: g.tensor_tensor(
                out=qpc[par][:], in0=qsl, in1=trc[par][:], op=ALU.mult))
            P.op("gpsimd", sPO, lambda g, qsl=qsl, par=par: g.tensor_tensor(
                out=qps[par][:], in0=qsl, in1=trs[par][:], op=ALU.mult))
            deps[0] = sPO.n
            if "qp_sp" in debug:
                P.wait("scalar", sPO, sPO.n)
                P.dma("scalar", sSC, qp_sp.ap()[:, ci*D:(ci+1)*D], qpc[par][:])
                P.dma("scalar", sSC, kp_sp.ap()[:, ci*D:(ci+1)*D], kpc[par][:])
                P.dma("scalar", sSC, trc_sp.ap()[:, ci*D:(ci+1)*D], trc[par][:])
                P.wait("gpsimd", sSC, sSC.n)
                P.wait("vector", sSC, sSC.n)
                P.wait("tensor", sSC, sSC.n)

        mst = {"mdv": 0}

        def mms(ci, deps):
            n_pool, n_sttr, n_v = deps
            t, c = divmod(ci, NCH)
            par = ci % 2
            evd = []
            for grp in range(4):
                gp = grp % 2
                P.wait("tensor", sPO, n_pool)
                P.wait("tensor", sDV, evd[grp - 2] if grp >= 2 else mst["mdv"])
                for hh in range(4):
                    h = grp * 4 + hh
                    P.op("tensor", None, lambda t_, h=h, hh=hh, gp=gp, par=par:
                         t_.matmul(
                             p_s[:, gp*512 + hh*128:gp*512 + (hh+1)*128],
                             kpc[par][:, h*128:(h+1)*128],
                             qpc[par][:, h*128:(h+1)*128],
                             start=(hh == 0), stop=False))
                    P.op("tensor", sPE if hh == 3 else None,
                         lambda t_, h=h, hh=hh, gp=gp, par=par: t_.matmul(
                             p_s[:, gp*512 + hh*128:gp*512 + (hh+1)*128],
                             kps[par][:, h*128:(h+1)*128],
                             qps[par][:, h*128:(h+1)*128],
                             start=False, stop=(hh == 3)))
                P.wait("vector", sPE, sPE.n)
                P.op("vector", sDV, lambda v, grp=grp, gp=gp, par=par:
                     v.tensor_tensor(
                         out=sTb[par][:],
                         in0=bass.AP(p_s, gp*512, [[1024, 128], [1, 512]]),
                         in1=ddt[:, grp*512:(grp+1)*512], op=ALU.mult))
                n_mask = sDV.n
                P.wait("tensor", sDV, n_mask)
                P.wait("tensor", sLV[par], n_v)
                for hh in range(4):
                    h = grp * 4 + hh
                    P.op("tensor", None,
                         lambda t_, h=h, hh=hh, gp=gp, par=par: t_.matmul(
                             p_o[:, gp*512 + hh*128:gp*512 + (hh+1)*128],
                             v_ch[par][:, h*128:(h+1)*128],
                             sTb[par][:, hh*128:(hh+1)*128],
                             start=(hh == 0), stop=False))
                    P.op("tensor", None, lambda t_, h=h, hh=hh, gp=gp, par=par:
                         t_.matmul(
                             p_q[:, gp*512 + hh*128:gp*512 + (hh+1)*128],
                             sttr[par][:, h*256:h*256 + 128],
                             qpc[par][:, h*128:(h+1)*128],
                             start=(hh == 0), stop=False))
                    P.op("tensor", sPE if hh == 3 else None,
                         lambda t_, h=h, hh=hh, gp=gp, par=par: t_.matmul(
                             p_q[:, gp*512 + hh*128:gp*512 + (hh+1)*128],
                             sttr[par][:, h*256 + 128:(h+1)*256],
                             qps[par][:, h*128:(h+1)*128],
                             start=False, stop=(hh == 3)))
                P.wait("vector", sPE, sPE.n)
                P.op("vector", sDV, lambda v, grp=grp, gp=gp: v.tensor_tensor(
                    out=tmp_o[:],
                    in0=bass.AP(p_q, gp*512, [[1024, 128], [1, 512]]),
                    in1=qd_fm[:, grp*512:(grp+1)*512], op=ALU.mult))
                P.wait("vector", sDV, sDV.n)
                P.op("vector", sDV, lambda v, grp=grp, gp=gp, c=c:
                     v.tensor_tensor(
                         out=bass.AP(o_t, grp*4*TT + c*128,
                                     [[H * TT, 128], [TT, 4], [1, 128]]),
                         in0=bass.AP(p_o, gp*512,
                                     [[1024, 128], [128, 4], [1, 128]]),
                         in1=tmp_o[:].rearrange("p (a b) -> p a b", a=4),
                         op=ALU.add))
                evd.append(sDV.n)
            mst["mdv"] = sDV.n

        pend = None
        pend_b = None
        pe_done = [0] * 40
        chained = {}
        qp_end = {-1: 0}
        for t in range(NT):
            P.wait("sync", sPE, qp_end[t - 1])
            P.wait("sync", sSPx, sSPx.n)
            n_xn = P.dma("sync", sXN, xn_fm[:],
                         xnfm_sp.ap()[:, t*NKC*TT:(t+1)*NKC*TT])
            if t == 0:
                chained[0] = chain(0, 0)
                pe_done[0] = sPE.n
            P.wait("tensor", sXN, n_xn)
            P.wait("vector", sPO, sPO.n)
            for j in range(H):
                buf = rq.take(P)
                P.wait("tensor", sDV, sDV.n)
                for kc in range(NKC):
                    P.op("tensor", sPE if kc == NKC - 1 else None,
                         lambda t_, j=j, kc=kc, buf=buf: t_.matmul(
                             p_q[:, (j % 2)*512:(j % 2)*512 + 512],
                             buf[:, kc*128:(kc+1)*128],
                             xn_fm[:, kc*TT:(kc+1)*TT],
                             start=(kc == 0), stop=(kc == NKC - 1)))
                rq.done(P, sPE)
                P.wait("vector", sPE, sPE.n)
                P.op("vector", sDV, lambda v, j=j: v.tensor_copy(
                    out=bass.AP(q_fm, j * 128,
                                [[NCH * D, 128], [D, NCH], [1, 128]]),
                    in_=bass.AP(p_q, (j % 2)*512,
                                [[1024, 128], [128, NCH], [1, 128]])))
            mst["mdv"] = max(mst["mdv"], sDV.n)
            n_qev = sDV.n
            qp_end[t] = sPE.n
            chain_b(t * NCH, chained[t * NCH], n_qev)
            for c in range(NCH):
                ci = t * NCH + c
                nxt = ci + 1
                if nxt < NT * NCH:
                    chained[nxt] = chain(nxt, pe_done[nxt - 2] if nxt >= 2 else 0)
                    pe_done[nxt] = sPE.n
                    if nxt % NCH != 0:
                        chain_b(nxt, chained[nxt], n_qev)
                mms(ci, chained[ci])
                pe_done[ci] = sPE.n
            if "o_sp" in debug:
                P.wait("scalar", sDV, sDV.n)
                P.dma("scalar", sSC, o_sp.ap()[:, t*H*TT:(t+1)*H*TT], o_t[:])
                P.wait("tensor", sSC, sSC.n)
            pvd = {}
            for fb in range(NFB):
                bufs = [rp.take(P), rp.take(P)]
                for c in range(NCH):
                    k2 = fb * NCH + c
                    if k2 >= 2:
                        P.wait("tensor", sAC, pvd[k2 - 2])
                    else:
                        P.wait("tensor", sDV, mst["mdv"])
                    for h in range(H):
                        wbuf = bufs[h // 8]
                        P.op("tensor", sPE if h == H - 1 else None,
                             lambda t_, h=h, c=c, wbuf=wbuf: t_.matmul(
                                 p_p[:, (c % 2)*512:(c % 2)*512 + 512],
                                 o_t[:, h*TT + c*128:h*TT + (c+1)*128],
                                 wbuf[:, (h % 8)*512:((h % 8)+1)*512],
                                 start=(h == 0), stop=(h == H - 1)))
                    P.wait("scalar", sPE, sPE.n)
                    P.wait("scalar", sPW, sPW.n)
                    P.op("scalar", sAC, lambda s, c=c: s.copy(
                        out=pout[:],
                        in_=bass.AP(p_p, (c % 2)*512, [[1024, 128], [1, 512]])))
                    pvd[fb * NCH + c] = sAC.n
                    P.wait("scalar", sAC, sAC.n)
                    P.dma("scalar", sPW,
                          partial.ap()[t*TT + c*128:t*TT + (c+1)*128,
                                       fb*512:(fb+1)*512], pout[:])
                rp.done(P, sPE)
                rp.done(P, sPE)
        P.run(nc, "phP2a")

    if "p2a" in phases:
        _phase_p2a()

    def _phase_p2b():
      with ExitStack() as st3:
        xn_fm = sbt("p3_xnfm", [128, NKC * TT], BF16, st3)
        h_t = sbt("p3_h", [128, NJG * TT], BF16, st3)
        silu_g = sbt("p3_silu", [128, TT], BF16, st3)
        sg_sb = sbt("p3_sg", [128, NJG], F32, st3)
        su_sb = sbt("p3_su", [128, NJG], F32, st3)
        sd_mat = sbt("p3_sd", [128, D], F32, st3)
        par_ch = [sbt("p3_par%d" % i, [128, D], BF16, st3) for i in range(2)]
        x_ch = [sbt("p3_x%d" % i, [128, D], F32, st3) for i in range(2)]
        o_st = [sbt("p3_o%d" % i, [128, D], F32, st3) for i in range(2)]
        xp_ch = [sbt("p3_xp%d" % i, [128, D], BF16, st3) for i in range(2)]
        tmp2 = [sbt("p3_tmp%d" % i, [128, D], BF16, st3) for i in range(2)]
        p8 = pst("p3_p8", [128, 4096], F32, st3)
        rg = Ring("p3_rg", "sg", 4096, st3, ring_sems[0], nbuf=3)
        ru = Ring("p3_ru", "su", 4096, st3, ring_sems[1], nbuf=3)
        rd = Ring("p3_rd", "sd", 4096, st3, ring_sems[2])

        P = Prog()
        P.dma("sync", sSY, sg_sb[:], sg_t.ap())
        P.dma("sync", sSY, su_sb[:], su_t.ap())
        P.dma("sync", sSY, sd_mat[:], sd_row.ap().broadcast_to([128, D]))
        nconst3 = sSY.n
        P.wait("sync", sAG["sg"], 1)
        P.wait("sync", sAG["su"], 1)
        P.wait("sync", sAG["sd"], 1)
        seq_g = [r * 4096 for _ in range(NT) for r in range(22)]
        rg.start(P, seq_g)
        ru.start(P, seq_g)
        rd.start(P, [r * 4096 for _ in range(NT) for _ in range(2)
                     for r in range(22)])

        NRD = 22
        n_out = {0: 0, 1: 0}
        n_par = {0: 0, 1: 0}
        n_x = {0: 0, 1: 0}
        gu_end = {-1: 0}
        xp_rd = {0: 0, 1: 0}
        xp_wr = {0: 0, 1: 0}
        for t in range(NT):
            P.wait("sync", sPE, gu_end[t - 1])
            n_xn = P.dma("sync", sXN, xn_fm[:],
                         xnfm_sp.ap()[:, t*NKC*TT:(t+1)*NKC*TT])
            P.wait("tensor", sXN, n_xn)
            P.wait("scalar", sSY, nconst3)
            P.wait("vector", sPE, sPE.n)
            for j in range(NJG):
                if j % 2 == 0:
                    gbuf = rg.take(P)
                    ubuf = ru.take(P)
                gp = j % 2
                P.wait("tensor", sDV, sDV.n)
                for kc in range(NKC):
                    P.op("tensor", None, lambda t_, j=j, kc=kc, gbuf=gbuf, gp=gp:
                         t_.matmul(p8[:, gp*512:gp*512 + 512],
                                   gbuf[:, (gp*NKC + kc)*128:(gp*NKC + kc + 1)*128],
                                   xn_fm[:, kc*TT:(kc+1)*TT],
                                   start=(kc == 0), stop=(kc == NKC - 1)))
                for kc in range(NKC):
                    P.op("tensor", sPE if kc == NKC - 1 else None,
                         lambda t_, j=j, kc=kc, ubuf=ubuf, gp=gp:
                         t_.matmul(p8[:, 1024 + gp*512:1024 + gp*512 + 512],
                                   ubuf[:, (gp*NKC + kc)*128:(gp*NKC + kc + 1)*128],
                                   xn_fm[:, kc*TT:(kc+1)*TT],
                                   start=(kc == 0), stop=(kc == NKC - 1)))
                P.wait("scalar", sPE, sPE.n)
                P.wait("scalar", sDV, sDV.n)
                P.op("scalar", sAC, lambda s, j=j, gp=gp: s.activation(
                    out=silu_g[:],
                    in_=bass.AP(p8, gp*512, [[4096, 128], [1, 512]]),
                    func=AF.Silu, scale=sg_sb[:, j:j+1]))
                P.wait("vector", sAC, sAC.n)
                P.op("vector", sDV, lambda v, j=j, gp=gp: v.scalar_tensor_tensor(
                    out=h_t[:, j*TT:(j+1)*TT],
                    in0=bass.AP(p8, 1024 + gp*512, [[4096, 128], [1, 512]]),
                    scalar=su_sb[:, j:j+1], in1=silu_g[:],
                    op0=ALU.mult, op1=ALU.mult))
                if j % 2 == 1 or j == NJG - 1:
                    rg.done(P, sPE)
                    ru.done(P, sPE)
            n_h = sDV.n
            gu_end[t] = sPE.n
            for half in range(2):
                cc0 = half * 2
                for cc in range(2):
                    c = cc0 + cc
                    par = c % 2
                    P.wait("scalar", sDV, sDV.n)
                    n_par[par] = P.dma(
                        "scalar", sRP[par], par_ch[par][:],
                        partial.ap()[t*TT + c*128:t*TT + (c+1)*128, :])
                    n_x[par] = P.dma(
                        "scalar", sRX[par], x_ch[par][:],
                        x_in.ap()[t*TT + c*128:t*TT + (c+1)*128, :])
                P.wait("tensor", sDV, sDV.n)
                P.wait("tensor", sDV, n_h)
                for r in range(NRD):
                    dbuf = rd.take(P)
                    for ki in range(2):
                        kc = r * 2 + ki
                        if kc >= NKD:
                            continue
                        for cc in range(2):
                            c = cc0 + cc
                            for fb in range(NFB):
                                last = ((ki == 1 or kc == NKD - 1)
                                        and fb == NFB - 1)
                                P.op("tensor",
                                     sPE if (last and cc == 1) else None,
                                     lambda t_, kc=kc, fb=fb, c=c, cc=cc,
                                     ki=ki, dbuf=dbuf:
                                     t_.matmul(
                                         p8[:, (cc*4 + fb)*512:(cc*4 + fb + 1)*512],
                                         h_t[:, kc*TT + c*128:kc*TT + (c+1)*128],
                                         dbuf[:, (ki*4 + fb)*512:(ki*4 + fb + 1)*512],
                                         start=(kc == 0), stop=(kc == NKD - 1)))
                    rd.done(P, sPE)
                n_dn = sPE.n
                for cc in range(2):
                    c = cc0 + cc
                    par = c % 2
                    P.wait("gpsimd", sRX[par], n_x[par])
                    P.wait("gpsimd", sRP[par], n_par[par])
                    P.wait("gpsimd", sPO, xp_rd[par])
                    P.op("gpsimd", sPO, lambda g, par=par: g.tensor_tensor(
                        out=xp_ch[par][:], in0=x_ch[par][:],
                        in1=par_ch[par][:], op=ALU.add))
                    xp_wr[par] = sPO.n
                P.wait("vector", sPE, n_dn)
                for cc in range(2):
                    c = cc0 + cc
                    par = c % 2
                    P.wait("vector", sOU[par], n_out[par])
                    for fb in range(NFB):
                        P.op("vector", sDV, lambda v, fb=fb, cc=cc, par=par:
                             v.tensor_tensor(
                                 out=tmp2[par][:, fb*512:(fb+1)*512],
                                 in0=p8[:, (cc*4 + fb)*512:(cc*4 + fb + 1)*512],
                                 in1=sd_mat[:, fb*512:(fb+1)*512], op=ALU.mult))
                    nmul = sDV.n
                    P.wait("gpsimd", sDV, nmul)
                    P.wait("gpsimd", sPO, xp_wr[par])
                    P.op("gpsimd", sPO, lambda g, par=par: g.tensor_tensor(
                        out=o_st[par][:], in0=tmp2[par][:],
                        in1=xp_ch[par][:], op=ALU.add))
                    xp_rd[par] = sPO.n
                    P.wait("scalar", sPO, sPO.n)
                    n_out[par] = P.dma(
                        "scalar", sOU[par],
                        out_t.ap()[t*TT + c*128:t*TT + (c+1)*128, :],
                        o_st[par][:])
        P.wait("scalar", sOU[0], sOU[0].n)
        P.wait("scalar", sOU[1], sOU[1].n)
        P.run(nc, "phP2b")

    if "p2b" in phases:
        _phase_p2b()

    Pd = Prog()
    alldma = [sSY, sSC, sGQ, sPE, sDV, sAC, sPO, sXB, sXC, sXN, sST,
              sSPx, sSPk, sSPv, sSPs, sSTP, sPW]
    alldma += [s for pr in ring_sems for s in pr]
    alldma += sLK + sLV + sLO + sRP + sRX + sOU
    for s in alldma:
        Pd.wait("sync", s)
    for k in sAG:
        Pd.wait("sync", sAG[k], sAG[k].n)
    for dout, din in dbg_pairs:
        Pd.dma("sync", sSY, dout.ap(), din.ap())
    Pd.wait("sync", sSY)
    Pd.run(nc, "phDrain")

    es.close()
    return nc


def _host_consts():
    s = _slopes().astype(np.float64)
    i = np.arange(CH, dtype=np.float64)
    qd = np.exp(-s[:, None] * (i[None, :] + 1.0))
    kd = np.exp(-s[None, :] * (CH - 1.0 - i[:, None]))
    diff = i[None, :] - i[:, None]
    dd = np.where(diff >= 0,
                  np.exp(-s[:, None, None] * diff[None, :, :]), 0.0)
    return dict(
        ident=np.eye(128),
        ones_col=np.ones((128, 1)),
        ones_row=np.ones((1, 128), dtype=np.float32),
        iota_fm=np.tile(np.arange(CH, dtype=np.float32)[None, :], (128, H)),
        iota_col=np.arange(CH, dtype=np.float32)[:, None].copy(),
        qd_fm=np.tile(qd.reshape(1, H * CH), (128, 1)).astype(np.float32),
        kd_tm=np.repeat(kd, E, axis=1).astype(np.float32),
        ddt=dd.transpose(1, 0, 2).reshape(CH, H * CH).astype(np.float32),
    )


def make_inputs(core, consts, L0_attn_x, L0_attn_w_qkv, L0_attn_w_proj,
                L0_attn_theta, L0_ffn_wq_gate, L0_ffn_s_gate, L0_ffn_wq_up,
                L0_ffn_s_up, L0_ffn_wq_down, L0_ffn_s_down):
    import ml_dtypes
    bf = ml_dtypes.bfloat16
    b, half = core // 2, core % 2
    c = core
    xq = np.asarray(L0_attn_x)
    wqkv = np.asarray(L0_attn_w_qkv)
    wproj = np.asarray(L0_attn_w_proj)
    theta = np.asarray(L0_attn_theta).reshape(H, E)
    wg = np.asarray(L0_ffn_wq_gate)
    wu = np.asarray(L0_ffn_wq_up)
    wd = np.asarray(L0_ffn_wq_down)
    wq = wqkv[0:D]
    wk = wqkv[D:2*D]
    wv = wqkv[2*D:3*D]
    fv, kh = c // 2, c % 2
    gpad = np.zeros((NJGP * 128, D), np.int32)
    gpad[:HID] = wg
    upad = np.zeros((NJGP * 128, D), np.int32)
    upad[:HID] = wu
    dpad = np.zeros((D, NKDP * 128), np.int32)
    dpad[:, :HID] = wd
    return {
        "x": np.ascontiguousarray(xq[b, half*TPC:(half+1)*TPC, :],
                                  dtype=np.float32),
        "wq_sh": np.ascontiguousarray(wq[c*256:(c+1)*256], dtype=np.float32),
        "wk_sh": np.ascontiguousarray(wk[c*256:(c+1)*256], dtype=np.float32),
        "wv_sh": np.ascontiguousarray(
            wv[fv*512:(fv+1)*512, kh*1024:(kh+1)*1024], dtype=np.float32),
        "wkt_sh": np.ascontiguousarray(
            wk[fv*512:(fv+1)*512, kh*1024:(kh+1)*1024], dtype=np.float32),
        "wp_sh": np.ascontiguousarray(
            wproj[fv*512:(fv+1)*512, kh*1024:(kh+1)*1024], dtype=np.float32),
        "wg_sh": np.ascontiguousarray(gpad[c*768:(c+1)*768]),
        "wu_sh": np.ascontiguousarray(upad[c*768:(c+1)*768]),
        "wd_sh": np.ascontiguousarray(dpad[:, c*768:(c+1)*768]),
        "sg_t": np.ascontiguousarray(
            np.asarray(L0_ffn_s_gate).reshape(NJG, 128).T, dtype=np.float32),
        "su_t": np.ascontiguousarray(
            np.asarray(L0_ffn_s_up).reshape(NJG, 128).T, dtype=np.float32),
        "sd_row": np.ascontiguousarray(
            np.asarray(L0_ffn_s_down).reshape(1, D), dtype=np.float32),
        "theta_tm": np.ascontiguousarray(theta.reshape(1, D),
                                         dtype=np.float32),
        "theta_t": np.ascontiguousarray(theta.T, dtype=np.float32),
        "ident": consts["ident"].astype(bf),
        "ones_col": consts["ones_col"].astype(bf),
        "ones_row": consts["ones_row"],
        "iota_fm": consts["iota_fm"],
        "iota_col": consts["iota_col"],
        "pos_col": np.full((128, 1), float(half * TPC), np.float32),
        "mask_col": np.full((128, 1), float(half), np.float32),
        "qd_fm": consts["qd_fm"],
        "kd_tm": consts["kd_tm"],
        "ddt": consts["ddt"],
    }


DEV_PHASES = ("w", "p1", "p2a", "p2b")
DEV_DEBUG = ()


def kernel(**inputs):
    consts = _host_consts()
    in_maps = [make_inputs(c, consts, **inputs) for c in range(8)]
    nc = build_program(debug=DEV_DEBUG, phases=DEV_PHASES)
    res = run_bass_kernel_spmd(nc, in_maps, core_ids=list(range(8)))
    out = np.empty((B, T, D), np.float32)
    for c in range(8):
        b, half = c // 2, c % 2
        out[b, half*TPC:(half+1)*TPC, :] = res.results[c]["out"]
    return out


# version 23
# speedup vs baseline: 1.0279x; 1.0279x over previous
import math
from contextlib import ExitStack

import numpy as np

import concourse.bass as bass
import concourse.mybir as mybir
from concourse.bass_utils import run_bass_kernel_spmd

F32 = mybir.dt.float32
BF16 = mybir.dt.bfloat16
I32 = mybir.dt.int32
AF = mybir.ActivationFunctionType
ALU = mybir.AluOpType

B, T, D = 4, 8192, 2048
H, E = 16, 128
HID = 5632
CH = 128
TPC = T // 2
NT = 8
TT = TPC // NT
NCH = TT // CH
NKC = D // 128
NJG = HID // 128
NJGP = 48
NFB = D // 512
NKD = NJG
NKDP = 48
PI = float(np.pi)
TWO_PI = float(2.0 * np.pi)
RND_M = 12582912.0

LQ = 16 * NKC * 128
LV = 4 * NKC * 512
LP = NFB * H * 512
LG = NJGP * NKC * 128
LDN = NKDP * NFB * 512

ENGS = ("sync", "scalar", "vector", "tensor", "gpsimd")


def _slopes():
    start = 2.0 ** (-(2.0 ** (-(math.log2(H) - 3))))
    return np.array([start ** (i + 1) for i in range(H)], dtype=np.float32)


class Sem:
    def __init__(self, es, nc, name):
        self.h = es.enter_context(nc.semaphore(name))
        self.n = 0


class Prog:

    TRACE = []

    def __init__(self):
        self.q = {e: [] for e in ENGS}

    def wait(self, eng, sem, n=None):
        n = sem.n if n is None else n
        if n > 0:
            Prog.TRACE.append((eng, "wait", sem.h.name, n))
            self.q[eng].append(lambda e, h=sem.h, n=n: e.wait_ge(h, n))

    def dma(self, eng, sem, out, in_, **kw):
        def f(e, out=out, in_=in_, kw=kw, h=sem.h):
            e.dma_start(out=out, in_=in_, **kw).then_inc(h, 16)
        Prog.TRACE.append((eng, "inc", sem.h.name, 16))
        self.q[eng].append(f)
        sem.n += 16
        return sem.n

    def op(self, eng, sem, fn):
        if sem is None:
            Prog.TRACE.append((eng, "op", None, 0))
            self.q[eng].append(fn)
            return None
        Prog.TRACE.append((eng, "inc", sem.h.name, 1))
        self.q[eng].append(lambda e, fn=fn, h=sem.h: fn(e).then_inc(h, 1))
        sem.n += 1
        return sem.n

    def run(self, nc, name):
        with nc.Block(name) as blk:
            for eng in ENGS:
                fns = self.q[eng]
                if fns:
                    getattr(blk, eng)(lambda e, fns=fns: [f(e) for f in fns])


def build_program(debug=(), phases=("w", "p1", "p2a", "p2b")):
    nc = bass.Bass("TRN2", target_bir_lowering=False)
    es = ExitStack()

    dbg_pairs = []

    def dram(name, shape, dt, kind="Internal"):
        th = nc.dram_tensor(name, shape, dt, kind=kind)
        if kind == "Internal" and name in debug:
            dbg_pairs.append(
                (nc.dram_tensor("dbg_" + name, shape, dt,
                                kind="ExternalOutput"), th))
        return th

    def sbt(name, shape, dt, stack=None):
        return (stack or es).enter_context(nc.sbuf_tensor(name, shape, dt))

    def pst(name, shape, dt, stack=None):
        return (stack or es).enter_context(nc.psum_tensor(name, shape, dt))

    def inp(name, shape, dt):
        return nc.dram_tensor(name, shape, dt, kind="ExternalInput")

    x_in = inp("x", [TPC, D], F32)
    wq_sh = inp("wq_sh", [256, D], F32)
    wk_sh = inp("wk_sh", [256, D], F32)
    wv_sh = inp("wv_sh", [512, 1024], F32)
    wkt_sh = inp("wkt_sh", [512, 1024], F32)
    wp_sh = inp("wp_sh", [512, 1024], F32)
    wg_sh = inp("wg_sh", [768, D], I32)
    wu_sh = inp("wu_sh", [768, D], I32)
    wd_sh = inp("wd_sh", [D, 768], I32)
    sg_t = inp("sg_t", [128, NJG], F32)
    su_t = inp("su_t", [128, NJG], F32)
    sd_row = inp("sd_row", [1, D], F32)
    th_tm_in = inp("theta_tm", [1, D], F32)
    th_t_in = inp("theta_t", [128, H], F32)
    ident_in = inp("ident", [128, 128], BF16)
    ones_col_in = inp("ones_col", [128, 1], BF16)
    ones_row_in = inp("ones_row", [1, 128], F32)
    iota_fm_in = inp("iota_fm", [128, D], F32)
    iota_col_in = inp("iota_col", [128, 1], F32)
    pos_col_in = inp("pos_col", [128, 1], F32)
    mask_col_in = inp("mask_col", [128, 1], F32)
    qd_fm_in = inp("qd_fm", [128, D], F32)
    kd_tm_in = inp("kd_tm", [128, D], F32)
    ddt_in = inp("ddt", [128, D], F32)
    out_t = nc.dram_tensor("out", [TPC, D], F32, kind="ExternalOutput")

    pre_loc, pre_all = {}, {}
    for nm, L in (("sq", LQ), ("sk", LQ), ("sv", LV), ("skt", LV), ("sp", LP),
                  ("sg", LG), ("su", LG), ("sd", LDN)):
        pre_loc[nm] = dram("pre_%s_loc" % nm, [128, L // 8], BF16)
        pre_all[nm] = dram("pre_%s_all" % nm, [1024, L // 8], BF16)
    x_bf = dram("x_bf", [TPC, D], BF16)
    xnfm_sp = dram("xnfm_sp", [128, NT * NKC * TT], BF16)
    kfm_sp = dram("kfm_sp", [128, 32 * D], BF16)
    vtm_sp = dram("vtm_sp", [128, 32 * D], BF16)
    st_sp = dram("st_sp", [128, 32 * H * 256], BF16)
    ktc_sp = dram("ktc_sp", [128, 32 * D], BF16)
    o_sp = dram("o_sp", [128, NT * H * TT], BF16)
    sT_sp = dram("sT_sp", [128, 32 * D], BF16)
    qp_sp = dram("qp_sp", [128, 32 * D], BF16)
    kp_sp = dram("kp_sp", [128, 32 * D], BF16)
    trc_sp = dram("trc_sp", [128, 32 * D], BF16)
    trs_sp = dram("trs_sp", [128, 32 * D], BF16)
    st_loc = dram("st_loc", [128, H * 256], F32)
    st_pair = dram("st_pair", [256, H * 256], F32)
    partial = dram("partial", [TPC, D], BF16)

    ident = sbt("c_ident", [128, 128], BF16)
    ones_col = sbt("c_ones_col", [128, 1], BF16)
    ones_row = sbt("c_ones_row", [1, 128], F32)
    iota_fm = sbt("c_iota_fm", [128, D], F32)
    iota_col = sbt("c_iota_col", [128, 1], F32)
    pos_col = sbt("c_pos_col", [128, 1], F32)
    mask_col = sbt("c_mask_col", [128, 1], F32)
    theta_tm = sbt("c_theta_tm", [128, D], F32)
    theta_t = sbt("c_theta_t", [128, H], F32)
    half_pi = sbt("c_half_pi", [128, 1], F32)
    c_n1 = sbt("c_n1", [128, 1], F32)

    sSY = Sem(es, nc, "sSY")
    sSC = Sem(es, nc, "sSC")
    sGQ = Sem(es, nc, "sGQ")
    sPE = Sem(es, nc, "sPE")
    sDV = Sem(es, nc, "sDV")
    sAC = Sem(es, nc, "sAC")
    sPO = Sem(es, nc, "sPO")
    sAG = {k: Sem(es, nc, "sAG" + k) for k in
           ("sk", "sv", "skt", "sq", "sp", "sg", "su", "sd", "st")}
    ring_sems = [(Sem(es, nc, "sR%da" % i), Sem(es, nc, "sR%db" % i),
                  Sem(es, nc, "sR%dc" % i)) for i in range(3)]
    sXB = Sem(es, nc, "sXB")
    sXC = Sem(es, nc, "sXC")
    sXN = Sem(es, nc, "sXN")
    sST = Sem(es, nc, "sST")
    sSPx = Sem(es, nc, "sSPx")
    sSPk = Sem(es, nc, "sSPk")
    sSPv = Sem(es, nc, "sSPv")
    sSPs = Sem(es, nc, "sSPs")
    sSTP = Sem(es, nc, "sSTP")
    sPW = Sem(es, nc, "sPW")
    sLK = [Sem(es, nc, "sLK%d" % i) for i in range(2)]
    sLV = [Sem(es, nc, "sLV%d" % i) for i in range(2)]
    sLO = [Sem(es, nc, "sLO%d" % i) for i in range(2)]
    sRP = [Sem(es, nc, "sRP%d" % i) for i in range(2)]
    sRX = [Sem(es, nc, "sRX%d" % i) for i in range(2)]
    sOU = [Sem(es, nc, "sOU%d" % i) for i in range(2)]

    slopes = _slopes()
    bdec = [float(np.exp(np.float64(-s) * CH))
            for s in slopes.astype(np.float64)]

    def stream_runs(nm, c0, c1):
        Lc = pre_all[nm].shape[1]
        runs, c = [], c0
        while c < c1:
            e = min(c1, (c // Lc + 1) * Lc)
            runs.append((c, e))
            c = e
        return runs

    def stream_ap(nm, c0, c1):
        Lc = pre_all[nm].shape[1]
        s = c0 // Lc
        return pre_all[nm].ap()[s * 128:(s + 1) * 128, c0 - s * Lc:c1 - s * Lc]

    class Ring:
        def __init__(self, name, nm, ncols, stack, sems, nbuf=2):
            self.nm, self.ncols, self.nbuf = nm, ncols, nbuf
            self.sems = sems
            self.bufs = [sbt("%s%d" % (name, i), [128, ncols], BF16, stack)
                         for i in range(nbuf)]
            self.loaded = []
            self.freed = []
            self.issue_i = 0
            self.cons_i = 0
            self.take_i = 0

        def issue(self, P, c0):
            i = self.issue_i % self.nbuf
            if self.issue_i >= self.nbuf:
                sem, n = self.freed[self.issue_i - self.nbuf]
                P.wait("sync", sem, n)
            for (a, b) in stream_runs(self.nm, c0, c0 + self.ncols):
                P.dma("sync", self.sems[i], self.bufs[i][:, a - c0:b - c0],
                      stream_ap(self.nm, a, b))
            self.loaded.append((i, self.sems[i].n))
            self.issue_i += 1

        def use(self, P, eng="tensor"):
            i, n = self.loaded[self.take_i]
            self.take_i += 1
            P.wait(eng, self.sems[i], n)
            return self.bufs[i]

        def release(self, sem):
            self.freed.append((sem, sem.n))
            self.cons_i += 1

        def start(self, P, seq):
            self.seq = list(seq)
            self.si = 0
            for i in range(min(self.nbuf, len(self.seq))):
                self.issue(P, self.seq[i])

        def take(self, P, eng="tensor"):
            return self.use(P, eng)

        def done(self, P, sem):
            self.release(sem)
            nxt = self.si + self.nbuf
            if nxt < len(self.seq):
                self.issue(P, self.seq[nxt])
            self.si += 1

    ctx = {"n_xbf": 0}

    def _phase_w():
      with ExitStack() as stw:
        stag = sbt("w_stag", [128, 12288], BF16, stw)
        rowf = [sbt("w_rowf%d" % i, [128, D], F32, stw) for i in range(2)]
        rowi = [sbt("w_rowi%d" % i, [128, D], I32, stw) for i in range(2)]
        rowb = [sbt("w_rowb%d" % i, [128, D], BF16, stw) for i in range(2)]
        ptr = pst("w_ptr", [128, 8192], BF16, stw)
        P = Prog()
        for dst, src in ((ident, ident_in), (ones_col, ones_col_in),
                         (ones_row, ones_row_in), (iota_fm, iota_fm_in),
                         (iota_col, iota_col_in), (pos_col, pos_col_in),
                         (mask_col, mask_col_in), (theta_t, th_t_in)):
            P.dma("sync", sSY, dst[:], src.ap())
        P.dma("sync", sSY, theta_tm[:], th_tm_in.ap().broadcast_to([128, D]))
        P.op("vector", sDV, lambda v: v.memset(half_pi[:], PI / 2.0))
        P.op("vector", sDV, lambda v: v.memset(c_n1[:], -1.0))
        nc.const_aps.register(F32, -1.0, c_n1[:, 0:1])
        P.wait("tensor", sSY)

        for t in range(NT):
            P.dma("gpsimd", sXC, x_bf.ap()[t * TT:(t + 1) * TT, :],
                  x_in.ap()[t * TT:(t + 1) * TT, :])
        ctx["n_xbf"] = sXC.n

        jobs = [
            ("sk", [(wk_sh.ap()[j*128:(j+1)*128, :], False, NKC,
                     (lambda kc, j=j: j*NKC + kc)) for j in range(2)], 2*NKC),
            ("sv", [(wv_sh.ap()[j*128:(j+1)*128, :], False, 8,
                     (lambda kc, j=j: kc*4 + j)) for j in range(4)], 32),
            ("skt", [(wkt_sh.ap()[j*128:(j+1)*128, :], False, 8,
                      (lambda kc, j=j: kc*4 + j)) for j in range(4)], 32),
            ("sq", [(wq_sh.ap()[j*128:(j+1)*128, :], False, NKC,
                     (lambda kc, j=j: j*NKC + kc)) for j in range(2)], 2*NKC),
            ("sp", [(wp_sh.ap()[j*128:(j+1)*128, :], False, 8,
                     (lambda kc, j=j: kc*4 + j)) for j in range(4)], 32),
            ("sg", [(wg_sh.ap()[j*128:(j+1)*128, :], True, NKC,
                     (lambda kc, j=j: j*NKC + kc)) for j in range(6)], 6*NKC),
            ("su", [(wu_sh.ap()[j*128:(j+1)*128, :], True, NKC,
                     (lambda kc, j=j: j*NKC + kc)) for j in range(6)], 6*NKC),
            ("sd", [(wd_sh.ap()[j*128:(j+1)*128, :], True, 6,
                     (lambda kc, j=j: kc*16 + j)) for j in range(16)], 96),
        ]
        ag_ready = {}
        pi = 0
        cv_done = {}
        tr_done = {}
        for nm, parts, nblk in jobs:
            n_store_prev = sST.n
            for srcap, is_int, nkc, scol in parts:
                cols = srcap.shape[1]
                par = pi % 2
                rbuf = rowi[par] if is_int else rowf[par]
                rb = rowb[par]
                P.wait("scalar", sDV, cv_done.get(pi - 2, 0))
                nld = P.dma("scalar", sRX[par], rbuf[:, 0:cols], srcap)
                P.wait("vector", sRX[par], nld)
                P.wait("vector", sPE, tr_done.get(pi - 2, 0))
                P.op("vector", sDV, lambda v, rbuf=rbuf, rb=rb, cols=cols,
                     a=(-128.0 if is_int else 0.0):
                     v.tensor_scalar(out=rb[:, 0:cols], in0=rbuf[:, 0:cols],
                                     scalar1=a, scalar2=None, op0=ALU.add))
                cv_done[pi] = sDV.n
                P.wait("tensor", sDV, cv_done[pi])
                for k0 in range(0, nkc, 8):
                    nb = min(8, nkc - k0)
                    P.wait("tensor", sDV, sDV.n)
                    for i in range(nb):
                        P.op("tensor", sPE if i == nb - 1 else None,
                             lambda t_, kc=k0 + i, i=i, rb=rb: t_.transpose(
                                 ptr[:, i*1024:i*1024 + 128],
                                 rb[:, kc*128:(kc+1)*128], ident[:]))
                    P.wait("vector", sPE, sPE.n)
                    P.wait("vector", sST, n_store_prev)

                    def ev(v, k0=k0, nb=nb, scol=scol):
                        r = None
                        for i in range(nb):
                            r = v.tensor_copy(
                                out=stag[:, scol(k0+i)*128:(scol(k0+i)+1)*128],
                                in_=bass.AP(ptr, i*1024, [[8192, 128], [1, 128]]))
                        return r
                    P.op("vector", sDV, ev)
                tr_done[pi] = sPE.n
                pi += 1
            P.wait("sync", sDV, sDV.n)
            P.dma("sync", sST, pre_loc[nm].ap()[:, 0:nblk*128],
                  stag[:, 0:nblk*128])
            ag_ready[nm] = sST.n
            P.wait("gpsimd", sST, ag_ready[nm])
            Prog.TRACE.append(("gpsimd", "inc", "sAG" + nm, 1))
            P.op("gpsimd", None, lambda g, nm=nm: g.collective_compute(
                "AllGather", ALU.bypass,
                replica_groups=[[0, 1, 2, 3, 4, 5, 6, 7]],
                ins=[pre_loc[nm].ap()], outs=[pre_all[nm].ap()],
            ).then_inc(sAG[nm].h, 1))
            sAG[nm].n = 1
        P.run(nc, "phW")

    if "w" in phases:
        _phase_w()

    def _phase_p1():
      with ExitStack() as st1:
        xf = [sbt("p1_xf%d" % i, [128, NKC * TT], BF16, st1)
              for i in range(2)]
        sq_sc = [sbt("p1_sq%d" % i, [128, 512], BF16, st1) for i in range(2)]
        row_s = sbt("p1_rows", [1, 4 * TT], F32, st1)
        k_fm = sbt("p1_kfm", [128, NCH * D], BF16, st1)
        v_tm = sbt("p1_vtm", [128, NCH * D], BF16, st1)
        k_tm = sbt("p1_ktm", [128, NCH * D], BF16, st1)
        kkd = sbt("p1_kkd", [128, D], BF16, st1)
        ktc = [sbt("p1_ktc%d" % i, [128, D], BF16, st1) for i in range(2)]
        kts = [sbt("p1_kts%d" % i, [128, D], BF16, st1) for i in range(2)]
        trc = sbt("p1_trc", [128, D], BF16, st1)
        trs = sbt("p1_trs", [128, D], BF16, st1)
        ph_b = sbt("p1_ph", [128, D], F32, st1)
        rr_b = sbt("p1_rr", [128, D], F32, st1)
        kd_tm = sbt("p1_kdtm", [128, D], BF16, st1)
        states = sbt("p1_st", [128, H * 256], F32, st1)
        st_bf = sbt("p1_stbf", [128, H * 256], BF16, st1)
        tcol = sbt("p1_tcol", [128, 1], F32, st1)
        c2pi = sbt("p1_c2pi", [128, 1], F32, st1)
        p_row = pst("p1_prow", [1, 1024], F32, st1)
        p_bc = pst("p1_pbc", [128, 1024], F32, st1)
        p_a = pst("p1_pa", [128, 2048], F32, st1)
        rk = Ring("p1_rk", "sk", 2048, st1, ring_sems[0], nbuf=3)
        rv = Ring("p1_rv", "sv", 2048, st1, ring_sems[1])
        rkt = Ring("p1_rkt", "skt", 2048, st1, ring_sems[2])

        P = Prog()
        P.dma("gpsimd", sGQ, kd_tm[:], kd_tm_in.ap())
        n_kdtm = sGQ.n
        P.op("vector", sDV, lambda v: v.memset(states[:], 0.0))
        P.op("vector", sDV, lambda v: v.memset(c2pi[:], -TWO_PI))
        P.wait("sync", sAG["sk"], 1)
        P.wait("sync", sAG["sv"], 1)
        P.wait("sync", sAG["skt"], 1)
        rk.start(P, [j * 2048 for _ in range(NT) for j in range(H)])
        seq_v = [r * 2048 for _ in range(NT) for r in range(16)]
        rv.start(P, seq_v)
        rkt.start(P, seq_v)

        st_c = {"spill": 0, "upd": 0, "mm_done": {}}
        sq_done = {}

        def xbar(t, bf):
            P.wait("sync", sXC, ctx["n_xbf"])
            for kc in range(NKC):
                P.dma("sync", sXB, bf[:, kc*TT:(kc+1)*TT],
                      x_bf.ap()[t*TT:(t+1)*TT, kc*128:(kc+1)*128],
                      transpose=True)
            return sXB.n

        def layer_norm(t, bf, n_xfm, n_spill_xn):
            P.wait("tensor", sXB, n_xfm)
            P.wait("tensor", sDV, sDV.n)
            for kc in range(NKC):
                P.wait("scalar", sXB, n_xfm)
                if kc >= 2:
                    P.wait("scalar", sPE, sq_done[kc - 2])
                elif sq_done.get(NKC - 2 + kc):
                    P.wait("scalar", sPE, sq_done[NKC - 2 + kc])
                P.op("scalar", sAC, lambda s, kc=kc, bf=bf: s.activation(
                    out=sq_sc[kc % 2][:], in_=bf[:, kc*TT:(kc+1)*TT],
                    func=AF.Square))
                P.op("tensor", None, lambda t_, kc=kc, bf=bf: t_.matmul(
                    p_row[0:1, 0:TT], ones_col[:], bf[:, kc*TT:(kc+1)*TT],
                    start=(kc == 0), stop=(kc == NKC - 1)))
                P.wait("tensor", sAC, sAC.n)
                sq_done[kc] = P.op("tensor", sPE, lambda t_, kc=kc: t_.matmul(
                    p_row[0:1, 512:512 + TT], ones_col[:], sq_sc[kc % 2][:],
                    start=(kc == 0), stop=(kc == NKC - 1)))
            P.wait("vector", sPE, sPE.n)
            P.op("vector", sDV, lambda v: v.tensor_scalar(
                out=row_s[0:1, 0:TT], in0=p_row[0:1, 0:TT],
                scalar1=-1.0 / D, scalar2=None, op0=ALU.mult))
            P.op("vector", sDV, lambda v: v.tensor_tensor(
                out=row_s[0:1, 1024:1024 + TT], in0=row_s[0:1, 0:TT],
                in1=row_s[0:1, 0:TT], op=ALU.mult))
            P.op("vector", sDV, lambda v: v.tensor_scalar(
                out=row_s[0:1, 1536:1536 + TT], in0=p_row[0:1, 512:512 + TT],
                scalar1=1.0 / D, scalar2=None, op0=ALU.mult))
            P.op("vector", sDV, lambda v: v.tensor_tensor(
                out=row_s[0:1, 1024:1024 + TT], in0=row_s[0:1, 1536:1536 + TT],
                in1=row_s[0:1, 1024:1024 + TT], op=ALU.subtract))
            P.op("vector", sDV, lambda v: v.tensor_scalar(
                out=row_s[0:1, 1024:1024 + TT], in0=row_s[0:1, 1024:1024 + TT],
                scalar1=1e-5, scalar2=None, op0=ALU.add))
            P.wait("scalar", sDV, sDV.n)
            P.op("scalar", sAC, lambda s: s.activation(
                out=row_s[0:1, 1536:1536 + TT], in_=row_s[0:1, 1024:1024 + TT],
                func=AF.Sqrt))
            P.wait("vector", sAC, sAC.n)
            P.op("vector", sDV, lambda v: v.reciprocal(
                out=row_s[0:1, 512:512 + TT], in_=row_s[0:1, 1536:1536 + TT]))
            n_rows = sDV.n
            P.wait("tensor", sDV, n_rows)
            P.op("tensor", None, lambda t_: t_.matmul(
                p_bc[:, 0:512], ones_row[:], row_s[0:1, 0:TT],
                start=True, stop=True))
            P.op("tensor", sPE, lambda t_: t_.matmul(
                p_bc[:, 512:1024], ones_row[:], row_s[0:1, 512:512 + TT],
                start=True, stop=True))
            P.wait("vector", sPE, sPE.n)
            P.wait("vector", sSPx, n_spill_xn)
            nm_rep = bass.AP(p_bc, 0, [[1024, 128], [0, NKC], [1, TT]])
            rs_rep = bass.AP(p_bc, 512, [[1024, 128], [0, NKC], [1, TT]])
            xn3 = bf[:].rearrange("p (kc t) -> p kc t", kc=NKC)
            P.op("vector", sDV, lambda v: v.tensor_tensor(
                out=xn3, in0=xn3, in1=nm_rep, op=ALU.add))
            P.wait("vector", sDV, sDV.n)
            P.op("vector", sDV, lambda v: v.tensor_tensor(
                out=xn3, in0=xn3, in1=rs_rep, op=ALU.mult))
            n_xn = sDV.n
            P.wait("sync", sDV, n_xn)
            return n_xn, P.dma("sync", sSPx,
                               xnfm_sp.ap()[:, t*NKC*TT:(t+1)*NKC*TT], bf[:])

        def chain(t, c, pe_done_before):
            gt0 = t * TT + c * CH
            par = c % 2
            P.wait("vector", sPO, st_c.get("ph_rd", 0))
            P.op("vector", sDV, lambda v, gt0=gt0: v.tensor_scalar(
                out=tcol[:], in0=iota_col[:], scalar1=pos_col[:, 0:1],
                scalar2=float(gt0), op0=ALU.add, op1=ALU.add))
            P.wait("vector", sDV, sDV.n)
            P.wait("gpsimd", sDV, sDV.n)
            P.wait("gpsimd", sAC, st_c.get("tr_rd", 0))
            P.wait("gpsimd", sGQ, n_kdtm)
            P.op("gpsimd", sPO, lambda g: g.tensor_tensor(
                out=ph_b[:], in0=theta_tm[:],
                in1=tcol[:].broadcast_to([128, D]), op=ALU.mult))
            n_ph = sPO.n
            P.wait("scalar", sPO, n_ph)
            P.op("scalar", sAC, lambda s: s.activation(
                out=rr_b[:], in_=ph_b[:], func=AF.Copy,
                scale=1.0 / TWO_PI, bias=RND_M))
            P.op("scalar", sAC, lambda s: s.activation(
                out=rr_b[:], in_=rr_b[:], func=AF.Copy, bias=-RND_M))
            n_rr = sAC.n
            P.wait("gpsimd", sAC, n_rr)
            P.op("gpsimd", sPO, lambda g: g.tensor_tensor(
                out=rr_b[:], in0=rr_b[:],
                in1=c2pi[:].broadcast_to([128, D]), op=ALU.mult))
            P.wait("gpsimd", sPO, sPO.n)
            P.op("gpsimd", sPO, lambda g: g.tensor_tensor(
                out=rr_b[:], in0=rr_b[:], in1=ph_b[:], op=ALU.add))
            st_c["ph_rd"] = sPO.n
            P.wait("scalar", sPO, sPO.n)
            P.op("scalar", sAC, lambda s: s.activation(
                out=trs[:], in_=rr_b[:], func=AF.Sin))
            P.op("scalar", sAC, lambda s: s.activation(
                out=rr_b[:], in_=rr_b[:], func=AF.Abs))
            P.op("scalar", sAC, lambda s: s.activation(
                out=trc[:], in_=rr_b[:], func=AF.Sin,
                scale=-1.0, bias=half_pi[:, 0:1]))
            st_c["tr_rd"] = 0
            n_cos = sAC.n
            P.wait("gpsimd", sAC, n_cos)
            P.wait("gpsimd", sPE, pe_done_before)
            P.op("gpsimd", sPO, lambda g, c=c: g.tensor_tensor(
                out=kkd[:], in0=k_tm[:, c*D:(c+1)*D], in1=kd_tm[:],
                op=ALU.mult))
            P.wait("gpsimd", sPO, sPO.n)
            P.op("gpsimd", sPO, lambda g, par=par: g.tensor_tensor(
                out=ktc[par][:], in0=kkd[:], in1=trc[:], op=ALU.mult))
            P.op("gpsimd", sPO, lambda g, par=par: g.tensor_tensor(
                out=kts[par][:], in0=kkd[:], in1=trs[:], op=ALU.mult))
            st_c["tr_rd"] = sPO.n
            return sPO.n

        def scan_mms(t, c, n_pool, n_vev):
            par = c % 2
            P.wait("vector", sSPs, st_c["spill"])
            P.op("vector", sDV,
                 lambda v: v.tensor_copy(out=st_bf[:], in_=states[:]))
            P.wait("sync", sDV, sDV.n)
            st_c["spill"] = P.dma(
                "sync", sSPs,
                st_sp.ap()[:, (t*NCH + c)*H*256:(t*NCH + c + 1)*H*256],
                st_bf[:])
            for grp in range(4):
                P.wait("tensor", sPO, n_pool)
                P.wait("tensor", sDV, st_c["upd"])
                P.wait("tensor", sDV, n_vev)
                for hh in range(4):
                    h = grp * 4 + hh
                    P.op("tensor", None,
                         lambda t_, h=h, hh=hh, c=c, par=par:
                         t_.matmul(
                             p_a[:, 1024 + hh*256:1024 + hh*256 + 128],
                             ktc[par][:, h*128:(h+1)*128],
                             v_tm[:, c*D + h*128:c*D + (h+1)*128],
                             start=(hh % 2 == 0), stop=False))
                    P.op("tensor", sPE if hh == 3 else None,
                         lambda t_, h=h, hh=hh, c=c, par=par:
                         t_.matmul(
                             p_a[:, 1024 + hh*256 + 128:1024 + (hh+1)*256],
                             kts[par][:, h*128:(h+1)*128],
                             v_tm[:, c*D + h*128:c*D + (h+1)*128],
                             start=False, stop=(hh % 2 == 1)))
                P.wait("vector", sPE, sPE.n)
                for hh in range(4):
                    h = grp * 4 + hh
                    P.op("vector", sDV if hh == 3 else None,
                         lambda v, h=h, hh=hh:
                         v.scalar_tensor_tensor(
                             out=states[:, h*256:(h+1)*256],
                             in0=states[:, h*256:(h+1)*256],
                             scalar=bdec[h],
                             in1=p_a[:, 1024 + hh*256:1024 + (hh+1)*256],
                             op0=ALU.mult, op1=ALU.add))
                st_c["upd"] = sDV.n
            st_c["mm_done"][t * NCH + c] = sPE.n

        n_spill_k = n_spill_v = 0
        spx = {0: 0, 1: 0}
        prod_end = {-1: 0, -2: 0}
        n_xn_t = {}
        n_xfm = xbar(0, xf[0])
        n_xn_t[0], spx[0] = layer_norm(0, xf[0], n_xfm, 0)
        for t in range(NT):
            cur = xf[t % 2]
            n_xn = n_xn_t[t]
            if t + 1 < NT:
                nb = xf[(t + 1) % 2]
                P.wait("sync", sPE, prod_end[t - 1])
                P.wait("sync", sSPx, spx[(t + 1) % 2])
                n_xfm2 = xbar(t + 1, nb)
            P.wait("tensor", sDV, n_xn)
            P.wait("vector", sSPv, n_spill_v)
            P.wait("vector", sPO, sPO.n)
            for ring, dst in ((rkt, k_tm), (rv, v_tm)):
                for fv in range(4):
                    P.wait("tensor", sDV, sDV.n)
                    for qtr in range(4):
                        buf = ring.take(P)
                        for kci in range(4):
                            kc = qtr * 4 + kci
                            for c in range(NCH):
                                P.op("tensor",
                                     sPE if (kci == 3 and c == NCH - 1) else None,
                                     lambda t_, kc=kc, kci=kci, c=c, buf=buf,
                                     cur=cur:
                                     t_.matmul(
                                         p_a[:, c*512:(c+1)*512],
                                         cur[:, kc*TT + c*128:kc*TT + (c+1)*128],
                                         buf[:, kci*512:(kci+1)*512],
                                         start=(kc == 0), stop=(kc == NKC - 1)))
                        ring.done(P, sPE)
                    P.wait("vector", sPE, sPE.n)
                    P.op("vector", sDV, lambda v, fv=fv, dst=dst: v.tensor_copy(
                        out=bass.AP(dst, fv * 512,
                                    [[NCH * D, 128], [D, NCH], [1, 512]]),
                        in_=bass.AP(p_a, 0, [[2048, 128], [512, NCH], [1, 512]])))
            n_vev = sDV.n
            P.wait("sync", sDV, n_vev)
            n_spill_v = P.dma("sync", sSPv,
                              vtm_sp.ap()[:, t*NCH*D:(t+1)*NCH*D], v_tm[:])
            if t + 1 < NT:
                n_xn_t[t + 1], spx[(t + 1) % 2] = layer_norm(
                    t + 1, nb, n_xfm2, spx[(t + 1) % 2])
            P.wait("vector", sSPk, n_spill_k)
            chains = {}
            evk = {}
            for j in range(H):
                buf = rk.take(P)
                P.wait("tensor", sDV, evk.get(j - 2, n_vev))
                for kc in range(NKC):
                    P.op("tensor", sPE if kc == NKC - 1 else None,
                         lambda t_, j=j, kc=kc, buf=buf, cur=cur: t_.matmul(
                             p_a[:, (j % 2)*512:(j % 2)*512 + 512],
                             buf[:, kc*128:(kc+1)*128],
                             cur[:, kc*TT:(kc+1)*TT],
                             start=(kc == 0), stop=(kc == NKC - 1)))
                rk.done(P, sPE)
                P.wait("vector", sPE, sPE.n)
                P.op("vector", sDV, lambda v, j=j: v.tensor_copy(
                    out=bass.AP(k_fm, j * 128,
                                [[NCH * D, 128], [D, NCH], [1, 128]]),
                    in_=bass.AP(p_a, (j % 2)*512,
                                [[2048, 128], [128, NCH], [1, 128]])))
                evk[j] = sDV.n
                gci0 = t * NCH
                if j == 0:
                    chains[0] = chain(t, 0, st_c["mm_done"].get(gci0 - 2, 0))
                    chains[1] = chain(t, 1, st_c["mm_done"].get(gci0 - 1, 0))
                if j == 7:
                    scan_mms(t, 0, chains[0], n_vev)
                    chains[2] = chain(t, 2, st_c["mm_done"][gci0])
                if j == 11:
                    scan_mms(t, 1, chains[1], n_vev)
                    chains[3] = chain(t, 3, st_c["mm_done"][gci0 + 1])
                if j == 15:
                    scan_mms(t, 2, chains[2], n_vev)
            P.wait("sync", sDV, sDV.n)
            n_spill_k = P.dma("sync", sSPk,
                              kfm_sp.ap()[:, t*NCH*D:(t+1)*NCH*D], k_fm[:])
            prod_end[t] = sPE.n
            scan_mms(t, 3, chains[3], n_vev)
        P.wait("sync", sDV, sDV.n)
        n_stl = P.dma("sync", sSPs, st_loc.ap(), states[:])
        P.wait("gpsimd", sSPs, n_stl)
        Prog.TRACE.append(("gpsimd", "inc", "sAGst", 1))
        P.op("gpsimd", None, lambda g: g.collective_compute(
            "AllGather", ALU.bypass,
            replica_groups=[[0, 1], [2, 3], [4, 5], [6, 7]],
            ins=[st_loc.ap()], outs=[st_pair.ap()],
        ).then_inc(sAG["st"].h, 1))
        sAG["st"].n = 1
        P.run(nc, "phP1")

    if "p1" in phases:
        _phase_p1()

    def _phase_p2a():
      with ExitStack() as st2:
        xn_fm = sbt("p2_xnfm", [128, NKC * TT], BF16, st2)
        q_fm = sbt("p2_qfm", [128, NCH * D], BF16, st2)
        k_ch = [sbt("p2_kch%d" % i, [128, D], BF16, st2) for i in range(2)]
        v_ch = [sbt("p2_vch%d" % i, [128, D], BF16, st2) for i in range(2)]
        qpc = [sbt("p2_qpc%d" % i, [128, D], BF16, st2) for i in range(2)]
        qps = [sbt("p2_qps%d" % i, [128, D], BF16, st2) for i in range(2)]
        kpc = [sbt("p2_kpc%d" % i, [128, D], BF16, st2) for i in range(2)]
        kps = [sbt("p2_kps%d" % i, [128, D], BF16, st2) for i in range(2)]
        trc = [sbt("p2_trc%d" % i, [128, D], BF16, st2) for i in range(2)]
        trs = [sbt("p2_trs%d" % i, [128, D], BF16, st2) for i in range(2)]
        sttr = [sbt("p2_sttr%d" % i, [128, H * 256], BF16, st2)
                for i in range(2)]
        sTb = [sbt("p2_sT%d" % i, [128, 512], BF16, st2) for i in range(2)]
        ph_b = sbt("p2_ph", [128, D], F32, st2)
        rr_b = sbt("p2_rr", [128, D], F32, st2)
        qd_fm = sbt("p2_qdfm", [128, D], BF16, st2)
        ddt = sbt("p2_ddt", [128, D], BF16, st2)
        stin = sbt("p2_stin", [128, H * 256], BF16, st2)
        o_t = sbt("p2_ot", [128, H * TT], BF16, st2)
        pout = sbt("p2_pout", [128, 512], BF16, st2)
        tmp_o = sbt("p2_tmpo", [128, 512], BF16, st2)
        cM = sbt("p2_cM", [128, 1], F32, st2)
        cMn = sbt("p2_cMn", [128, 1], F32, st2)
        cI2P = sbt("p2_cI2P", [128, 1], F32, st2)
        cM4 = sbt("p2_cM4", [128, 1], F32, st2)
        p_q = pst("p2_pq", [128, 1024], F32, st2)
        p_s = pst("p2_ps", [128, 1024], F32, st2)
        p_o = pst("p2_po", [128, 1024], F32, st2)
        p_p = pst("p2_pp", [128, 1024], F32, st2)
        rq = Ring("p2_rq", "sq", 2048, st2, ring_sems[0])
        rp = Ring("p2_rp", "sp", 4096, st2, ring_sems[1])

        P = Prog()
        P.dma("gpsimd", sGQ, qd_fm[:], qd_fm_in.ap())
        P.dma("gpsimd", sGQ, ddt[:], ddt_in.ap())
        n_c2 = sGQ.n
        P.op("vector", sDV, lambda v: v.memset(cM[:], RND_M))
        P.op("vector", sDV, lambda v: v.memset(cMn[:], -RND_M))
        P.op("vector", sDV, lambda v: v.memset(cI2P[:], 1.0 / TWO_PI))
        P.op("vector", sDV, lambda v: v.memset(cM4[:], RND_M + 0.25))
        nc.const_aps.register(F32, RND_M, cM[:, 0:1])
        nc.const_aps.register(F32, -RND_M, cMn[:, 0:1])
        nc.const_aps.register(F32, 1.0 / TWO_PI, cI2P[:, 0:1])
        nc.const_aps.register(F32, RND_M + 0.25, cM4[:, 0:1])
        P.wait("gpsimd", sAG["st"], 1)
        nstin = P.dma("gpsimd", sSTP, stin[:], st_pair.ap()[0:128, :])
        P.wait("vector", sSTP, nstin)
        P.op("vector", sDV, lambda v: v.tensor_scalar(
            out=stin[:], in0=stin[:], scalar1=mask_col[:, 0:1], scalar2=None,
            op0=ALU.mult))
        P.wait("sync", sAG["sq"], 1)
        P.wait("sync", sAG["sp"], 1)
        rq.start(P, [j * 2048 for _ in range(NT) for j in range(H)])
        rp.start(P, [r * 4096 for _ in range(NT) for r in range(8)])

        def chain(ci, pe_done_before):
            t, c = divmod(ci, NCH)
            par = ci % 2
            gt0 = t * TT + c * CH
            P.wait("sync", sPE, pe_done_before)
            P.wait("sync", sDV, sDV.n)
            n_k = P.dma("sync", sLK[par], k_ch[par][:],
                        kfm_sp.ap()[:, ci*D:(ci+1)*D])
            n_v = P.dma("sync", sLV[par], v_ch[par][:],
                        vtm_sp.ap()[:, ci*D:(ci+1)*D])
            n_lo = P.dma("sync", sLO[par], sttr[par][:],
                         st_sp.ap()[:, ci*H*256:(ci+1)*H*256])
            P.wait("vector", sAC, sAC.n)
            P.wait("vector", sPO, sPO.n)
            P.op("vector", sDV, lambda v, gt0=gt0: v.tensor_scalar(
                out=rr_b[:], in0=iota_fm[:], scalar1=pos_col[:, 0:1],
                scalar2=float(gt0), op0=ALU.add, op1=ALU.add))
            P.wait("vector", sDV, sDV.n)
            th_rep = theta_t[:].unsqueeze(2).broadcast_to([128, H, 128])
            P.op("vector", sDV, lambda v, th_rep=th_rep: v.tensor_tensor(
                out=ph_b[:].rearrange("p (h t) -> p h t", h=H),
                in0=rr_b[:].rearrange("p (h t) -> p h t", h=H),
                in1=th_rep, op=ALU.mult))
            n_ph = sDV.n
            P.wait("scalar", sDV, n_ph)
            P.op("scalar", sAC, lambda s: s.activation(
                out=rr_b[:], in_=ph_b[:], func=AF.Copy,
                scale=1.0 / TWO_PI, bias=RND_M))
            P.op("scalar", sAC, lambda s: s.activation(
                out=rr_b[:], in_=rr_b[:], func=AF.Copy, bias=-RND_M))
            n_rr = sAC.n
            P.wait("vector", sAC, n_rr)
            P.op("vector", sDV, lambda v: v.scalar_tensor_tensor(
                out=rr_b[:], in0=rr_b[:], scalar=-TWO_PI, in1=ph_b[:],
                op0=ALU.mult, op1=ALU.add))
            P.wait("scalar", sDV, sDV.n)
            P.wait("scalar", sPE, pe_done_before)
            P.op("scalar", sAC, lambda s, par=par: s.activation(
                out=trs[par][:], in_=rr_b[:], func=AF.Sin))
            n_sin = sAC.n
            P.op("scalar", sAC, lambda s: s.activation(
                out=rr_b[:], in_=ph_b[:], func=AF.Copy,
                scale=1.0 / TWO_PI, bias=RND_M + 0.25))
            P.op("scalar", sAC, lambda s: s.activation(
                out=rr_b[:], in_=rr_b[:], func=AF.Copy, bias=-RND_M))
            n_rr2 = sAC.n
            P.wait("vector", sAC, n_rr2)
            P.op("vector", sDV, lambda v: v.scalar_tensor_tensor(
                out=rr_b[:], in0=rr_b[:], scalar=-TWO_PI, in1=ph_b[:],
                op0=ALU.mult, op1=ALU.add))
            P.wait("scalar", sDV, sDV.n)
            P.op("scalar", sAC, lambda s, par=par: s.activation(
                out=trc[par][:], in_=rr_b[:], func=AF.Sin,
                bias=half_pi[:, 0:1]))
            n_cos = sAC.n
            P.wait("gpsimd", sAC, n_cos)
            P.wait("gpsimd", sLK[par], n_k)
            P.wait("gpsimd", sPE, pe_done_before)
            P.op("gpsimd", sPO, lambda g, par=par: g.tensor_tensor(
                out=kpc[par][:], in0=k_ch[par][:], in1=trc[par][:],
                op=ALU.mult))
            P.op("gpsimd", sPO, lambda g, par=par: g.tensor_tensor(
                out=kps[par][:], in0=k_ch[par][:], in1=trs[par][:],
                op=ALU.mult))
            P.wait("vector", sLO[par], n_lo)
            P.wait("vector", sGQ, n_c2)
            P.wait("vector", sPE, pe_done_before)
            for h in range(H):
                P.op("vector", sDV if h == H - 1 else None,
                     lambda v, h=h, m=ci, par=par: v.scalar_tensor_tensor(
                         out=sttr[par][:, h*256:(h+1)*256],
                         in0=stin[:, h*256:(h+1)*256],
                         scalar=float(bdec[h] ** m),
                         in1=sttr[par][:, h*256:(h+1)*256],
                         op0=ALU.mult, op1=ALU.add))
            return [sPO.n, sDV.n, n_v]

        def chain_b(ci, deps, n_qev):
            t, c = divmod(ci, NCH)
            par = ci % 2
            P.wait("gpsimd", sDV, n_qev)
            qsl = bass.AP(q_fm, c * D, [[NCH * D, 128], [1, D]])
            P.op("gpsimd", sPO, lambda g, qsl=qsl, par=par: g.tensor_tensor(
                out=qpc[par][:], in0=qsl, in1=trc[par][:], op=ALU.mult))
            P.op("gpsimd", sPO, lambda g, qsl=qsl, par=par: g.tensor_tensor(
                out=qps[par][:], in0=qsl, in1=trs[par][:], op=ALU.mult))
            deps[0] = sPO.n
            if "qp_sp" in debug:
                P.wait("scalar", sPO, sPO.n)
                P.dma("scalar", sSC, qp_sp.ap()[:, ci*D:(ci+1)*D], qpc[par][:])
                P.dma("scalar", sSC, kp_sp.ap()[:, ci*D:(ci+1)*D], kpc[par][:])
                P.dma("scalar", sSC, trc_sp.ap()[:, ci*D:(ci+1)*D], trc[par][:])
                P.wait("gpsimd", sSC, sSC.n)
                P.wait("vector", sSC, sSC.n)
                P.wait("tensor", sSC, sSC.n)

        mst = {"mdv": 0}

        def mms(ci, deps):
            n_pool, n_sttr, n_v = deps
            t, c = divmod(ci, NCH)
            par = ci % 2
            evd = []
            for grp in range(4):
                gp = grp % 2
                P.wait("tensor", sPO, n_pool)
                P.wait("tensor", sDV, evd[grp - 2] if grp >= 2 else mst["mdv"])
                for hh in range(4):
                    h = grp * 4 + hh
                    P.op("tensor", None, lambda t_, h=h, hh=hh, gp=gp, par=par:
                         t_.matmul(
                             p_s[:, gp*512 + hh*128:gp*512 + (hh+1)*128],
                             kpc[par][:, h*128:(h+1)*128],
                             qpc[par][:, h*128:(h+1)*128],
                             start=(hh == 0), stop=False))
                    P.op("tensor", sPE if hh == 3 else None,
                         lambda t_, h=h, hh=hh, gp=gp, par=par: t_.matmul(
                             p_s[:, gp*512 + hh*128:gp*512 + (hh+1)*128],
                             kps[par][:, h*128:(h+1)*128],
                             qps[par][:, h*128:(h+1)*128],
                             start=False, stop=(hh == 3)))
                P.wait("vector", sPE, sPE.n)
                P.op("vector", sDV, lambda v, grp=grp, gp=gp, par=par:
                     v.tensor_tensor(
                         out=sTb[par][:],
                         in0=bass.AP(p_s, gp*512, [[1024, 128], [1, 512]]),
                         in1=ddt[:, grp*512:(grp+1)*512], op=ALU.mult))
                n_mask = sDV.n
                P.wait("tensor", sDV, n_mask)
                P.wait("tensor", sLV[par], n_v)
                for hh in range(4):
                    h = grp * 4 + hh
                    P.op("tensor", None,
                         lambda t_, h=h, hh=hh, gp=gp, par=par: t_.matmul(
                             p_o[:, gp*512 + hh*128:gp*512 + (hh+1)*128],
                             v_ch[par][:, h*128:(h+1)*128],
                             sTb[par][:, hh*128:(hh+1)*128],
                             start=(hh == 0), stop=False))
                    P.op("tensor", None, lambda t_, h=h, hh=hh, gp=gp, par=par:
                         t_.matmul(
                             p_q[:, gp*512 + hh*128:gp*512 + (hh+1)*128],
                             sttr[par][:, h*256:h*256 + 128],
                             qpc[par][:, h*128:(h+1)*128],
                             start=(hh == 0), stop=False))
                    P.op("tensor", sPE if hh == 3 else None,
                         lambda t_, h=h, hh=hh, gp=gp, par=par: t_.matmul(
                             p_q[:, gp*512 + hh*128:gp*512 + (hh+1)*128],
                             sttr[par][:, h*256 + 128:(h+1)*256],
                             qps[par][:, h*128:(h+1)*128],
                             start=False, stop=(hh == 3)))
                P.wait("vector", sPE, sPE.n)
                P.op("vector", sDV, lambda v, grp=grp, gp=gp: v.tensor_tensor(
                    out=tmp_o[:],
                    in0=bass.AP(p_q, gp*512, [[1024, 128], [1, 512]]),
                    in1=qd_fm[:, grp*512:(grp+1)*512], op=ALU.mult))
                P.wait("vector", sDV, sDV.n)
                P.op("vector", sDV, lambda v, grp=grp, gp=gp, c=c:
                     v.tensor_tensor(
                         out=bass.AP(o_t, grp*4*TT + c*128,
                                     [[H * TT, 128], [TT, 4], [1, 128]]),
                         in0=bass.AP(p_o, gp*512,
                                     [[1024, 128], [128, 4], [1, 128]]),
                         in1=tmp_o[:].rearrange("p (a b) -> p a b", a=4),
                         op=ALU.add))
                evd.append(sDV.n)
            mst["mdv"] = sDV.n

        pend = None
        pend_b = None
        pe_done = [0] * 40
        chained = {}
        qp_end = {-1: 0}
        for t in range(NT):
            P.wait("sync", sPE, qp_end[t - 1])
            P.wait("sync", sSPx, sSPx.n)
            n_xn = P.dma("sync", sXN, xn_fm[:],
                         xnfm_sp.ap()[:, t*NKC*TT:(t+1)*NKC*TT])
            if t == 0:
                chained[0] = chain(0, 0)
                pe_done[0] = sPE.n
            P.wait("tensor", sXN, n_xn)
            P.wait("vector", sPO, sPO.n)
            for j in range(H):
                buf = rq.take(P)
                P.wait("tensor", sDV, sDV.n)
                for kc in range(NKC):
                    P.op("tensor", sPE if kc == NKC - 1 else None,
                         lambda t_, j=j, kc=kc, buf=buf: t_.matmul(
                             p_q[:, (j % 2)*512:(j % 2)*512 + 512],
                             buf[:, kc*128:(kc+1)*128],
                             xn_fm[:, kc*TT:(kc+1)*TT],
                             start=(kc == 0), stop=(kc == NKC - 1)))
                rq.done(P, sPE)
                P.wait("vector", sPE, sPE.n)
                P.op("vector", sDV, lambda v, j=j: v.tensor_copy(
                    out=bass.AP(q_fm, j * 128,
                                [[NCH * D, 128], [D, NCH], [1, 128]]),
                    in_=bass.AP(p_q, (j % 2)*512,
                                [[1024, 128], [128, NCH], [1, 128]])))
            mst["mdv"] = max(mst["mdv"], sDV.n)
            n_qev = sDV.n
            qp_end[t] = sPE.n
            chain_b(t * NCH, chained[t * NCH], n_qev)
            for c in range(NCH):
                ci = t * NCH + c
                nxt = ci + 1
                if nxt < NT * NCH:
                    chained[nxt] = chain(nxt, pe_done[nxt - 2] if nxt >= 2 else 0)
                    pe_done[nxt] = sPE.n
                    if nxt % NCH != 0:
                        chain_b(nxt, chained[nxt], n_qev)
                mms(ci, chained[ci])
                pe_done[ci] = sPE.n
            if "o_sp" in debug:
                P.wait("scalar", sDV, sDV.n)
                P.dma("scalar", sSC, o_sp.ap()[:, t*H*TT:(t+1)*H*TT], o_t[:])
                P.wait("tensor", sSC, sSC.n)
            pvd = {}
            for fb in range(NFB):
                bufs = [rp.take(P), rp.take(P)]
                for c in range(NCH):
                    k2 = fb * NCH + c
                    if k2 >= 2:
                        P.wait("tensor", sAC, pvd[k2 - 2])
                    else:
                        P.wait("tensor", sDV, mst["mdv"])
                    for h in range(H):
                        wbuf = bufs[h // 8]
                        P.op("tensor", sPE if h == H - 1 else None,
                             lambda t_, h=h, c=c, wbuf=wbuf: t_.matmul(
                                 p_p[:, (c % 2)*512:(c % 2)*512 + 512],
                                 o_t[:, h*TT + c*128:h*TT + (c+1)*128],
                                 wbuf[:, (h % 8)*512:((h % 8)+1)*512],
                                 start=(h == 0), stop=(h == H - 1)))
                    P.wait("scalar", sPE, sPE.n)
                    P.wait("scalar", sPW, sPW.n)
                    P.op("scalar", sAC, lambda s, c=c: s.copy(
                        out=pout[:],
                        in_=bass.AP(p_p, (c % 2)*512, [[1024, 128], [1, 512]])))
                    pvd[fb * NCH + c] = sAC.n
                    P.wait("scalar", sAC, sAC.n)
                    P.dma("scalar", sPW,
                          partial.ap()[t*TT + c*128:t*TT + (c+1)*128,
                                       fb*512:(fb+1)*512], pout[:])
                rp.done(P, sPE)
                rp.done(P, sPE)
        P.run(nc, "phP2a")

    if "p2a" in phases:
        _phase_p2a()

    def _phase_p2b():
      with ExitStack() as st3:
        xn_fm = sbt("p3_xnfm", [128, NKC * TT], BF16, st3)
        h_t = sbt("p3_h", [128, NJG * TT], BF16, st3)
        silu_g = sbt("p3_silu", [128, TT], BF16, st3)
        sg_sb = sbt("p3_sg", [128, NJG], F32, st3)
        su_sb = sbt("p3_su", [128, NJG], F32, st3)
        sd_mat = sbt("p3_sd", [128, D], F32, st3)
        par_ch = [sbt("p3_par%d" % i, [128, D], BF16, st3) for i in range(2)]
        x_ch = [sbt("p3_x%d" % i, [128, D], BF16, st3) for i in range(2)]
        o_st = [sbt("p3_o%d" % i, [128, D], F32, st3) for i in range(2)]
        xp_ch = [sbt("p3_xp%d" % i, [128, D], BF16, st3) for i in range(2)]
        tmp2 = [sbt("p3_tmp%d" % i, [128, D], BF16, st3) for i in range(2)]
        p8 = pst("p3_p8", [128, 4096], F32, st3)
        rg = Ring("p3_rg", "sg", 4096, st3, ring_sems[0], nbuf=3)
        ru = Ring("p3_ru", "su", 4096, st3, ring_sems[1], nbuf=3)
        rd = Ring("p3_rd", "sd", 4096, st3, ring_sems[2], nbuf=3)

        P = Prog()
        P.dma("sync", sSY, sg_sb[:], sg_t.ap())
        P.dma("sync", sSY, su_sb[:], su_t.ap())
        P.dma("sync", sSY, sd_mat[:], sd_row.ap().broadcast_to([128, D]))
        nconst3 = sSY.n
        P.wait("sync", sAG["sg"], 1)
        P.wait("sync", sAG["su"], 1)
        P.wait("sync", sAG["sd"], 1)
        seq_g = [r * 4096 for _ in range(NT) for r in range(22)]
        rg.start(P, seq_g)
        ru.start(P, seq_g)
        rd.start(P, [r * 4096 for _ in range(NT) for _ in range(2)
                     for r in range(22)])

        NRD = 22
        n_out = {0: 0, 1: 0}
        n_par = {0: 0, 1: 0}
        n_x = {0: 0, 1: 0}
        gu_end = {-1: 0}
        xp_rd = {0: 0, 1: 0}
        xp_wr = {0: 0, 1: 0}
        for t in range(NT):
            P.wait("sync", sPE, gu_end[t - 1])
            n_xn = P.dma("sync", sXN, xn_fm[:],
                         xnfm_sp.ap()[:, t*NKC*TT:(t+1)*NKC*TT])
            P.wait("tensor", sXN, n_xn)
            P.wait("scalar", sSY, nconst3)
            P.wait("vector", sPE, sPE.n)
            for j in range(NJG):
                if j % 2 == 0:
                    gbuf = rg.take(P)
                    ubuf = ru.take(P)
                gp = j % 2
                P.wait("tensor", sDV, sDV.n)
                for kc in range(NKC):
                    P.op("tensor", None, lambda t_, j=j, kc=kc, gbuf=gbuf, gp=gp:
                         t_.matmul(p8[:, gp*512:gp*512 + 512],
                                   gbuf[:, (gp*NKC + kc)*128:(gp*NKC + kc + 1)*128],
                                   xn_fm[:, kc*TT:(kc+1)*TT],
                                   start=(kc == 0), stop=(kc == NKC - 1)))
                for kc in range(NKC):
                    P.op("tensor", sPE if kc == NKC - 1 else None,
                         lambda t_, j=j, kc=kc, ubuf=ubuf, gp=gp:
                         t_.matmul(p8[:, 1024 + gp*512:1024 + gp*512 + 512],
                                   ubuf[:, (gp*NKC + kc)*128:(gp*NKC + kc + 1)*128],
                                   xn_fm[:, kc*TT:(kc+1)*TT],
                                   start=(kc == 0), stop=(kc == NKC - 1)))
                P.wait("scalar", sPE, sPE.n)
                P.wait("scalar", sDV, sDV.n)
                P.op("scalar", sAC, lambda s, j=j, gp=gp: s.activation(
                    out=silu_g[:],
                    in_=bass.AP(p8, gp*512, [[4096, 128], [1, 512]]),
                    func=AF.Silu, scale=sg_sb[:, j:j+1]))
                P.wait("vector", sAC, sAC.n)
                P.op("vector", sDV, lambda v, j=j, gp=gp: v.scalar_tensor_tensor(
                    out=h_t[:, j*TT:(j+1)*TT],
                    in0=bass.AP(p8, 1024 + gp*512, [[4096, 128], [1, 512]]),
                    scalar=su_sb[:, j:j+1], in1=silu_g[:],
                    op0=ALU.mult, op1=ALU.mult))
                if j % 2 == 1 or j == NJG - 1:
                    rg.done(P, sPE)
                    ru.done(P, sPE)
            n_h = sDV.n
            gu_end[t] = sPE.n
            for half in range(2):
                cc0 = half * 2
                for cc in range(2):
                    c = cc0 + cc
                    par = c % 2
                    P.wait("scalar", sDV, sDV.n)
                    n_par[par] = P.dma(
                        "scalar", sRP[par], par_ch[par][:],
                        partial.ap()[t*TT + c*128:t*TT + (c+1)*128, :])
                    n_x[par] = P.dma(
                        "gpsimd", sRX[par], x_ch[par][:],
                        x_in.ap()[t*TT + c*128:t*TT + (c+1)*128, :])
                P.wait("tensor", sDV, sDV.n)
                P.wait("tensor", sDV, n_h)
                for r in range(NRD):
                    dbuf = rd.take(P)
                    for ki in range(2):
                        kc = r * 2 + ki
                        if kc >= NKD:
                            continue
                        for cc in range(2):
                            c = cc0 + cc
                            for fb in range(NFB):
                                last = ((ki == 1 or kc == NKD - 1)
                                        and fb == NFB - 1)
                                P.op("tensor",
                                     sPE if (last and cc == 1) else None,
                                     lambda t_, kc=kc, fb=fb, c=c, cc=cc,
                                     ki=ki, dbuf=dbuf:
                                     t_.matmul(
                                         p8[:, (cc*4 + fb)*512:(cc*4 + fb + 1)*512],
                                         h_t[:, kc*TT + c*128:kc*TT + (c+1)*128],
                                         dbuf[:, (ki*4 + fb)*512:(ki*4 + fb + 1)*512],
                                         start=(kc == 0), stop=(kc == NKD - 1)))
                    rd.done(P, sPE)
                n_dn = sPE.n
                for cc in range(2):
                    c = cc0 + cc
                    par = c % 2
                    P.wait("gpsimd", sRX[par], n_x[par])
                    P.wait("gpsimd", sRP[par], n_par[par])
                    P.wait("gpsimd", sPO, xp_rd[par])
                    P.op("gpsimd", sPO, lambda g, par=par: g.tensor_tensor(
                        out=xp_ch[par][:], in0=x_ch[par][:],
                        in1=par_ch[par][:], op=ALU.add))
                    xp_wr[par] = sPO.n
                P.wait("vector", sPE, n_dn)
                for cc in range(2):
                    c = cc0 + cc
                    par = c % 2
                    P.wait("vector", sOU[par], n_out[par])
                    for fb in range(NFB):
                        P.op("vector", sDV, lambda v, fb=fb, cc=cc, par=par:
                             v.tensor_tensor(
                                 out=tmp2[par][:, fb*512:(fb+1)*512],
                                 in0=p8[:, (cc*4 + fb)*512:(cc*4 + fb + 1)*512],
                                 in1=sd_mat[:, fb*512:(fb+1)*512], op=ALU.mult))
                    nmul = sDV.n
                    P.wait("gpsimd", sDV, nmul)
                    P.wait("gpsimd", sPO, xp_wr[par])
                    P.op("gpsimd", sPO, lambda g, par=par: g.tensor_tensor(
                        out=o_st[par][:], in0=tmp2[par][:],
                        in1=xp_ch[par][:], op=ALU.add))
                    xp_rd[par] = sPO.n
                    P.wait("scalar", sPO, sPO.n)
                    n_out[par] = P.dma(
                        "scalar", sOU[par],
                        out_t.ap()[t*TT + c*128:t*TT + (c+1)*128, :],
                        o_st[par][:])
        P.wait("scalar", sOU[0], sOU[0].n)
        P.wait("scalar", sOU[1], sOU[1].n)
        P.run(nc, "phP2b")

    if "p2b" in phases:
        _phase_p2b()

    Pd = Prog()
    alldma = [sSY, sSC, sGQ, sPE, sDV, sAC, sPO, sXB, sXC, sXN, sST,
              sSPx, sSPk, sSPv, sSPs, sSTP, sPW]
    alldma += [s for pr in ring_sems for s in pr]
    alldma += sLK + sLV + sLO + sRP + sRX + sOU
    for s in alldma:
        Pd.wait("sync", s)
    for k in sAG:
        Pd.wait("sync", sAG[k], sAG[k].n)
    for dout, din in dbg_pairs:
        Pd.dma("sync", sSY, dout.ap(), din.ap())
    Pd.wait("sync", sSY)
    Pd.run(nc, "phDrain")

    es.close()
    return nc


def _host_consts():
    s = _slopes().astype(np.float64)
    i = np.arange(CH, dtype=np.float64)
    qd = np.exp(-s[:, None] * (i[None, :] + 1.0))
    kd = np.exp(-s[None, :] * (CH - 1.0 - i[:, None]))
    diff = i[None, :] - i[:, None]
    dd = np.where(diff >= 0,
                  np.exp(-s[:, None, None] * diff[None, :, :]), 0.0)
    return dict(
        ident=np.eye(128),
        ones_col=np.ones((128, 1)),
        ones_row=np.ones((1, 128), dtype=np.float32),
        iota_fm=np.tile(np.arange(CH, dtype=np.float32)[None, :], (128, H)),
        iota_col=np.arange(CH, dtype=np.float32)[:, None].copy(),
        qd_fm=np.tile(qd.reshape(1, H * CH), (128, 1)).astype(np.float32),
        kd_tm=np.repeat(kd, E, axis=1).astype(np.float32),
        ddt=dd.transpose(1, 0, 2).reshape(CH, H * CH).astype(np.float32),
    )


def make_inputs(core, consts, L0_attn_x, L0_attn_w_qkv, L0_attn_w_proj,
                L0_attn_theta, L0_ffn_wq_gate, L0_ffn_s_gate, L0_ffn_wq_up,
                L0_ffn_s_up, L0_ffn_wq_down, L0_ffn_s_down):
    import ml_dtypes
    bf = ml_dtypes.bfloat16
    b, half = core // 2, core % 2
    c = core
    xq = np.asarray(L0_attn_x)
    wqkv = np.asarray(L0_attn_w_qkv)
    wproj = np.asarray(L0_attn_w_proj)
    theta = np.asarray(L0_attn_theta).reshape(H, E)
    wg = np.asarray(L0_ffn_wq_gate)
    wu = np.asarray(L0_ffn_wq_up)
    wd = np.asarray(L0_ffn_wq_down)
    wq = wqkv[0:D]
    wk = wqkv[D:2*D]
    wv = wqkv[2*D:3*D]
    fv, kh = c // 2, c % 2
    gpad = np.zeros((NJGP * 128, D), np.int32)
    gpad[:HID] = wg
    upad = np.zeros((NJGP * 128, D), np.int32)
    upad[:HID] = wu
    dpad = np.zeros((D, NKDP * 128), np.int32)
    dpad[:, :HID] = wd
    return {
        "x": np.ascontiguousarray(xq[b, half*TPC:(half+1)*TPC, :],
                                  dtype=np.float32),
        "wq_sh": np.ascontiguousarray(wq[c*256:(c+1)*256], dtype=np.float32),
        "wk_sh": np.ascontiguousarray(wk[c*256:(c+1)*256], dtype=np.float32),
        "wv_sh": np.ascontiguousarray(
            wv[fv*512:(fv+1)*512, kh*1024:(kh+1)*1024], dtype=np.float32),
        "wkt_sh": np.ascontiguousarray(
            wk[fv*512:(fv+1)*512, kh*1024:(kh+1)*1024], dtype=np.float32),
        "wp_sh": np.ascontiguousarray(
            wproj[fv*512:(fv+1)*512, kh*1024:(kh+1)*1024], dtype=np.float32),
        "wg_sh": np.ascontiguousarray(gpad[c*768:(c+1)*768]),
        "wu_sh": np.ascontiguousarray(upad[c*768:(c+1)*768]),
        "wd_sh": np.ascontiguousarray(dpad[:, c*768:(c+1)*768]),
        "sg_t": np.ascontiguousarray(
            np.asarray(L0_ffn_s_gate).reshape(NJG, 128).T, dtype=np.float32),
        "su_t": np.ascontiguousarray(
            np.asarray(L0_ffn_s_up).reshape(NJG, 128).T, dtype=np.float32),
        "sd_row": np.ascontiguousarray(
            np.asarray(L0_ffn_s_down).reshape(1, D), dtype=np.float32),
        "theta_tm": np.ascontiguousarray(theta.reshape(1, D),
                                         dtype=np.float32),
        "theta_t": np.ascontiguousarray(theta.T, dtype=np.float32),
        "ident": consts["ident"].astype(bf),
        "ones_col": consts["ones_col"].astype(bf),
        "ones_row": consts["ones_row"],
        "iota_fm": consts["iota_fm"],
        "iota_col": consts["iota_col"],
        "pos_col": np.full((128, 1), float(half * TPC), np.float32),
        "mask_col": np.full((128, 1), float(half), np.float32),
        "qd_fm": consts["qd_fm"],
        "kd_tm": consts["kd_tm"],
        "ddt": consts["ddt"],
    }


DEV_PHASES = ("w", "p1", "p2a", "p2b")
DEV_DEBUG = ()


def kernel(**inputs):
    consts = _host_consts()
    in_maps = [make_inputs(c, consts, **inputs) for c in range(8)]
    nc = build_program(debug=DEV_DEBUG, phases=DEV_PHASES)
    res = run_bass_kernel_spmd(nc, in_maps, core_ids=list(range(8)))
    out = np.empty((B, T, D), np.float32)
    for c in range(8):
        b, half = c // 2, c % 2
        out[b, half*TPC:(half+1)*TPC, :] = res.results[c]["out"]
    return out


# version 24
# speedup vs baseline: 1.1054x; 1.0754x over previous
import math
from contextlib import ExitStack

import numpy as np

import concourse.bass as bass
import concourse.mybir as mybir
from concourse.bass_utils import run_bass_kernel_spmd

F32 = mybir.dt.float32
BF16 = mybir.dt.bfloat16
I32 = mybir.dt.int32
AF = mybir.ActivationFunctionType
ALU = mybir.AluOpType

B, T, D = 4, 8192, 2048
H, E = 16, 128
HID = 5632
CH = 128
TPC = T // 2
NT = 8
TT = TPC // NT
NCH = TT // CH
NKC = D // 128
NJG = HID // 128
NJGP = 48
NFB = D // 512
NKD = NJG
NKDP = 48
PI = float(np.pi)
TWO_PI = float(2.0 * np.pi)
RND_M = 12582912.0

LQ = 16 * NKC * 128
LV = 4 * NKC * 512
LP = NFB * H * 512
LG = NJGP * NKC * 128
LDN = NKDP * NFB * 512

ENGS = ("sync", "scalar", "vector", "tensor", "gpsimd")


def _slopes():
    start = 2.0 ** (-(2.0 ** (-(math.log2(H) - 3))))
    return np.array([start ** (i + 1) for i in range(H)], dtype=np.float32)


class Sem:
    def __init__(self, es, nc, name):
        self.h = es.enter_context(nc.semaphore(name))
        self.n = 0


class Prog:

    TRACE = []

    def __init__(self):
        self.q = {e: [] for e in ENGS}

    def wait(self, eng, sem, n=None):
        n = sem.n if n is None else n
        if n > 0:
            Prog.TRACE.append((eng, "wait", sem.h.name, n))
            self.q[eng].append(lambda e, h=sem.h, n=n: e.wait_ge(h, n))

    def dma(self, eng, sem, out, in_, **kw):
        def f(e, out=out, in_=in_, kw=kw, h=sem.h):
            e.dma_start(out=out, in_=in_, **kw).then_inc(h, 16)
        Prog.TRACE.append((eng, "inc", sem.h.name, 16))
        self.q[eng].append(f)
        sem.n += 16
        return sem.n

    def op(self, eng, sem, fn):
        if sem is None:
            Prog.TRACE.append((eng, "op", None, 0))
            self.q[eng].append(fn)
            return None
        Prog.TRACE.append((eng, "inc", sem.h.name, 1))
        self.q[eng].append(lambda e, fn=fn, h=sem.h: fn(e).then_inc(h, 1))
        sem.n += 1
        return sem.n

    def run(self, nc, name):
        with nc.Block(name) as blk:
            for eng in ENGS:
                fns = self.q[eng]
                if fns:
                    getattr(blk, eng)(lambda e, fns=fns: [f(e) for f in fns])


def build_program(debug=(), phases=("w", "p1", "p2a", "p2b")):
    nc = bass.Bass("TRN2", target_bir_lowering=False)
    es = ExitStack()

    dbg_pairs = []

    def dram(name, shape, dt, kind="Internal"):
        th = nc.dram_tensor(name, shape, dt, kind=kind)
        if kind == "Internal" and name in debug:
            dbg_pairs.append(
                (nc.dram_tensor("dbg_" + name, shape, dt,
                                kind="ExternalOutput"), th))
        return th

    def sbt(name, shape, dt, stack=None):
        return (stack or es).enter_context(nc.sbuf_tensor(name, shape, dt))

    def pst(name, shape, dt, stack=None):
        return (stack or es).enter_context(nc.psum_tensor(name, shape, dt))

    def inp(name, shape, dt):
        return nc.dram_tensor(name, shape, dt, kind="ExternalInput")

    x_in = inp("x", [TPC, D], F32)
    wq_sh = inp("wq_sh", [256, D], F32)
    wk_sh = inp("wk_sh", [256, D], F32)
    wv_sh = inp("wv_sh", [512, 1024], F32)
    wkt_sh = inp("wkt_sh", [512, 1024], F32)
    wp_sh = inp("wp_sh", [512, 1024], F32)
    wg_sh = inp("wg_sh", [768, D], I32)
    wu_sh = inp("wu_sh", [768, D], I32)
    wd_sh = inp("wd_sh", [D, 768], I32)
    sg_t = inp("sg_t", [128, NJG], F32)
    su_t = inp("su_t", [128, NJG], F32)
    sd_row = inp("sd_row", [1, D], F32)
    th_tm_in = inp("theta_tm", [1, D], F32)
    th_t_in = inp("theta_t", [128, H], F32)
    ident_in = inp("ident", [128, 128], BF16)
    ones_col_in = inp("ones_col", [128, 1], BF16)
    ones_row_in = inp("ones_row", [1, 128], F32)
    iota_fm_in = inp("iota_fm", [128, D], F32)
    iota_col_in = inp("iota_col", [128, 1], F32)
    pos_col_in = inp("pos_col", [128, 1], F32)
    mask_col_in = inp("mask_col", [128, 1], F32)
    qd_fm_in = inp("qd_fm", [128, D], F32)
    kd_tm_in = inp("kd_tm", [128, D], F32)
    ddt_in = inp("ddt", [128, D], F32)
    out_t = nc.dram_tensor("out", [TPC, D], F32, kind="ExternalOutput")

    pre_loc, pre_all = {}, {}
    for nm, L in (("sq", LQ), ("sk", LQ), ("sv", LV), ("skt", LV), ("sp", LP),
                  ("sg", LG), ("su", LG), ("sd", LDN)):
        pre_loc[nm] = dram("pre_%s_loc" % nm, [128, L // 8], BF16)
        pre_all[nm] = dram("pre_%s_all" % nm, [1024, L // 8], BF16)
    x_bf = dram("x_bf", [TPC, D], BF16)
    xnfm_sp = dram("xnfm_sp", [128, NT * NKC * TT], BF16)
    kfm_sp = dram("kfm_sp", [128, 32 * D], BF16)
    vtm_sp = dram("vtm_sp", [128, 32 * D], BF16)
    st_sp = dram("st_sp", [128, 32 * H * 256], BF16)
    ktc_sp = dram("ktc_sp", [128, 32 * D], BF16)
    o_sp = dram("o_sp", [128, NT * H * TT], BF16)
    sT_sp = dram("sT_sp", [128, 32 * D], BF16)
    qp_sp = dram("qp_sp", [128, 32 * D], BF16)
    kp_sp = dram("kp_sp", [128, 32 * D], BF16)
    trc_sp = dram("trc_sp", [128, 32 * D], BF16)
    trs_sp = dram("trs_sp", [128, 32 * D], BF16)
    st_loc = dram("st_loc", [128, H * 256], F32)
    st_pair = dram("st_pair", [256, H * 256], F32)
    partial = dram("partial", [TPC, D], BF16)

    ident = sbt("c_ident", [128, 128], BF16)
    ones_col = sbt("c_ones_col", [128, 1], BF16)
    ones_row = sbt("c_ones_row", [1, 128], F32)
    iota_fm = sbt("c_iota_fm", [128, D], F32)
    iota_col = sbt("c_iota_col", [128, 1], F32)
    pos_col = sbt("c_pos_col", [128, 1], F32)
    mask_col = sbt("c_mask_col", [128, 1], F32)
    theta_tm = sbt("c_theta_tm", [128, D], F32)
    theta_t = sbt("c_theta_t", [128, H], F32)
    half_pi = sbt("c_half_pi", [128, 1], F32)
    c_n1 = sbt("c_n1", [128, 1], F32)

    sSY = Sem(es, nc, "sSY")
    sSC = Sem(es, nc, "sSC")
    sGQ = Sem(es, nc, "sGQ")
    sPE = Sem(es, nc, "sPE")
    sDV = Sem(es, nc, "sDV")
    sAC = Sem(es, nc, "sAC")
    sPO = Sem(es, nc, "sPO")
    sAG = {k: Sem(es, nc, "sAG" + k) for k in
           ("sk", "sv", "skt", "sq", "sp", "sg", "su", "sd", "st")}
    ring_sems = [(Sem(es, nc, "sR%da" % i), Sem(es, nc, "sR%db" % i),
                  Sem(es, nc, "sR%dc" % i)) for i in range(3)]
    sXB = Sem(es, nc, "sXB")
    sXC = Sem(es, nc, "sXC")
    sXN = Sem(es, nc, "sXN")
    sST = Sem(es, nc, "sST")
    sSPx = Sem(es, nc, "sSPx")
    sSPk = Sem(es, nc, "sSPk")
    sSPv = Sem(es, nc, "sSPv")
    sSPs = Sem(es, nc, "sSPs")
    sSTP = Sem(es, nc, "sSTP")
    sPW = Sem(es, nc, "sPW")
    sLK = [Sem(es, nc, "sLK%d" % i) for i in range(2)]
    sLV = [Sem(es, nc, "sLV%d" % i) for i in range(2)]
    sLO = [Sem(es, nc, "sLO%d" % i) for i in range(2)]
    sRP = [Sem(es, nc, "sRP%d" % i) for i in range(2)]
    sRX = [Sem(es, nc, "sRX%d" % i) for i in range(2)]
    sOU = [Sem(es, nc, "sOU%d" % i) for i in range(2)]

    slopes = _slopes()
    bdec = [float(np.exp(np.float64(-s) * CH))
            for s in slopes.astype(np.float64)]

    def stream_runs(nm, c0, c1):
        Lc = pre_all[nm].shape[1]
        runs, c = [], c0
        while c < c1:
            e = min(c1, (c // Lc + 1) * Lc)
            runs.append((c, e))
            c = e
        return runs

    def stream_ap(nm, c0, c1):
        Lc = pre_all[nm].shape[1]
        s = c0 // Lc
        return pre_all[nm].ap()[s * 128:(s + 1) * 128, c0 - s * Lc:c1 - s * Lc]

    class Ring:
        def __init__(self, name, nm, ncols, stack, sems, nbuf=2):
            self.nm, self.ncols, self.nbuf = nm, ncols, nbuf
            self.sems = sems
            self.bufs = [sbt("%s%d" % (name, i), [128, ncols], BF16, stack)
                         for i in range(nbuf)]
            self.loaded = []
            self.freed = []
            self.issue_i = 0
            self.cons_i = 0
            self.take_i = 0

        def issue(self, P, c0):
            i = self.issue_i % self.nbuf
            if self.issue_i >= self.nbuf:
                sem, n = self.freed[self.issue_i - self.nbuf]
                P.wait("sync", sem, n)
            for (a, b) in stream_runs(self.nm, c0, c0 + self.ncols):
                P.dma("sync", self.sems[i], self.bufs[i][:, a - c0:b - c0],
                      stream_ap(self.nm, a, b))
            self.loaded.append((i, self.sems[i].n))
            self.issue_i += 1

        def use(self, P, eng="tensor"):
            i, n = self.loaded[self.take_i]
            self.take_i += 1
            P.wait(eng, self.sems[i], n)
            return self.bufs[i]

        def release(self, sem):
            self.freed.append((sem, sem.n))
            self.cons_i += 1

        def start(self, P, seq):
            self.seq = list(seq)
            self.si = 0
            for i in range(min(self.nbuf, len(self.seq))):
                self.issue(P, self.seq[i])

        def take(self, P, eng="tensor"):
            return self.use(P, eng)

        def done(self, P, sem):
            self.release(sem)
            nxt = self.si + self.nbuf
            if nxt < len(self.seq):
                self.issue(P, self.seq[nxt])
            self.si += 1

    ctx = {"n_xbf": 0}

    def _phase_w():
      with ExitStack() as stw:
        stag = sbt("w_stag", [128, 12288], BF16, stw)
        rowf = [sbt("w_rowf%d" % i, [128, D], F32, stw) for i in range(2)]
        rowi = [sbt("w_rowi%d" % i, [128, D], I32, stw) for i in range(2)]
        rowb = [sbt("w_rowb%d" % i, [128, D], BF16, stw) for i in range(2)]
        ptr = pst("w_ptr", [128, 8192], BF16, stw)
        P = Prog()
        for dst, src in ((ident, ident_in), (ones_col, ones_col_in),
                         (ones_row, ones_row_in), (iota_fm, iota_fm_in),
                         (iota_col, iota_col_in), (pos_col, pos_col_in),
                         (mask_col, mask_col_in), (theta_t, th_t_in)):
            P.dma("sync", sSY, dst[:], src.ap())
        P.dma("sync", sSY, theta_tm[:], th_tm_in.ap().broadcast_to([128, D]))
        P.op("vector", sDV, lambda v: v.memset(half_pi[:], PI / 2.0))
        P.op("vector", sDV, lambda v: v.memset(c_n1[:], -1.0))
        nc.const_aps.register(F32, -1.0, c_n1[:, 0:1])
        P.wait("tensor", sSY)

        for t in range(NT):
            P.dma("gpsimd", sXC, x_bf.ap()[t * TT:(t + 1) * TT, :],
                  x_in.ap()[t * TT:(t + 1) * TT, :])
        ctx["n_xbf"] = sXC.n

        jobs = [
            ("sk", [(wk_sh.ap()[j*128:(j+1)*128, :], False, NKC,
                     (lambda kc, j=j: j*NKC + kc)) for j in range(2)], 2*NKC),
            ("sv", [(wv_sh.ap()[j*128:(j+1)*128, :], False, 8,
                     (lambda kc, j=j: kc*4 + j)) for j in range(4)], 32),
            ("skt", [(wkt_sh.ap()[j*128:(j+1)*128, :], False, 8,
                      (lambda kc, j=j: kc*4 + j)) for j in range(4)], 32),
            ("sq", [(wq_sh.ap()[j*128:(j+1)*128, :], False, NKC,
                     (lambda kc, j=j: j*NKC + kc)) for j in range(2)], 2*NKC),
            ("sp", [(wp_sh.ap()[j*128:(j+1)*128, :], False, 8,
                     (lambda kc, j=j: kc*4 + j)) for j in range(4)], 32),
            ("sg", [(wg_sh.ap()[j*128:(j+1)*128, :], True, NKC,
                     (lambda kc, j=j: j*NKC + kc)) for j in range(6)], 6*NKC),
            ("su", [(wu_sh.ap()[j*128:(j+1)*128, :], True, NKC,
                     (lambda kc, j=j: j*NKC + kc)) for j in range(6)], 6*NKC),
            ("sd", [(wd_sh.ap()[j*128:(j+1)*128, :], True, 6,
                     (lambda kc, j=j: kc*16 + j)) for j in range(16)], 96),
        ]
        ag_ready = {}
        pi = 0
        cv_done = {}
        tr_done = {}
        for nm, parts, nblk in jobs:
            n_store_prev = sST.n
            for srcap, is_int, nkc, scol in parts:
                cols = srcap.shape[1]
                par = pi % 2
                rbuf = rowi[par] if is_int else rowf[par]
                rb = rowb[par]
                P.wait("scalar", sDV, cv_done.get(pi - 2, 0))
                nld = P.dma("scalar", sRX[par], rbuf[:, 0:cols], srcap)
                P.wait("vector", sRX[par], nld)
                P.wait("vector", sPE, tr_done.get(pi - 2, 0))
                P.op("vector", sDV, lambda v, rbuf=rbuf, rb=rb, cols=cols,
                     a=(-128.0 if is_int else 0.0):
                     v.tensor_scalar(out=rb[:, 0:cols], in0=rbuf[:, 0:cols],
                                     scalar1=a, scalar2=None, op0=ALU.add))
                cv_done[pi] = sDV.n
                P.wait("tensor", sDV, cv_done[pi])
                for k0 in range(0, nkc, 8):
                    nb = min(8, nkc - k0)
                    P.wait("tensor", sDV, sDV.n)
                    for i in range(nb):
                        P.op("tensor", sPE if i == nb - 1 else None,
                             lambda t_, kc=k0 + i, i=i, rb=rb: t_.transpose(
                                 ptr[:, i*1024:i*1024 + 128],
                                 rb[:, kc*128:(kc+1)*128], ident[:]))
                    P.wait("vector", sPE, sPE.n)
                    P.wait("vector", sST, n_store_prev)

                    def ev(v, k0=k0, nb=nb, scol=scol):
                        r = None
                        for i in range(nb):
                            r = v.tensor_copy(
                                out=stag[:, scol(k0+i)*128:(scol(k0+i)+1)*128],
                                in_=bass.AP(ptr, i*1024, [[8192, 128], [1, 128]]))
                        return r
                    P.op("vector", sDV, ev)
                tr_done[pi] = sPE.n
                pi += 1
            P.wait("sync", sDV, sDV.n)
            P.dma("sync", sST, pre_loc[nm].ap()[:, 0:nblk*128],
                  stag[:, 0:nblk*128])
            ag_ready[nm] = sST.n
            P.wait("gpsimd", sST, ag_ready[nm])
            Prog.TRACE.append(("gpsimd", "inc", "sAG" + nm, 1))
            P.op("gpsimd", None, lambda g, nm=nm: g.collective_compute(
                "AllGather", ALU.bypass,
                replica_groups=[[0, 1, 2, 3, 4, 5, 6, 7]],
                ins=[pre_loc[nm].ap()], outs=[pre_all[nm].ap()],
            ).then_inc(sAG[nm].h, 1))
            sAG[nm].n = 1
        P.run(nc, "phW")

    if "w" in phases:
        _phase_w()

    def _phase_p1():
      with ExitStack() as st1:
        xf = [sbt("p1_xf%d" % i, [128, NKC * TT], BF16, st1)
              for i in range(2)]
        sq_sc = [sbt("p1_sq%d" % i, [128, 512], BF16, st1) for i in range(2)]
        row_s = sbt("p1_rows", [1, 4 * TT], F32, st1)
        k_fm = sbt("p1_kfm", [128, NCH * D], BF16, st1)
        v_tm = sbt("p1_vtm", [128, NCH * D], BF16, st1)
        k_tm = sbt("p1_ktm", [128, NCH * D], BF16, st1)
        kkd = sbt("p1_kkd", [128, D], BF16, st1)
        ktc = [sbt("p1_ktc%d" % i, [128, D], BF16, st1) for i in range(2)]
        kts = [sbt("p1_kts%d" % i, [128, D], BF16, st1) for i in range(2)]
        trc = sbt("p1_trc", [128, D], BF16, st1)
        trs = sbt("p1_trs", [128, D], BF16, st1)
        ph_b = sbt("p1_ph", [128, D], F32, st1)
        rr_b = sbt("p1_rr", [128, D], F32, st1)
        kd_tm = sbt("p1_kdtm", [128, D], BF16, st1)
        states = sbt("p1_st", [128, H * 256], F32, st1)
        st_bf = sbt("p1_stbf", [128, H * 256], BF16, st1)
        tcol = sbt("p1_tcol", [128, 1], F32, st1)
        c2pi = sbt("p1_c2pi", [128, 1], F32, st1)
        p_row = pst("p1_prow", [1, 1024], F32, st1)
        p_bc = pst("p1_pbc", [128, 1024], F32, st1)
        p_a = pst("p1_pa", [128, 2048], F32, st1)
        rk = Ring("p1_rk", "sk", 2048, st1, ring_sems[0], nbuf=3)
        rv = Ring("p1_rv", "sv", 2048, st1, ring_sems[1])
        rkt = Ring("p1_rkt", "skt", 2048, st1, ring_sems[2])

        P = Prog()
        P.dma("gpsimd", sGQ, kd_tm[:], kd_tm_in.ap())
        n_kdtm = sGQ.n
        P.op("vector", sDV, lambda v: v.memset(states[:], 0.0))
        P.op("vector", sDV, lambda v: v.memset(c2pi[:], -TWO_PI))
        P.wait("sync", sAG["sk"], 1)
        P.wait("sync", sAG["sv"], 1)
        P.wait("sync", sAG["skt"], 1)
        rk.start(P, [j * 2048 for _ in range(NT) for j in range(H)])
        seq_v = [r * 2048 for _ in range(NT) for r in range(16)]
        rv.start(P, seq_v)
        rkt.start(P, seq_v)

        st_c = {"spill": 0, "upd": 0, "mm_done": {}}
        sq_done = {}

        def xbar(t, bf):
            P.wait("sync", sXC, ctx["n_xbf"])
            for kc in range(NKC):
                P.dma("sync", sXB, bf[:, kc*TT:(kc+1)*TT],
                      x_bf.ap()[t*TT:(t+1)*TT, kc*128:(kc+1)*128],
                      transpose=True)
            return sXB.n

        def layer_norm(t, bf, n_xfm, n_spill_xn):
            P.wait("tensor", sXB, n_xfm)
            P.wait("tensor", sDV, sDV.n)
            for kc in range(NKC):
                P.wait("scalar", sXB, n_xfm)
                if kc >= 2:
                    P.wait("scalar", sPE, sq_done[kc - 2])
                elif sq_done.get(NKC - 2 + kc):
                    P.wait("scalar", sPE, sq_done[NKC - 2 + kc])
                P.op("scalar", sAC, lambda s, kc=kc, bf=bf: s.activation(
                    out=sq_sc[kc % 2][:], in_=bf[:, kc*TT:(kc+1)*TT],
                    func=AF.Square))
                P.op("tensor", None, lambda t_, kc=kc, bf=bf: t_.matmul(
                    p_row[0:1, 0:TT], ones_col[:], bf[:, kc*TT:(kc+1)*TT],
                    start=(kc == 0), stop=(kc == NKC - 1)))
                P.wait("tensor", sAC, sAC.n)
                sq_done[kc] = P.op("tensor", sPE, lambda t_, kc=kc: t_.matmul(
                    p_row[0:1, 512:512 + TT], ones_col[:], sq_sc[kc % 2][:],
                    start=(kc == 0), stop=(kc == NKC - 1)))
            P.wait("vector", sPE, sPE.n)
            P.op("vector", sDV, lambda v: v.tensor_scalar(
                out=row_s[0:1, 0:TT], in0=p_row[0:1, 0:TT],
                scalar1=-1.0 / D, scalar2=None, op0=ALU.mult))
            P.op("vector", sDV, lambda v: v.tensor_tensor(
                out=row_s[0:1, 1024:1024 + TT], in0=row_s[0:1, 0:TT],
                in1=row_s[0:1, 0:TT], op=ALU.mult))
            P.op("vector", sDV, lambda v: v.tensor_scalar(
                out=row_s[0:1, 1536:1536 + TT], in0=p_row[0:1, 512:512 + TT],
                scalar1=1.0 / D, scalar2=None, op0=ALU.mult))
            P.op("vector", sDV, lambda v: v.tensor_tensor(
                out=row_s[0:1, 1024:1024 + TT], in0=row_s[0:1, 1536:1536 + TT],
                in1=row_s[0:1, 1024:1024 + TT], op=ALU.subtract))
            P.op("vector", sDV, lambda v: v.tensor_scalar(
                out=row_s[0:1, 1024:1024 + TT], in0=row_s[0:1, 1024:1024 + TT],
                scalar1=1e-5, scalar2=None, op0=ALU.add))
            P.wait("scalar", sDV, sDV.n)
            P.op("scalar", sAC, lambda s: s.activation(
                out=row_s[0:1, 1536:1536 + TT], in_=row_s[0:1, 1024:1024 + TT],
                func=AF.Sqrt))
            P.wait("vector", sAC, sAC.n)
            P.op("vector", sDV, lambda v: v.reciprocal(
                out=row_s[0:1, 512:512 + TT], in_=row_s[0:1, 1536:1536 + TT]))
            n_rows = sDV.n
            P.wait("tensor", sDV, n_rows)
            P.op("tensor", None, lambda t_: t_.matmul(
                p_bc[:, 0:512], ones_row[:], row_s[0:1, 0:TT],
                start=True, stop=True))
            P.op("tensor", sPE, lambda t_: t_.matmul(
                p_bc[:, 512:1024], ones_row[:], row_s[0:1, 512:512 + TT],
                start=True, stop=True))
            P.wait("vector", sPE, sPE.n)
            P.wait("vector", sSPx, n_spill_xn)
            nm_rep = bass.AP(p_bc, 0, [[1024, 128], [0, NKC], [1, TT]])
            rs_rep = bass.AP(p_bc, 512, [[1024, 128], [0, NKC], [1, TT]])
            xn3 = bf[:].rearrange("p (kc t) -> p kc t", kc=NKC)
            P.op("vector", sDV, lambda v: v.tensor_tensor(
                out=xn3, in0=xn3, in1=nm_rep, op=ALU.add))
            P.wait("vector", sDV, sDV.n)
            P.op("vector", sDV, lambda v: v.tensor_tensor(
                out=xn3, in0=xn3, in1=rs_rep, op=ALU.mult))
            n_xn = sDV.n
            P.wait("sync", sDV, n_xn)
            return n_xn, P.dma("sync", sSPx,
                               xnfm_sp.ap()[:, t*NKC*TT:(t+1)*NKC*TT], bf[:])

        def chain(t, c, pe_done_before):
            gt0 = t * TT + c * CH
            par = c % 2
            P.wait("vector", sPO, st_c.get("ph_rd", 0))
            P.op("vector", sDV, lambda v, gt0=gt0: v.tensor_scalar(
                out=tcol[:], in0=iota_col[:], scalar1=pos_col[:, 0:1],
                scalar2=float(gt0), op0=ALU.add, op1=ALU.add))
            P.wait("vector", sDV, sDV.n)
            P.wait("gpsimd", sDV, sDV.n)
            P.wait("gpsimd", sAC, st_c.get("tr_rd", 0))
            P.wait("gpsimd", sGQ, n_kdtm)
            P.op("gpsimd", sPO, lambda g: g.tensor_tensor(
                out=ph_b[:], in0=theta_tm[:],
                in1=tcol[:].broadcast_to([128, D]), op=ALU.mult))
            n_ph = sPO.n
            P.wait("scalar", sPO, n_ph)
            P.op("scalar", sAC, lambda s: s.activation(
                out=rr_b[:], in_=ph_b[:], func=AF.Copy,
                scale=1.0 / TWO_PI, bias=RND_M))
            P.op("scalar", sAC, lambda s: s.activation(
                out=rr_b[:], in_=rr_b[:], func=AF.Copy, bias=-RND_M))
            n_rr = sAC.n
            P.wait("gpsimd", sAC, n_rr)
            P.op("gpsimd", sPO, lambda g: g.tensor_tensor(
                out=rr_b[:], in0=rr_b[:],
                in1=c2pi[:].broadcast_to([128, D]), op=ALU.mult))
            P.wait("gpsimd", sPO, sPO.n)
            P.op("gpsimd", sPO, lambda g: g.tensor_tensor(
                out=rr_b[:], in0=rr_b[:], in1=ph_b[:], op=ALU.add))
            st_c["ph_rd"] = sPO.n
            P.wait("scalar", sPO, sPO.n)
            P.op("scalar", sAC, lambda s: s.activation(
                out=trs[:], in_=rr_b[:], func=AF.Sin))
            P.op("scalar", sAC, lambda s: s.activation(
                out=rr_b[:], in_=rr_b[:], func=AF.Abs))
            P.op("scalar", sAC, lambda s: s.activation(
                out=trc[:], in_=rr_b[:], func=AF.Sin,
                scale=-1.0, bias=half_pi[:, 0:1]))
            st_c["tr_rd"] = 0
            n_cos = sAC.n
            P.wait("gpsimd", sAC, n_cos)
            P.wait("gpsimd", sPE, pe_done_before)
            P.op("gpsimd", sPO, lambda g, c=c: g.tensor_tensor(
                out=kkd[:], in0=k_tm[:, c*D:(c+1)*D], in1=kd_tm[:],
                op=ALU.mult))
            P.wait("gpsimd", sPO, sPO.n)
            P.op("gpsimd", sPO, lambda g, par=par: g.tensor_tensor(
                out=ktc[par][:], in0=kkd[:], in1=trc[:], op=ALU.mult))
            P.op("gpsimd", sPO, lambda g, par=par: g.tensor_tensor(
                out=kts[par][:], in0=kkd[:], in1=trs[:], op=ALU.mult))
            st_c["tr_rd"] = sPO.n
            return sPO.n

        def scan_mms(t, c, n_pool, n_vev):
            par = c % 2
            P.wait("vector", sSPs, st_c["spill"])
            P.op("vector", sDV,
                 lambda v: v.tensor_copy(out=st_bf[:], in_=states[:]))
            P.wait("sync", sDV, sDV.n)
            st_c["spill"] = P.dma(
                "sync", sSPs,
                st_sp.ap()[:, (t*NCH + c)*H*256:(t*NCH + c + 1)*H*256],
                st_bf[:])
            for grp in range(4):
                P.wait("tensor", sPO, n_pool)
                P.wait("tensor", sDV, st_c["upd"])
                P.wait("tensor", sDV, n_vev)
                for hh in range(4):
                    h = grp * 4 + hh
                    P.op("tensor", None,
                         lambda t_, h=h, hh=hh, c=c, par=par:
                         t_.matmul(
                             p_a[:, 1024 + hh*256:1024 + hh*256 + 128],
                             ktc[par][:, h*128:(h+1)*128],
                             v_tm[:, c*D + h*128:c*D + (h+1)*128],
                             start=(hh % 2 == 0), stop=False))
                    P.op("tensor", sPE if hh == 3 else None,
                         lambda t_, h=h, hh=hh, c=c, par=par:
                         t_.matmul(
                             p_a[:, 1024 + hh*256 + 128:1024 + (hh+1)*256],
                             kts[par][:, h*128:(h+1)*128],
                             v_tm[:, c*D + h*128:c*D + (h+1)*128],
                             start=False, stop=(hh % 2 == 1)))
                P.wait("vector", sPE, sPE.n)
                for hh in range(4):
                    h = grp * 4 + hh
                    P.op("vector", sDV if hh == 3 else None,
                         lambda v, h=h, hh=hh:
                         v.scalar_tensor_tensor(
                             out=states[:, h*256:(h+1)*256],
                             in0=states[:, h*256:(h+1)*256],
                             scalar=bdec[h],
                             in1=p_a[:, 1024 + hh*256:1024 + (hh+1)*256],
                             op0=ALU.mult, op1=ALU.add))
                st_c["upd"] = sDV.n
            st_c["mm_done"][t * NCH + c] = sPE.n

        n_spill_k = n_spill_v = 0
        spx = {0: 0, 1: 0}
        prod_end = {-1: 0, -2: 0}
        n_xn_t = {}
        n_xfm = xbar(0, xf[0])
        n_xn_t[0], spx[0] = layer_norm(0, xf[0], n_xfm, 0)
        for t in range(NT):
            cur = xf[t % 2]
            n_xn = n_xn_t[t]
            if t + 1 < NT:
                nb = xf[(t + 1) % 2]
                P.wait("sync", sPE, prod_end[t - 1])
                P.wait("sync", sSPx, spx[(t + 1) % 2])
                n_xfm2 = xbar(t + 1, nb)
            P.wait("tensor", sDV, n_xn)
            P.wait("vector", sSPv, n_spill_v)
            P.wait("vector", sPO, sPO.n)
            for ring, dst in ((rkt, k_tm), (rv, v_tm)):
                for fv in range(4):
                    P.wait("tensor", sDV, sDV.n)
                    for qtr in range(4):
                        buf = ring.take(P)
                        for kci in range(4):
                            kc = qtr * 4 + kci
                            for c in range(NCH):
                                P.op("tensor",
                                     sPE if (kci == 3 and c == NCH - 1) else None,
                                     lambda t_, kc=kc, kci=kci, c=c, buf=buf,
                                     cur=cur:
                                     t_.matmul(
                                         p_a[:, c*512:(c+1)*512],
                                         cur[:, kc*TT + c*128:kc*TT + (c+1)*128],
                                         buf[:, kci*512:(kci+1)*512],
                                         start=(kc == 0), stop=(kc == NKC - 1)))
                        ring.done(P, sPE)
                    P.wait("vector", sPE, sPE.n)
                    P.op("vector", sDV, lambda v, fv=fv, dst=dst: v.tensor_copy(
                        out=bass.AP(dst, fv * 512,
                                    [[NCH * D, 128], [D, NCH], [1, 512]]),
                        in_=bass.AP(p_a, 0, [[2048, 128], [512, NCH], [1, 512]])))
            n_vev = sDV.n
            P.wait("sync", sDV, n_vev)
            n_spill_v = P.dma("sync", sSPv,
                              vtm_sp.ap()[:, t*NCH*D:(t+1)*NCH*D], v_tm[:])
            if t + 1 < NT:
                n_xn_t[t + 1], spx[(t + 1) % 2] = layer_norm(
                    t + 1, nb, n_xfm2, spx[(t + 1) % 2])
            P.wait("vector", sSPk, n_spill_k)
            chains = {}
            evk = {}
            for j in range(H):
                buf = rk.take(P)
                P.wait("tensor", sDV, evk.get(j - 2, n_vev))
                for kc in range(NKC):
                    P.op("tensor", sPE if kc == NKC - 1 else None,
                         lambda t_, j=j, kc=kc, buf=buf, cur=cur: t_.matmul(
                             p_a[:, (j % 2)*512:(j % 2)*512 + 512],
                             buf[:, kc*128:(kc+1)*128],
                             cur[:, kc*TT:(kc+1)*TT],
                             start=(kc == 0), stop=(kc == NKC - 1)))
                rk.done(P, sPE)
                P.wait("vector", sPE, sPE.n)
                P.op("vector", sDV, lambda v, j=j: v.tensor_copy(
                    out=bass.AP(k_fm, j * 128,
                                [[NCH * D, 128], [D, NCH], [1, 128]]),
                    in_=bass.AP(p_a, (j % 2)*512,
                                [[2048, 128], [128, NCH], [1, 128]])))
                evk[j] = sDV.n
                gci0 = t * NCH
                if j == 0:
                    chains[0] = chain(t, 0, st_c["mm_done"].get(gci0 - 2, 0))
                    chains[1] = chain(t, 1, st_c["mm_done"].get(gci0 - 1, 0))
                if j == 7:
                    scan_mms(t, 0, chains[0], n_vev)
                    chains[2] = chain(t, 2, st_c["mm_done"][gci0])
                if j == 11:
                    scan_mms(t, 1, chains[1], n_vev)
                    chains[3] = chain(t, 3, st_c["mm_done"][gci0 + 1])
                if j == 15:
                    scan_mms(t, 2, chains[2], n_vev)
            P.wait("sync", sDV, sDV.n)
            n_spill_k = P.dma("sync", sSPk,
                              kfm_sp.ap()[:, t*NCH*D:(t+1)*NCH*D], k_fm[:])
            prod_end[t] = sPE.n
            scan_mms(t, 3, chains[3], n_vev)
        P.wait("sync", sDV, sDV.n)
        n_stl = P.dma("sync", sSPs, st_loc.ap(), states[:])
        P.wait("gpsimd", sSPs, n_stl)
        Prog.TRACE.append(("gpsimd", "inc", "sAGst", 1))
        P.op("gpsimd", None, lambda g: g.collective_compute(
            "AllGather", ALU.bypass,
            replica_groups=[[0, 1], [2, 3], [4, 5], [6, 7]],
            ins=[st_loc.ap()], outs=[st_pair.ap()],
        ).then_inc(sAG["st"].h, 1))
        sAG["st"].n = 1
        P.run(nc, "phP1")

    if "p1" in phases:
        _phase_p1()

    def _phase_p2a():
      with ExitStack() as st2:
        xn_fm = sbt("p2_xnfm", [128, NKC * TT], BF16, st2)
        q_fm = sbt("p2_qfm", [128, NCH * D], BF16, st2)
        k_ch = [sbt("p2_kch%d" % i, [128, D], BF16, st2) for i in range(2)]
        v_ch = [sbt("p2_vch%d" % i, [128, D], BF16, st2) for i in range(2)]
        qpc = [sbt("p2_qpc%d" % i, [128, D], BF16, st2) for i in range(2)]
        qps = [sbt("p2_qps%d" % i, [128, D], BF16, st2) for i in range(2)]
        kpc = [sbt("p2_kpc%d" % i, [128, D], BF16, st2) for i in range(2)]
        kps = [sbt("p2_kps%d" % i, [128, D], BF16, st2) for i in range(2)]
        trc = [sbt("p2_trc%d" % i, [128, D], BF16, st2) for i in range(2)]
        trs = [sbt("p2_trs%d" % i, [128, D], BF16, st2) for i in range(2)]
        sttr = [sbt("p2_sttr%d" % i, [128, H * 256], BF16, st2)
                for i in range(2)]
        sTb = [sbt("p2_sT%d" % i, [128, 512], BF16, st2) for i in range(2)]
        ph_b = sbt("p2_ph", [128, D], F32, st2)
        rr_b = sbt("p2_rr", [128, D], F32, st2)
        qd_fm = sbt("p2_qdfm", [128, D], BF16, st2)
        ddt = sbt("p2_ddt", [128, D], BF16, st2)
        stin = sbt("p2_stin", [128, H * 256], BF16, st2)
        o_t = sbt("p2_ot", [128, H * TT], BF16, st2)
        pout = sbt("p2_pout", [128, 512], BF16, st2)
        tmp_o = sbt("p2_tmpo", [128, 512], BF16, st2)
        cM = sbt("p2_cM", [128, 1], F32, st2)
        cMn = sbt("p2_cMn", [128, 1], F32, st2)
        cI2P = sbt("p2_cI2P", [128, 1], F32, st2)
        cM4 = sbt("p2_cM4", [128, 1], F32, st2)
        p_q = pst("p2_pq", [128, 1024], F32, st2)
        p_s = pst("p2_ps", [128, 1024], F32, st2)
        p_o = pst("p2_po", [128, 1024], F32, st2)
        p_p = pst("p2_pp", [128, 1024], F32, st2)
        rq = Ring("p2_rq", "sq", 2048, st2, ring_sems[0])
        rp = Ring("p2_rp", "sp", 4096, st2, ring_sems[1])

        P = Prog()
        P.dma("gpsimd", sGQ, qd_fm[:], qd_fm_in.ap())
        P.dma("gpsimd", sGQ, ddt[:], ddt_in.ap())
        n_c2 = sGQ.n
        P.op("vector", sDV, lambda v: v.memset(cM[:], RND_M))
        P.op("vector", sDV, lambda v: v.memset(cMn[:], -RND_M))
        P.op("vector", sDV, lambda v: v.memset(cI2P[:], 1.0 / TWO_PI))
        P.op("vector", sDV, lambda v: v.memset(cM4[:], RND_M + 0.25))
        nc.const_aps.register(F32, RND_M, cM[:, 0:1])
        nc.const_aps.register(F32, -RND_M, cMn[:, 0:1])
        nc.const_aps.register(F32, 1.0 / TWO_PI, cI2P[:, 0:1])
        nc.const_aps.register(F32, RND_M + 0.25, cM4[:, 0:1])
        P.wait("gpsimd", sAG["st"], 1)
        nstin = P.dma("gpsimd", sSTP, stin[:], st_pair.ap()[0:128, :])
        P.wait("vector", sSTP, nstin)
        P.op("vector", sDV, lambda v: v.tensor_scalar(
            out=stin[:], in0=stin[:], scalar1=mask_col[:, 0:1], scalar2=None,
            op0=ALU.mult))
        P.wait("sync", sAG["sq"], 1)
        P.wait("sync", sAG["sp"], 1)
        rq.start(P, [j * 2048 for _ in range(NT) for j in range(H)])
        rp.start(P, [r * 4096 for _ in range(NT) for r in range(8)])

        def chain(ci, pe_done_before):
            t, c = divmod(ci, NCH)
            par = ci % 2
            gt0 = t * TT + c * CH
            P.wait("sync", sPE, pe_done_before)
            P.wait("sync", sDV, sDV.n)
            n_k = P.dma("sync", sLK[par], k_ch[par][:],
                        kfm_sp.ap()[:, ci*D:(ci+1)*D])
            n_v = P.dma("sync", sLV[par], v_ch[par][:],
                        vtm_sp.ap()[:, ci*D:(ci+1)*D])
            n_lo = P.dma("sync", sLO[par], sttr[par][:],
                         st_sp.ap()[:, ci*H*256:(ci+1)*H*256])
            P.wait("vector", sAC, sAC.n)
            P.wait("vector", sPO, sPO.n)
            P.op("vector", sDV, lambda v, gt0=gt0: v.tensor_scalar(
                out=rr_b[:], in0=iota_fm[:], scalar1=pos_col[:, 0:1],
                scalar2=float(gt0), op0=ALU.add, op1=ALU.add))
            P.wait("vector", sDV, sDV.n)
            th_rep = theta_t[:].unsqueeze(2).broadcast_to([128, H, 128])
            P.op("vector", sDV, lambda v, th_rep=th_rep: v.tensor_tensor(
                out=ph_b[:].rearrange("p (h t) -> p h t", h=H),
                in0=rr_b[:].rearrange("p (h t) -> p h t", h=H),
                in1=th_rep, op=ALU.mult))
            n_ph = sDV.n
            P.wait("scalar", sDV, n_ph)
            P.op("scalar", sAC, lambda s: s.activation(
                out=rr_b[:], in_=ph_b[:], func=AF.Copy,
                scale=1.0 / TWO_PI, bias=RND_M))
            P.op("scalar", sAC, lambda s: s.activation(
                out=rr_b[:], in_=rr_b[:], func=AF.Copy, bias=-RND_M))
            n_rr = sAC.n
            P.wait("vector", sAC, n_rr)
            P.op("vector", sDV, lambda v: v.scalar_tensor_tensor(
                out=rr_b[:], in0=rr_b[:], scalar=-TWO_PI, in1=ph_b[:],
                op0=ALU.mult, op1=ALU.add))
            P.wait("scalar", sDV, sDV.n)
            P.wait("scalar", sPE, pe_done_before)
            P.op("scalar", sAC, lambda s, par=par: s.activation(
                out=trs[par][:], in_=rr_b[:], func=AF.Sin))
            n_sin = sAC.n
            P.op("scalar", sAC, lambda s: s.activation(
                out=rr_b[:], in_=ph_b[:], func=AF.Copy,
                scale=1.0 / TWO_PI, bias=RND_M + 0.25))
            P.op("scalar", sAC, lambda s: s.activation(
                out=rr_b[:], in_=rr_b[:], func=AF.Copy, bias=-RND_M))
            n_rr2 = sAC.n
            P.wait("vector", sAC, n_rr2)
            P.op("vector", sDV, lambda v: v.scalar_tensor_tensor(
                out=rr_b[:], in0=rr_b[:], scalar=-TWO_PI, in1=ph_b[:],
                op0=ALU.mult, op1=ALU.add))
            P.wait("scalar", sDV, sDV.n)
            P.op("scalar", sAC, lambda s, par=par: s.activation(
                out=trc[par][:], in_=rr_b[:], func=AF.Sin,
                bias=half_pi[:, 0:1]))
            n_cos = sAC.n
            P.wait("gpsimd", sAC, n_cos)
            P.wait("gpsimd", sLK[par], n_k)
            P.wait("gpsimd", sPE, pe_done_before)
            P.op("gpsimd", sPO, lambda g, par=par: g.tensor_tensor(
                out=kpc[par][:], in0=k_ch[par][:], in1=trc[par][:],
                op=ALU.mult))
            P.op("gpsimd", sPO, lambda g, par=par: g.tensor_tensor(
                out=kps[par][:], in0=k_ch[par][:], in1=trs[par][:],
                op=ALU.mult))
            P.wait("vector", sLO[par], n_lo)
            P.wait("vector", sGQ, n_c2)
            P.wait("vector", sPE, pe_done_before)
            for h in range(H):
                P.op("vector", sDV if h == H - 1 else None,
                     lambda v, h=h, m=ci, par=par: v.scalar_tensor_tensor(
                         out=sttr[par][:, h*256:(h+1)*256],
                         in0=stin[:, h*256:(h+1)*256],
                         scalar=float(bdec[h] ** m),
                         in1=sttr[par][:, h*256:(h+1)*256],
                         op0=ALU.mult, op1=ALU.add))
            return [sPO.n, sDV.n, n_v]

        def chain_b(ci, deps, n_qev):
            t, c = divmod(ci, NCH)
            par = ci % 2
            P.wait("gpsimd", sDV, n_qev)
            qsl = bass.AP(q_fm, c * D, [[NCH * D, 128], [1, D]])
            P.op("gpsimd", sPO, lambda g, qsl=qsl, par=par: g.tensor_tensor(
                out=qpc[par][:], in0=qsl, in1=trc[par][:], op=ALU.mult))
            P.op("gpsimd", sPO, lambda g, qsl=qsl, par=par: g.tensor_tensor(
                out=qps[par][:], in0=qsl, in1=trs[par][:], op=ALU.mult))
            deps[0] = sPO.n
            if "qp_sp" in debug:
                P.wait("scalar", sPO, sPO.n)
                P.dma("scalar", sSC, qp_sp.ap()[:, ci*D:(ci+1)*D], qpc[par][:])
                P.dma("scalar", sSC, kp_sp.ap()[:, ci*D:(ci+1)*D], kpc[par][:])
                P.dma("scalar", sSC, trc_sp.ap()[:, ci*D:(ci+1)*D], trc[par][:])
                P.wait("gpsimd", sSC, sSC.n)
                P.wait("vector", sSC, sSC.n)
                P.wait("tensor", sSC, sSC.n)

        mst = {"mdv": 0}

        def mms(ci, deps):
            n_pool, n_sttr, n_v = deps
            t, c = divmod(ci, NCH)
            par = ci % 2
            evd = []
            for grp in range(4):
                gp = grp % 2
                P.wait("tensor", sPO, n_pool)
                P.wait("tensor", sDV, evd[grp - 2] if grp >= 2 else mst["mdv"])
                for hh in range(4):
                    h = grp * 4 + hh
                    P.op("tensor", None, lambda t_, h=h, hh=hh, gp=gp, par=par:
                         t_.matmul(
                             p_s[:, gp*512 + hh*128:gp*512 + (hh+1)*128],
                             kpc[par][:, h*128:(h+1)*128],
                             qpc[par][:, h*128:(h+1)*128],
                             start=(hh == 0), stop=False))
                    P.op("tensor", sPE if hh == 3 else None,
                         lambda t_, h=h, hh=hh, gp=gp, par=par: t_.matmul(
                             p_s[:, gp*512 + hh*128:gp*512 + (hh+1)*128],
                             kps[par][:, h*128:(h+1)*128],
                             qps[par][:, h*128:(h+1)*128],
                             start=False, stop=(hh == 3)))
                P.wait("vector", sPE, sPE.n)
                P.op("vector", sDV, lambda v, grp=grp, gp=gp, par=par:
                     v.tensor_tensor(
                         out=sTb[par][:],
                         in0=bass.AP(p_s, gp*512, [[1024, 128], [1, 512]]),
                         in1=ddt[:, grp*512:(grp+1)*512], op=ALU.mult))
                n_mask = sDV.n
                P.wait("tensor", sDV, n_mask)
                P.wait("tensor", sLV[par], n_v)
                for hh in range(4):
                    h = grp * 4 + hh
                    P.op("tensor", None,
                         lambda t_, h=h, hh=hh, gp=gp, par=par: t_.matmul(
                             p_o[:, gp*512 + hh*128:gp*512 + (hh+1)*128],
                             v_ch[par][:, h*128:(h+1)*128],
                             sTb[par][:, hh*128:(hh+1)*128],
                             start=(hh == 0), stop=False))
                    P.op("tensor", None, lambda t_, h=h, hh=hh, gp=gp, par=par:
                         t_.matmul(
                             p_q[:, gp*512 + hh*128:gp*512 + (hh+1)*128],
                             sttr[par][:, h*256:h*256 + 128],
                             qpc[par][:, h*128:(h+1)*128],
                             start=(hh == 0), stop=False))
                    P.op("tensor", sPE if hh == 3 else None,
                         lambda t_, h=h, hh=hh, gp=gp, par=par: t_.matmul(
                             p_q[:, gp*512 + hh*128:gp*512 + (hh+1)*128],
                             sttr[par][:, h*256 + 128:(h+1)*256],
                             qps[par][:, h*128:(h+1)*128],
                             start=False, stop=(hh == 3)))
                P.wait("vector", sPE, sPE.n)
                P.op("vector", sDV, lambda v, grp=grp, gp=gp: v.tensor_tensor(
                    out=tmp_o[:],
                    in0=bass.AP(p_q, gp*512, [[1024, 128], [1, 512]]),
                    in1=qd_fm[:, grp*512:(grp+1)*512], op=ALU.mult))
                P.wait("vector", sDV, sDV.n)
                P.op("vector", sDV, lambda v, grp=grp, gp=gp, c=c:
                     v.tensor_tensor(
                         out=bass.AP(o_t, grp*4*TT + c*128,
                                     [[H * TT, 128], [TT, 4], [1, 128]]),
                         in0=bass.AP(p_o, gp*512,
                                     [[1024, 128], [128, 4], [1, 128]]),
                         in1=tmp_o[:].rearrange("p (a b) -> p a b", a=4),
                         op=ALU.add))
                evd.append(sDV.n)
            mst["mdv"] = sDV.n

        pend = None
        pend_b = None
        pe_done = [0] * 40
        chained = {}
        qp_end = {-1: 0}
        for t in range(NT):
            P.wait("sync", sPE, qp_end[t - 1])
            P.wait("sync", sSPx, sSPx.n)
            n_xn = P.dma("sync", sXN, xn_fm[:],
                         xnfm_sp.ap()[:, t*NKC*TT:(t+1)*NKC*TT])
            if t == 0:
                chained[0] = chain(0, 0)
                pe_done[0] = sPE.n
            P.wait("tensor", sXN, n_xn)
            P.wait("vector", sPO, sPO.n)
            for j in range(H):
                buf = rq.take(P)
                P.wait("tensor", sDV, sDV.n)
                for kc in range(NKC):
                    P.op("tensor", sPE if kc == NKC - 1 else None,
                         lambda t_, j=j, kc=kc, buf=buf: t_.matmul(
                             p_q[:, (j % 2)*512:(j % 2)*512 + 512],
                             buf[:, kc*128:(kc+1)*128],
                             xn_fm[:, kc*TT:(kc+1)*TT],
                             start=(kc == 0), stop=(kc == NKC - 1)))
                rq.done(P, sPE)
                P.wait("vector", sPE, sPE.n)
                P.op("vector", sDV, lambda v, j=j: v.tensor_copy(
                    out=bass.AP(q_fm, j * 128,
                                [[NCH * D, 128], [D, NCH], [1, 128]]),
                    in_=bass.AP(p_q, (j % 2)*512,
                                [[1024, 128], [128, NCH], [1, 128]])))
            mst["mdv"] = max(mst["mdv"], sDV.n)
            n_qev = sDV.n
            qp_end[t] = sPE.n
            chain_b(t * NCH, chained[t * NCH], n_qev)
            for c in range(NCH):
                ci = t * NCH + c
                nxt = ci + 1
                if nxt < NT * NCH:
                    chained[nxt] = chain(nxt, pe_done[nxt - 2] if nxt >= 2 else 0)
                    pe_done[nxt] = sPE.n
                    if nxt % NCH != 0:
                        chain_b(nxt, chained[nxt], n_qev)
                mms(ci, chained[ci])
                pe_done[ci] = sPE.n
            if "o_sp" in debug:
                P.wait("scalar", sDV, sDV.n)
                P.dma("scalar", sSC, o_sp.ap()[:, t*H*TT:(t+1)*H*TT], o_t[:])
                P.wait("tensor", sSC, sSC.n)
            pvd = {}
            for fb in range(NFB):
                bufs = [rp.take(P), rp.take(P)]
                for c in range(NCH):
                    k2 = fb * NCH + c
                    if k2 >= 2:
                        P.wait("tensor", sAC, pvd[k2 - 2])
                    else:
                        P.wait("tensor", sDV, mst["mdv"])
                    for h in range(H):
                        wbuf = bufs[h // 8]
                        P.op("tensor", sPE if h == H - 1 else None,
                             lambda t_, h=h, c=c, wbuf=wbuf: t_.matmul(
                                 p_p[:, (c % 2)*512:(c % 2)*512 + 512],
                                 o_t[:, h*TT + c*128:h*TT + (c+1)*128],
                                 wbuf[:, (h % 8)*512:((h % 8)+1)*512],
                                 start=(h == 0), stop=(h == H - 1)))
                    P.wait("scalar", sPE, sPE.n)
                    P.wait("scalar", sPW, sPW.n)
                    P.op("scalar", sAC, lambda s, c=c: s.copy(
                        out=pout[:],
                        in_=bass.AP(p_p, (c % 2)*512, [[1024, 128], [1, 512]])))
                    pvd[fb * NCH + c] = sAC.n
                    P.wait("scalar", sAC, sAC.n)
                    P.dma("scalar", sPW,
                          partial.ap()[t*TT + c*128:t*TT + (c+1)*128,
                                       fb*512:(fb+1)*512], pout[:])
                rp.done(P, sPE)
                rp.done(P, sPE)
        P.run(nc, "phP2a")

    if "p2a" in phases:
        _phase_p2a()

    def _phase_p2b():
      with ExitStack() as st3:
        xn_fm = sbt("p3_xnfm", [128, NKC * TT], BF16, st3)
        h_t = sbt("p3_h", [128, NJG * TT], BF16, st3)
        silu_g = sbt("p3_silu", [128, TT], BF16, st3)
        sg_sb = sbt("p3_sg", [128, NJG], F32, st3)
        su_sb = sbt("p3_su", [128, NJG], F32, st3)
        sd_mat = sbt("p3_sd", [128, D], F32, st3)
        par_ch = [sbt("p3_par%d" % i, [128, D], BF16, st3) for i in range(2)]
        x_ch = [sbt("p3_x%d" % i, [128, D], BF16, st3) for i in range(2)]
        o_st = [sbt("p3_o%d" % i, [128, D], F32, st3) for i in range(2)]
        xp_ch = [sbt("p3_xp%d" % i, [128, D], BF16, st3) for i in range(2)]
        tmp2 = [sbt("p3_tmp%d" % i, [128, D], BF16, st3) for i in range(2)]
        p8 = pst("p3_p8", [128, 4096], F32, st3)
        rg = Ring("p3_rg", "sg", 4096, st3, ring_sems[0], nbuf=3)
        ru = Ring("p3_ru", "su", 4096, st3, ring_sems[1], nbuf=3)
        rd = Ring("p3_rd", "sd", 4096, st3, ring_sems[2], nbuf=3)

        P = Prog()
        P.dma("sync", sSY, sg_sb[:], sg_t.ap())
        P.dma("sync", sSY, su_sb[:], su_t.ap())
        P.dma("sync", sSY, sd_mat[:], sd_row.ap().broadcast_to([128, D]))
        nconst3 = sSY.n
        P.wait("sync", sAG["sg"], 1)
        P.wait("sync", sAG["su"], 1)
        P.wait("sync", sAG["sd"], 1)
        seq_g = [r * 4096 for _ in range(NT) for r in range(22)]
        rg.start(P, seq_g)
        ru.start(P, seq_g)
        rd.start(P, [r * 4096 for _ in range(NT) for _ in range(2)
                     for r in range(22)])

        NRD = 22
        n_out = {0: 0, 1: 0}
        n_par = {0: 0, 1: 0}
        n_x = {0: 0, 1: 0}
        gu_end = {-1: 0}
        xp_rd = {0: 0, 1: 0}
        xp_wr = {0: 0, 1: 0}
        for t in range(NT):
            P.wait("sync", sPE, gu_end[t - 1])
            n_xn = P.dma("sync", sXN, xn_fm[:],
                         xnfm_sp.ap()[:, t*NKC*TT:(t+1)*NKC*TT])
            P.wait("tensor", sXN, n_xn)
            P.wait("scalar", sSY, nconst3)
            P.wait("vector", sPE, sPE.n)
            h_done = {}
            dn_ev = sDV.n
            for j in range(NJG):
                if j % 2 == 0:
                    gbuf = rg.take(P)
                    ubuf = ru.take(P)
                gp = j % 2
                P.wait("tensor", sDV, h_done.get(j - 2, dn_ev))
                for kc in range(NKC):
                    P.op("tensor", None, lambda t_, j=j, kc=kc, gbuf=gbuf, gp=gp:
                         t_.matmul(p8[:, gp*512:gp*512 + 512],
                                   gbuf[:, (gp*NKC + kc)*128:(gp*NKC + kc + 1)*128],
                                   xn_fm[:, kc*TT:(kc+1)*TT],
                                   start=(kc == 0), stop=(kc == NKC - 1)))
                for kc in range(NKC):
                    P.op("tensor", sPE if kc == NKC - 1 else None,
                         lambda t_, j=j, kc=kc, ubuf=ubuf, gp=gp:
                         t_.matmul(p8[:, 1024 + gp*512:1024 + gp*512 + 512],
                                   ubuf[:, (gp*NKC + kc)*128:(gp*NKC + kc + 1)*128],
                                   xn_fm[:, kc*TT:(kc+1)*TT],
                                   start=(kc == 0), stop=(kc == NKC - 1)))
                P.wait("scalar", sPE, sPE.n)
                P.wait("scalar", sDV, h_done.get(j - 1, 0))
                P.op("scalar", sAC, lambda s, j=j, gp=gp: s.activation(
                    out=silu_g[:],
                    in_=bass.AP(p8, gp*512, [[4096, 128], [1, 512]]),
                    func=AF.Silu, scale=sg_sb[:, j:j+1]))
                P.wait("vector", sAC, sAC.n)
                P.op("vector", sDV, lambda v, j=j, gp=gp: v.scalar_tensor_tensor(
                    out=h_t[:, j*TT:(j+1)*TT],
                    in0=bass.AP(p8, 1024 + gp*512, [[4096, 128], [1, 512]]),
                    scalar=su_sb[:, j:j+1], in1=silu_g[:],
                    op0=ALU.mult, op1=ALU.mult))
                h_done[j] = sDV.n
                if j % 2 == 1 or j == NJG - 1:
                    rg.done(P, sPE)
                    ru.done(P, sPE)
            n_h = sDV.n
            gu_end[t] = sPE.n
            for half in range(2):
                cc0 = half * 2
                for cc in range(2):
                    c = cc0 + cc
                    par = c % 2
                    P.wait("scalar", sDV, sDV.n)
                    n_par[par] = P.dma(
                        "scalar", sRP[par], par_ch[par][:],
                        partial.ap()[t*TT + c*128:t*TT + (c+1)*128, :])
                    n_x[par] = P.dma(
                        "gpsimd", sRX[par], x_ch[par][:],
                        x_in.ap()[t*TT + c*128:t*TT + (c+1)*128, :])
                P.wait("tensor", sDV, sDV.n)
                P.wait("tensor", sDV, n_h)
                for r in range(NRD):
                    dbuf = rd.take(P)
                    for ki in range(2):
                        kc = r * 2 + ki
                        if kc >= NKD:
                            continue
                        for cc in range(2):
                            c = cc0 + cc
                            for fb in range(NFB):
                                last = ((ki == 1 or kc == NKD - 1)
                                        and fb == NFB - 1)
                                P.op("tensor",
                                     sPE if (last and cc == 1) else None,
                                     lambda t_, kc=kc, fb=fb, c=c, cc=cc,
                                     ki=ki, dbuf=dbuf:
                                     t_.matmul(
                                         p8[:, (cc*4 + fb)*512:(cc*4 + fb + 1)*512],
                                         h_t[:, kc*TT + c*128:kc*TT + (c+1)*128],
                                         dbuf[:, (ki*4 + fb)*512:(ki*4 + fb + 1)*512],
                                         start=(kc == 0), stop=(kc == NKD - 1)))
                    rd.done(P, sPE)
                n_dn = sPE.n
                for cc in range(2):
                    c = cc0 + cc
                    par = c % 2
                    P.wait("gpsimd", sRX[par], n_x[par])
                    P.wait("gpsimd", sRP[par], n_par[par])
                    P.wait("gpsimd", sPO, xp_rd[par])
                    P.op("gpsimd", sPO, lambda g, par=par: g.tensor_tensor(
                        out=xp_ch[par][:], in0=x_ch[par][:],
                        in1=par_ch[par][:], op=ALU.add))
                    xp_wr[par] = sPO.n
                P.wait("vector", sPE, n_dn)
                for cc in range(2):
                    c = cc0 + cc
                    par = c % 2
                    P.wait("vector", sOU[par], n_out[par])
                    for fb in range(NFB):
                        P.op("vector", sDV, lambda v, fb=fb, cc=cc, par=par:
                             v.tensor_tensor(
                                 out=tmp2[par][:, fb*512:(fb+1)*512],
                                 in0=p8[:, (cc*4 + fb)*512:(cc*4 + fb + 1)*512],
                                 in1=sd_mat[:, fb*512:(fb+1)*512], op=ALU.mult))
                    nmul = sDV.n
                    P.wait("gpsimd", sDV, nmul)
                    P.wait("gpsimd", sPO, xp_wr[par])
                    P.op("gpsimd", sPO, lambda g, par=par: g.tensor_tensor(
                        out=o_st[par][:], in0=tmp2[par][:],
                        in1=xp_ch[par][:], op=ALU.add))
                    xp_rd[par] = sPO.n
                    P.wait("scalar", sPO, sPO.n)
                    n_out[par] = P.dma(
                        "scalar", sOU[par],
                        out_t.ap()[t*TT + c*128:t*TT + (c+1)*128, :],
                        o_st[par][:])
        P.wait("scalar", sOU[0], sOU[0].n)
        P.wait("scalar", sOU[1], sOU[1].n)
        P.run(nc, "phP2b")

    if "p2b" in phases:
        _phase_p2b()

    Pd = Prog()
    alldma = [sSY, sSC, sGQ, sPE, sDV, sAC, sPO, sXB, sXC, sXN, sST,
              sSPx, sSPk, sSPv, sSPs, sSTP, sPW]
    alldma += [s for pr in ring_sems for s in pr]
    alldma += sLK + sLV + sLO + sRP + sRX + sOU
    for s in alldma:
        Pd.wait("sync", s)
    for k in sAG:
        Pd.wait("sync", sAG[k], sAG[k].n)
    for dout, din in dbg_pairs:
        Pd.dma("sync", sSY, dout.ap(), din.ap())
    Pd.wait("sync", sSY)
    Pd.run(nc, "phDrain")

    es.close()
    return nc


def _host_consts():
    s = _slopes().astype(np.float64)
    i = np.arange(CH, dtype=np.float64)
    qd = np.exp(-s[:, None] * (i[None, :] + 1.0))
    kd = np.exp(-s[None, :] * (CH - 1.0 - i[:, None]))
    diff = i[None, :] - i[:, None]
    dd = np.where(diff >= 0,
                  np.exp(-s[:, None, None] * diff[None, :, :]), 0.0)
    return dict(
        ident=np.eye(128),
        ones_col=np.ones((128, 1)),
        ones_row=np.ones((1, 128), dtype=np.float32),
        iota_fm=np.tile(np.arange(CH, dtype=np.float32)[None, :], (128, H)),
        iota_col=np.arange(CH, dtype=np.float32)[:, None].copy(),
        qd_fm=np.tile(qd.reshape(1, H * CH), (128, 1)).astype(np.float32),
        kd_tm=np.repeat(kd, E, axis=1).astype(np.float32),
        ddt=dd.transpose(1, 0, 2).reshape(CH, H * CH).astype(np.float32),
    )


def make_inputs(core, consts, L0_attn_x, L0_attn_w_qkv, L0_attn_w_proj,
                L0_attn_theta, L0_ffn_wq_gate, L0_ffn_s_gate, L0_ffn_wq_up,
                L0_ffn_s_up, L0_ffn_wq_down, L0_ffn_s_down):
    import ml_dtypes
    bf = ml_dtypes.bfloat16
    b, half = core // 2, core % 2
    c = core
    xq = np.asarray(L0_attn_x)
    wqkv = np.asarray(L0_attn_w_qkv)
    wproj = np.asarray(L0_attn_w_proj)
    theta = np.asarray(L0_attn_theta).reshape(H, E)
    wg = np.asarray(L0_ffn_wq_gate)
    wu = np.asarray(L0_ffn_wq_up)
    wd = np.asarray(L0_ffn_wq_down)
    wq = wqkv[0:D]
    wk = wqkv[D:2*D]
    wv = wqkv[2*D:3*D]
    fv, kh = c // 2, c % 2
    gpad = np.zeros((NJGP * 128, D), np.int32)
    gpad[:HID] = wg
    upad = np.zeros((NJGP * 128, D), np.int32)
    upad[:HID] = wu
    dpad = np.zeros((D, NKDP * 128), np.int32)
    dpad[:, :HID] = wd
    return {
        "x": np.ascontiguousarray(xq[b, half*TPC:(half+1)*TPC, :],
                                  dtype=np.float32),
        "wq_sh": np.ascontiguousarray(wq[c*256:(c+1)*256], dtype=np.float32),
        "wk_sh": np.ascontiguousarray(wk[c*256:(c+1)*256], dtype=np.float32),
        "wv_sh": np.ascontiguousarray(
            wv[fv*512:(fv+1)*512, kh*1024:(kh+1)*1024], dtype=np.float32),
        "wkt_sh": np.ascontiguousarray(
            wk[fv*512:(fv+1)*512, kh*1024:(kh+1)*1024], dtype=np.float32),
        "wp_sh": np.ascontiguousarray(
            wproj[fv*512:(fv+1)*512, kh*1024:(kh+1)*1024], dtype=np.float32),
        "wg_sh": np.ascontiguousarray(gpad[c*768:(c+1)*768]),
        "wu_sh": np.ascontiguousarray(upad[c*768:(c+1)*768]),
        "wd_sh": np.ascontiguousarray(dpad[:, c*768:(c+1)*768]),
        "sg_t": np.ascontiguousarray(
            np.asarray(L0_ffn_s_gate).reshape(NJG, 128).T, dtype=np.float32),
        "su_t": np.ascontiguousarray(
            np.asarray(L0_ffn_s_up).reshape(NJG, 128).T, dtype=np.float32),
        "sd_row": np.ascontiguousarray(
            np.asarray(L0_ffn_s_down).reshape(1, D), dtype=np.float32),
        "theta_tm": np.ascontiguousarray(theta.reshape(1, D),
                                         dtype=np.float32),
        "theta_t": np.ascontiguousarray(theta.T, dtype=np.float32),
        "ident": consts["ident"].astype(bf),
        "ones_col": consts["ones_col"].astype(bf),
        "ones_row": consts["ones_row"],
        "iota_fm": consts["iota_fm"],
        "iota_col": consts["iota_col"],
        "pos_col": np.full((128, 1), float(half * TPC), np.float32),
        "mask_col": np.full((128, 1), float(half), np.float32),
        "qd_fm": consts["qd_fm"],
        "kd_tm": consts["kd_tm"],
        "ddt": consts["ddt"],
    }


DEV_PHASES = ("w", "p1", "p2a", "p2b")
DEV_DEBUG = ()


def kernel(**inputs):
    consts = _host_consts()
    in_maps = [make_inputs(c, consts, **inputs) for c in range(8)]
    nc = build_program(debug=DEV_DEBUG, phases=DEV_PHASES)
    res = run_bass_kernel_spmd(nc, in_maps, core_ids=list(range(8)))
    out = np.empty((B, T, D), np.float32)
    for c in range(8):
        b, half = c // 2, c % 2
        out[b, half*TPC:(half+1)*TPC, :] = res.results[c]["out"]
    return out


# version 25
# speedup vs baseline: 1.1328x; 1.0248x over previous
import math
from contextlib import ExitStack

import numpy as np

import concourse.bass as bass
import concourse.mybir as mybir
from concourse.bass_utils import run_bass_kernel_spmd

F32 = mybir.dt.float32
BF16 = mybir.dt.bfloat16
I32 = mybir.dt.int32
AF = mybir.ActivationFunctionType
ALU = mybir.AluOpType

B, T, D = 4, 8192, 2048
H, E = 16, 128
HID = 5632
CH = 128
TPC = T // 2
NT = 8
TT = TPC // NT
NCH = TT // CH
NKC = D // 128
NJG = HID // 128
NJGP = 48
NFB = D // 512
NKD = NJG
NKDP = 48
PI = float(np.pi)
TWO_PI = float(2.0 * np.pi)
RND_M = 12582912.0

LQ = 16 * NKC * 128
LV = 4 * NKC * 512
LP = NFB * H * 512
LG = NJGP * NKC * 128
LDN = NKDP * NFB * 512

ENGS = ("sync", "scalar", "vector", "tensor", "gpsimd")


def _slopes():
    start = 2.0 ** (-(2.0 ** (-(math.log2(H) - 3))))
    return np.array([start ** (i + 1) for i in range(H)], dtype=np.float32)


class Sem:
    def __init__(self, es, nc, name):
        self.h = es.enter_context(nc.semaphore(name))
        self.n = 0


class Prog:

    TRACE = []

    def __init__(self):
        self.q = {e: [] for e in ENGS}

    def wait(self, eng, sem, n=None):
        n = sem.n if n is None else n
        if n > 0:
            Prog.TRACE.append((eng, "wait", sem.h.name, n))
            self.q[eng].append(lambda e, h=sem.h, n=n: e.wait_ge(h, n))

    def dma(self, eng, sem, out, in_, **kw):
        def f(e, out=out, in_=in_, kw=kw, h=sem.h):
            e.dma_start(out=out, in_=in_, **kw).then_inc(h, 16)
        Prog.TRACE.append((eng, "inc", sem.h.name, 16))
        self.q[eng].append(f)
        sem.n += 16
        return sem.n

    def op(self, eng, sem, fn):
        if sem is None:
            Prog.TRACE.append((eng, "op", None, 0))
            self.q[eng].append(fn)
            return None
        Prog.TRACE.append((eng, "inc", sem.h.name, 1))
        self.q[eng].append(lambda e, fn=fn, h=sem.h: fn(e).then_inc(h, 1))
        sem.n += 1
        return sem.n

    def run(self, nc, name):
        with nc.Block(name) as blk:
            for eng in ENGS:
                fns = self.q[eng]
                if fns:
                    getattr(blk, eng)(lambda e, fns=fns: [f(e) for f in fns])


def build_program(debug=(), phases=("w", "p1", "p2a", "p2b")):
    nc = bass.Bass("TRN2", target_bir_lowering=False)
    es = ExitStack()

    dbg_pairs = []

    def dram(name, shape, dt, kind="Internal"):
        th = nc.dram_tensor(name, shape, dt, kind=kind)
        if kind == "Internal" and name in debug:
            dbg_pairs.append(
                (nc.dram_tensor("dbg_" + name, shape, dt,
                                kind="ExternalOutput"), th))
        return th

    def sbt(name, shape, dt, stack=None):
        return (stack or es).enter_context(nc.sbuf_tensor(name, shape, dt))

    def pst(name, shape, dt, stack=None):
        return (stack or es).enter_context(nc.psum_tensor(name, shape, dt))

    def inp(name, shape, dt):
        return nc.dram_tensor(name, shape, dt, kind="ExternalInput")

    x_in = inp("x", [TPC, D], F32)
    wq_sh = inp("wq_sh", [256, D], F32)
    wk_sh = inp("wk_sh", [256, D], F32)
    wv_sh = inp("wv_sh", [512, 1024], F32)
    wkt_sh = inp("wkt_sh", [512, 1024], F32)
    wp_sh = inp("wp_sh", [512, 1024], F32)
    wg_sh = inp("wg_sh", [768, D], I32)
    wu_sh = inp("wu_sh", [768, D], I32)
    wd_sh = inp("wd_sh", [D, 768], I32)
    sg_t = inp("sg_t", [128, NJG], F32)
    su_t = inp("su_t", [128, NJG], F32)
    sd_row = inp("sd_row", [1, D], F32)
    th_tm_in = inp("theta_tm", [1, D], F32)
    th_t_in = inp("theta_t", [128, H], F32)
    ident_in = inp("ident", [128, 128], BF16)
    ones_col_in = inp("ones_col", [128, 1], BF16)
    ones_row_in = inp("ones_row", [1, 128], F32)
    iota_fm_in = inp("iota_fm", [128, D], F32)
    iota_col_in = inp("iota_col", [128, 1], F32)
    pos_col_in = inp("pos_col", [128, 1], F32)
    mask_col_in = inp("mask_col", [128, 1], F32)
    qd_fm_in = inp("qd_fm", [128, D], F32)
    kd_tm_in = inp("kd_tm", [128, D], F32)
    ddt_in = inp("ddt", [128, D], F32)
    out_t = nc.dram_tensor("out", [TPC, D], F32, kind="ExternalOutput")

    pre_loc, pre_all = {}, {}
    for nm, L in (("sq", LQ), ("sk", LQ), ("sv", LV), ("skt", LV), ("sp", LP),
                  ("sg", LG), ("su", LG), ("sd", LDN)):
        pre_loc[nm] = dram("pre_%s_loc" % nm, [128, L // 8], BF16)
        pre_all[nm] = dram("pre_%s_all" % nm, [1024, L // 8], BF16)
    x_bf = dram("x_bf", [TPC, D], BF16)
    xnfm_sp = dram("xnfm_sp", [128, NT * NKC * TT], BF16)
    kfm_sp = dram("kfm_sp", [128, 32 * D], BF16)
    vtm_sp = dram("vtm_sp", [128, 32 * D], BF16)
    st_sp = dram("st_sp", [128, 32 * H * 256], BF16)
    ktc_sp = dram("ktc_sp", [128, 32 * D], BF16)
    o_sp = dram("o_sp", [128, NT * H * TT], BF16)
    sT_sp = dram("sT_sp", [128, 32 * D], BF16)
    qp_sp = dram("qp_sp", [128, 32 * D], BF16)
    kp_sp = dram("kp_sp", [128, 32 * D], BF16)
    trc_sp = dram("trc_sp", [128, 32 * D], BF16)
    trs_sp = dram("trs_sp", [128, 32 * D], BF16)
    st_loc = dram("st_loc", [128, H * 256], F32)
    st_pair = dram("st_pair", [256, H * 256], F32)
    partial = dram("partial", [TPC, D], BF16)

    ident = sbt("c_ident", [128, 128], BF16)
    ones_col = sbt("c_ones_col", [128, 1], BF16)
    ones_row = sbt("c_ones_row", [1, 128], F32)
    iota_fm = sbt("c_iota_fm", [128, D], F32)
    iota_col = sbt("c_iota_col", [128, 1], F32)
    pos_col = sbt("c_pos_col", [128, 1], F32)
    mask_col = sbt("c_mask_col", [128, 1], F32)
    theta_tm = sbt("c_theta_tm", [128, D], F32)
    theta_t = sbt("c_theta_t", [128, H], F32)
    half_pi = sbt("c_half_pi", [128, 1], F32)
    c_n1 = sbt("c_n1", [128, 1], F32)

    sSY = Sem(es, nc, "sSY")
    sSC = Sem(es, nc, "sSC")
    sGQ = Sem(es, nc, "sGQ")
    sPE = Sem(es, nc, "sPE")
    sDV = Sem(es, nc, "sDV")
    sAC = Sem(es, nc, "sAC")
    sPO = Sem(es, nc, "sPO")
    sAG = {k: Sem(es, nc, "sAG" + k) for k in
           ("sk", "sv", "skt", "sq", "sp", "sg", "su", "sd", "st")}
    ring_sems = [(Sem(es, nc, "sR%da" % i), Sem(es, nc, "sR%db" % i),
                  Sem(es, nc, "sR%dc" % i)) for i in range(3)]
    sXB = Sem(es, nc, "sXB")
    sXC = Sem(es, nc, "sXC")
    sXN = Sem(es, nc, "sXN")
    sST = Sem(es, nc, "sST")
    sSPx = Sem(es, nc, "sSPx")
    sSPk = Sem(es, nc, "sSPk")
    sSPv = Sem(es, nc, "sSPv")
    sSPs = Sem(es, nc, "sSPs")
    sSTP = Sem(es, nc, "sSTP")
    sPW = Sem(es, nc, "sPW")
    sLK = [Sem(es, nc, "sLK%d" % i) for i in range(2)]
    sLV = [Sem(es, nc, "sLV%d" % i) for i in range(2)]
    sLO = [Sem(es, nc, "sLO%d" % i) for i in range(2)]
    sRP = [Sem(es, nc, "sRP%d" % i) for i in range(2)]
    sRX = [Sem(es, nc, "sRX%d" % i) for i in range(2)]
    sOU = [Sem(es, nc, "sOU%d" % i) for i in range(2)]

    slopes = _slopes()
    bdec = [float(np.exp(np.float64(-s) * CH))
            for s in slopes.astype(np.float64)]

    def stream_runs(nm, c0, c1):
        Lc = pre_all[nm].shape[1]
        runs, c = [], c0
        while c < c1:
            e = min(c1, (c // Lc + 1) * Lc)
            runs.append((c, e))
            c = e
        return runs

    def stream_ap(nm, c0, c1):
        Lc = pre_all[nm].shape[1]
        s = c0 // Lc
        return pre_all[nm].ap()[s * 128:(s + 1) * 128, c0 - s * Lc:c1 - s * Lc]

    class Ring:
        def __init__(self, name, nm, ncols, stack, sems, nbuf=2):
            self.nm, self.ncols, self.nbuf = nm, ncols, nbuf
            self.sems = sems
            self.bufs = [sbt("%s%d" % (name, i), [128, ncols], BF16, stack)
                         for i in range(nbuf)]
            self.loaded = []
            self.freed = []
            self.issue_i = 0
            self.cons_i = 0
            self.take_i = 0

        def issue(self, P, c0):
            i = self.issue_i % self.nbuf
            if self.issue_i >= self.nbuf:
                sem, n = self.freed[self.issue_i - self.nbuf]
                P.wait("sync", sem, n)
            for (a, b) in stream_runs(self.nm, c0, c0 + self.ncols):
                P.dma("sync", self.sems[i], self.bufs[i][:, a - c0:b - c0],
                      stream_ap(self.nm, a, b))
            self.loaded.append((i, self.sems[i].n))
            self.issue_i += 1

        def use(self, P, eng="tensor"):
            i, n = self.loaded[self.take_i]
            self.take_i += 1
            P.wait(eng, self.sems[i], n)
            return self.bufs[i]

        def release(self, sem):
            self.freed.append((sem, sem.n))
            self.cons_i += 1

        def start(self, P, seq):
            self.seq = list(seq)
            self.si = 0
            for i in range(min(self.nbuf, len(self.seq))):
                self.issue(P, self.seq[i])

        def take(self, P, eng="tensor"):
            return self.use(P, eng)

        def done(self, P, sem):
            self.release(sem)
            nxt = self.si + self.nbuf
            if nxt < len(self.seq):
                self.issue(P, self.seq[nxt])
            self.si += 1

    ctx = {"n_xbf": 0}

    def _phase_w():
      with ExitStack() as stw:
        stag = sbt("w_stag", [128, 12288], BF16, stw)
        rowf = [sbt("w_rowf%d" % i, [128, D], F32, stw) for i in range(2)]
        rowi = [sbt("w_rowi%d" % i, [128, D], I32, stw) for i in range(2)]
        rowb = [sbt("w_rowb%d" % i, [128, D], BF16, stw) for i in range(2)]
        ptr = pst("w_ptr", [128, 8192], BF16, stw)
        P = Prog()
        for dst, src in ((ident, ident_in), (ones_col, ones_col_in),
                         (ones_row, ones_row_in), (iota_fm, iota_fm_in),
                         (iota_col, iota_col_in), (pos_col, pos_col_in),
                         (mask_col, mask_col_in), (theta_t, th_t_in)):
            P.dma("sync", sSY, dst[:], src.ap())
        P.dma("sync", sSY, theta_tm[:], th_tm_in.ap().broadcast_to([128, D]))
        P.op("vector", sDV, lambda v: v.memset(half_pi[:], PI / 2.0))
        P.op("vector", sDV, lambda v: v.memset(c_n1[:], -1.0))
        nc.const_aps.register(F32, -1.0, c_n1[:, 0:1])
        P.wait("tensor", sSY)

        for t in range(NT):
            P.dma("gpsimd", sXC, x_bf.ap()[t * TT:(t + 1) * TT, :],
                  x_in.ap()[t * TT:(t + 1) * TT, :])
        ctx["n_xbf"] = sXC.n

        jobs = [
            ("sk", [(wk_sh.ap()[j*128:(j+1)*128, :], False, NKC,
                     (lambda kc, j=j: j*NKC + kc)) for j in range(2)], 2*NKC),
            ("sv", [(wv_sh.ap()[j*128:(j+1)*128, :], False, 8,
                     (lambda kc, j=j: kc*4 + j)) for j in range(4)], 32),
            ("skt", [(wkt_sh.ap()[j*128:(j+1)*128, :], False, 8,
                      (lambda kc, j=j: kc*4 + j)) for j in range(4)], 32),
            ("sq", [(wq_sh.ap()[j*128:(j+1)*128, :], False, NKC,
                     (lambda kc, j=j: j*NKC + kc)) for j in range(2)], 2*NKC),
            ("sp", [(wp_sh.ap()[j*128:(j+1)*128, :], False, 8,
                     (lambda kc, j=j: kc*4 + j)) for j in range(4)], 32),
            ("sg", [(wg_sh.ap()[j*128:(j+1)*128, :], True, NKC,
                     (lambda kc, j=j: j*NKC + kc)) for j in range(6)], 6*NKC),
            ("su", [(wu_sh.ap()[j*128:(j+1)*128, :], True, NKC,
                     (lambda kc, j=j: j*NKC + kc)) for j in range(6)], 6*NKC),
            ("sd", [(wd_sh.ap()[j*128:(j+1)*128, :], True, 6,
                     (lambda kc, j=j: kc*16 + j)) for j in range(16)], 96),
        ]
        ag_ready = {}
        pi = 0
        cv_done = {}
        tr_done = {}
        for nm, parts, nblk in jobs:
            n_store_prev = sST.n
            for srcap, is_int, nkc, scol in parts:
                cols = srcap.shape[1]
                par = pi % 2
                rbuf = rowi[par] if is_int else rowf[par]
                rb = rowb[par]
                P.wait("scalar", sDV, cv_done.get(pi - 2, 0))
                nld = P.dma("scalar", sRX[par], rbuf[:, 0:cols], srcap)
                P.wait("vector", sRX[par], nld)
                P.wait("vector", sPE, tr_done.get(pi - 2, 0))
                P.op("vector", sDV, lambda v, rbuf=rbuf, rb=rb, cols=cols,
                     a=(-128.0 if is_int else 0.0):
                     v.tensor_scalar(out=rb[:, 0:cols], in0=rbuf[:, 0:cols],
                                     scalar1=a, scalar2=None, op0=ALU.add))
                cv_done[pi] = sDV.n
                P.wait("tensor", sDV, cv_done[pi])
                for k0 in range(0, nkc, 8):
                    nb = min(8, nkc - k0)
                    P.wait("tensor", sDV, sDV.n)
                    for i in range(nb):
                        P.op("tensor", sPE if i == nb - 1 else None,
                             lambda t_, kc=k0 + i, i=i, rb=rb: t_.transpose(
                                 ptr[:, i*1024:i*1024 + 128],
                                 rb[:, kc*128:(kc+1)*128], ident[:]))
                    P.wait("vector", sPE, sPE.n)
                    P.wait("vector", sST, n_store_prev)

                    def ev(v, k0=k0, nb=nb, scol=scol):
                        r = None
                        for i in range(nb):
                            r = v.tensor_copy(
                                out=stag[:, scol(k0+i)*128:(scol(k0+i)+1)*128],
                                in_=bass.AP(ptr, i*1024, [[8192, 128], [1, 128]]))
                        return r
                    P.op("vector", sDV, ev)
                tr_done[pi] = sPE.n
                pi += 1
            P.wait("sync", sDV, sDV.n)
            P.dma("sync", sST, pre_loc[nm].ap()[:, 0:nblk*128],
                  stag[:, 0:nblk*128])
            ag_ready[nm] = sST.n
            P.wait("gpsimd", sST, ag_ready[nm])
            Prog.TRACE.append(("gpsimd", "inc", "sAG" + nm, 1))
            P.op("gpsimd", None, lambda g, nm=nm: g.collective_compute(
                "AllGather", ALU.bypass,
                replica_groups=[[0, 1, 2, 3, 4, 5, 6, 7]],
                ins=[pre_loc[nm].ap()], outs=[pre_all[nm].ap()],
            ).then_inc(sAG[nm].h, 1))
            sAG[nm].n = 1
        P.run(nc, "phW")

    if "w" in phases:
        _phase_w()

    def _phase_p1():
      with ExitStack() as st1:
        xf = [sbt("p1_xf%d" % i, [128, NKC * TT], BF16, st1)
              for i in range(2)]
        sq_sc = [sbt("p1_sq%d" % i, [128, 512], BF16, st1) for i in range(2)]
        row_s = sbt("p1_rows", [1, 4 * TT], F32, st1)
        k_fm = sbt("p1_kfm", [128, NCH * D], BF16, st1)
        v_tm = sbt("p1_vtm", [128, NCH * D], BF16, st1)
        k_tm = sbt("p1_ktm", [128, NCH * D], BF16, st1)
        kkd = sbt("p1_kkd", [128, D], BF16, st1)
        ktc = [sbt("p1_ktc%d" % i, [128, D], BF16, st1) for i in range(2)]
        kts = [sbt("p1_kts%d" % i, [128, D], BF16, st1) for i in range(2)]
        trc = sbt("p1_trc", [128, D], BF16, st1)
        trs = sbt("p1_trs", [128, D], BF16, st1)
        ph_b = sbt("p1_ph", [128, D], F32, st1)
        rr_b = sbt("p1_rr", [128, D], F32, st1)
        kd_tm = sbt("p1_kdtm", [128, D], BF16, st1)
        states = sbt("p1_st", [128, H * 256], F32, st1)
        st_bf = sbt("p1_stbf", [128, H * 256], BF16, st1)
        tcol = sbt("p1_tcol", [128, 1], F32, st1)
        c2pi = sbt("p1_c2pi", [128, 1], F32, st1)
        p_row = pst("p1_prow", [1, 1024], F32, st1)
        p_bc = pst("p1_pbc", [128, 1024], F32, st1)
        p_a = pst("p1_pa", [128, 2048], F32, st1)
        rk = Ring("p1_rk", "sk", 2048, st1, ring_sems[0], nbuf=3)
        rv = Ring("p1_rv", "sv", 2048, st1, ring_sems[1])
        rkt = Ring("p1_rkt", "skt", 2048, st1, ring_sems[2])

        P = Prog()
        P.dma("gpsimd", sGQ, kd_tm[:], kd_tm_in.ap())
        n_kdtm = sGQ.n
        P.op("vector", sDV, lambda v: v.memset(states[:], 0.0))
        P.op("vector", sDV, lambda v: v.memset(c2pi[:], -TWO_PI))
        P.wait("sync", sAG["sk"], 1)
        P.wait("sync", sAG["sv"], 1)
        P.wait("sync", sAG["skt"], 1)
        rk.start(P, [j * 2048 for _ in range(NT) for j in range(H)])
        seq_v = [r * 2048 for _ in range(NT) for r in range(16)]
        rv.start(P, seq_v)
        rkt.start(P, seq_v)

        st_c = {"spill": 0, "upd": 0, "mm_done": {}}
        sq_done = {}

        def xbar(t, bf):
            P.wait("sync", sXC, ctx["n_xbf"])
            for kc in range(NKC):
                P.dma("sync", sXB, bf[:, kc*TT:(kc+1)*TT],
                      x_bf.ap()[t*TT:(t+1)*TT, kc*128:(kc+1)*128],
                      transpose=True)
            return sXB.n

        def layer_norm(t, bf, n_xfm, n_spill_xn):
            P.wait("tensor", sXB, n_xfm)
            P.wait("tensor", sDV, sDV.n)
            for kc in range(NKC):
                P.wait("scalar", sXB, n_xfm)
                if kc >= 2:
                    P.wait("scalar", sPE, sq_done[kc - 2])
                elif sq_done.get(NKC - 2 + kc):
                    P.wait("scalar", sPE, sq_done[NKC - 2 + kc])
                P.op("scalar", sAC, lambda s, kc=kc, bf=bf: s.activation(
                    out=sq_sc[kc % 2][:], in_=bf[:, kc*TT:(kc+1)*TT],
                    func=AF.Square))
                P.op("tensor", None, lambda t_, kc=kc, bf=bf: t_.matmul(
                    p_row[0:1, 0:TT], ones_col[:], bf[:, kc*TT:(kc+1)*TT],
                    start=(kc == 0), stop=(kc == NKC - 1)))
                P.wait("tensor", sAC, sAC.n)
                sq_done[kc] = P.op("tensor", sPE, lambda t_, kc=kc: t_.matmul(
                    p_row[0:1, 512:512 + TT], ones_col[:], sq_sc[kc % 2][:],
                    start=(kc == 0), stop=(kc == NKC - 1)))
            P.wait("vector", sPE, sPE.n)
            P.op("vector", sDV, lambda v: v.tensor_scalar(
                out=row_s[0:1, 0:TT], in0=p_row[0:1, 0:TT],
                scalar1=-1.0 / D, scalar2=None, op0=ALU.mult))
            P.op("vector", sDV, lambda v: v.tensor_tensor(
                out=row_s[0:1, 1024:1024 + TT], in0=row_s[0:1, 0:TT],
                in1=row_s[0:1, 0:TT], op=ALU.mult))
            P.op("vector", sDV, lambda v: v.tensor_scalar(
                out=row_s[0:1, 1536:1536 + TT], in0=p_row[0:1, 512:512 + TT],
                scalar1=1.0 / D, scalar2=None, op0=ALU.mult))
            P.op("vector", sDV, lambda v: v.tensor_tensor(
                out=row_s[0:1, 1024:1024 + TT], in0=row_s[0:1, 1536:1536 + TT],
                in1=row_s[0:1, 1024:1024 + TT], op=ALU.subtract))
            P.op("vector", sDV, lambda v: v.tensor_scalar(
                out=row_s[0:1, 1024:1024 + TT], in0=row_s[0:1, 1024:1024 + TT],
                scalar1=1e-5, scalar2=None, op0=ALU.add))
            P.wait("scalar", sDV, sDV.n)
            P.op("scalar", sAC, lambda s: s.activation(
                out=row_s[0:1, 1536:1536 + TT], in_=row_s[0:1, 1024:1024 + TT],
                func=AF.Sqrt))
            P.wait("vector", sAC, sAC.n)
            P.op("vector", sDV, lambda v: v.reciprocal(
                out=row_s[0:1, 512:512 + TT], in_=row_s[0:1, 1536:1536 + TT]))
            n_rows = sDV.n
            P.wait("tensor", sDV, n_rows)
            P.op("tensor", None, lambda t_: t_.matmul(
                p_bc[:, 0:512], ones_row[:], row_s[0:1, 0:TT],
                start=True, stop=True))
            P.op("tensor", sPE, lambda t_: t_.matmul(
                p_bc[:, 512:1024], ones_row[:], row_s[0:1, 512:512 + TT],
                start=True, stop=True))
            P.wait("vector", sPE, sPE.n)
            P.wait("vector", sSPx, n_spill_xn)
            nm_rep = bass.AP(p_bc, 0, [[1024, 128], [0, NKC], [1, TT]])
            rs_rep = bass.AP(p_bc, 512, [[1024, 128], [0, NKC], [1, TT]])
            xn3 = bf[:].rearrange("p (kc t) -> p kc t", kc=NKC)
            P.op("vector", sDV, lambda v: v.tensor_tensor(
                out=xn3, in0=xn3, in1=nm_rep, op=ALU.add))
            P.wait("vector", sDV, sDV.n)
            P.op("vector", sDV, lambda v: v.tensor_tensor(
                out=xn3, in0=xn3, in1=rs_rep, op=ALU.mult))
            n_xn = sDV.n
            P.wait("sync", sDV, n_xn)
            return n_xn, P.dma("sync", sSPx,
                               xnfm_sp.ap()[:, t*NKC*TT:(t+1)*NKC*TT], bf[:])

        def chain(t, c, pe_done_before):
            gt0 = t * TT + c * CH
            par = c % 2
            P.wait("vector", sPO, st_c.get("ph_rd", 0))
            P.op("vector", sDV, lambda v, gt0=gt0: v.tensor_scalar(
                out=tcol[:], in0=iota_col[:], scalar1=pos_col[:, 0:1],
                scalar2=float(gt0), op0=ALU.add, op1=ALU.add))
            P.wait("vector", sDV, sDV.n)
            P.wait("gpsimd", sDV, sDV.n)
            P.wait("gpsimd", sAC, st_c.get("tr_rd", 0))
            P.wait("gpsimd", sGQ, n_kdtm)
            P.op("gpsimd", sPO, lambda g: g.tensor_tensor(
                out=ph_b[:], in0=theta_tm[:],
                in1=tcol[:].broadcast_to([128, D]), op=ALU.mult))
            n_ph = sPO.n
            P.wait("scalar", sPO, n_ph)
            P.op("scalar", sAC, lambda s: s.activation(
                out=rr_b[:], in_=ph_b[:], func=AF.Copy,
                scale=1.0 / TWO_PI, bias=RND_M))
            P.op("scalar", sAC, lambda s: s.activation(
                out=rr_b[:], in_=rr_b[:], func=AF.Copy, bias=-RND_M))
            n_rr = sAC.n
            P.wait("gpsimd", sAC, n_rr)
            P.op("gpsimd", sPO, lambda g: g.tensor_tensor(
                out=rr_b[:], in0=rr_b[:],
                in1=c2pi[:].broadcast_to([128, D]), op=ALU.mult))
            P.wait("gpsimd", sPO, sPO.n)
            P.op("gpsimd", sPO, lambda g: g.tensor_tensor(
                out=rr_b[:], in0=rr_b[:], in1=ph_b[:], op=ALU.add))
            st_c["ph_rd"] = sPO.n
            P.wait("scalar", sPO, sPO.n)
            P.op("scalar", sAC, lambda s: s.activation(
                out=trs[:], in_=rr_b[:], func=AF.Sin))
            P.op("scalar", sAC, lambda s: s.activation(
                out=rr_b[:], in_=rr_b[:], func=AF.Abs))
            P.op("scalar", sAC, lambda s: s.activation(
                out=trc[:], in_=rr_b[:], func=AF.Sin,
                scale=-1.0, bias=half_pi[:, 0:1]))
            st_c["tr_rd"] = 0
            n_cos = sAC.n
            P.wait("gpsimd", sAC, n_cos)
            P.wait("gpsimd", sPE, pe_done_before)
            P.op("gpsimd", sPO, lambda g, c=c: g.tensor_tensor(
                out=kkd[:], in0=k_tm[:, c*D:(c+1)*D], in1=kd_tm[:],
                op=ALU.mult))
            P.wait("gpsimd", sPO, sPO.n)
            P.op("gpsimd", sPO, lambda g, par=par: g.tensor_tensor(
                out=ktc[par][:], in0=kkd[:], in1=trc[:], op=ALU.mult))
            P.op("gpsimd", sPO, lambda g, par=par: g.tensor_tensor(
                out=kts[par][:], in0=kkd[:], in1=trs[:], op=ALU.mult))
            st_c["tr_rd"] = sPO.n
            return sPO.n

        def scan_mms(t, c, n_pool, n_vev):
            par = c % 2
            P.wait("vector", sSPs, st_c["spill"])
            P.op("vector", sDV,
                 lambda v: v.tensor_copy(out=st_bf[:], in_=states[:]))
            P.wait("sync", sDV, sDV.n)
            st_c["spill"] = P.dma(
                "sync", sSPs,
                st_sp.ap()[:, (t*NCH + c)*H*256:(t*NCH + c + 1)*H*256],
                st_bf[:])
            for grp in range(4):
                P.wait("tensor", sPO, n_pool)
                P.wait("tensor", sDV, st_c["upd"])
                P.wait("tensor", sDV, n_vev)
                for hh in range(4):
                    h = grp * 4 + hh
                    P.op("tensor", None,
                         lambda t_, h=h, hh=hh, c=c, par=par:
                         t_.matmul(
                             p_a[:, 1024 + hh*256:1024 + hh*256 + 128],
                             ktc[par][:, h*128:(h+1)*128],
                             v_tm[:, c*D + h*128:c*D + (h+1)*128],
                             start=(hh % 2 == 0), stop=False))
                    P.op("tensor", sPE if hh == 3 else None,
                         lambda t_, h=h, hh=hh, c=c, par=par:
                         t_.matmul(
                             p_a[:, 1024 + hh*256 + 128:1024 + (hh+1)*256],
                             kts[par][:, h*128:(h+1)*128],
                             v_tm[:, c*D + h*128:c*D + (h+1)*128],
                             start=False, stop=(hh % 2 == 1)))
                P.wait("vector", sPE, sPE.n)
                for hh in range(4):
                    h = grp * 4 + hh
                    P.op("vector", sDV if hh == 3 else None,
                         lambda v, h=h, hh=hh:
                         v.scalar_tensor_tensor(
                             out=states[:, h*256:(h+1)*256],
                             in0=states[:, h*256:(h+1)*256],
                             scalar=bdec[h],
                             in1=p_a[:, 1024 + hh*256:1024 + (hh+1)*256],
                             op0=ALU.mult, op1=ALU.add))
                st_c["upd"] = sDV.n
            st_c["mm_done"][t * NCH + c] = sPE.n

        n_spill_k = n_spill_v = 0
        spx = {0: 0, 1: 0}
        prod_end = {-1: 0, -2: 0}
        n_xn_t = {}
        n_xfm = xbar(0, xf[0])
        n_xn_t[0], spx[0] = layer_norm(0, xf[0], n_xfm, 0)
        for t in range(NT):
            cur = xf[t % 2]
            n_xn = n_xn_t[t]
            if t + 1 < NT:
                nb = xf[(t + 1) % 2]
                P.wait("sync", sPE, prod_end[t - 1])
                P.wait("sync", sSPx, spx[(t + 1) % 2])
                n_xfm2 = xbar(t + 1, nb)
            P.wait("tensor", sDV, n_xn)
            P.wait("vector", sSPv, n_spill_v)
            P.wait("vector", sPO, sPO.n)
            for ring, dst in ((rkt, k_tm), (rv, v_tm)):
                for fv in range(4):
                    P.wait("tensor", sDV, sDV.n)
                    for qtr in range(4):
                        buf = ring.take(P)
                        for kci in range(4):
                            kc = qtr * 4 + kci
                            for c in range(NCH):
                                P.op("tensor",
                                     sPE if (kci == 3 and c == NCH - 1) else None,
                                     lambda t_, kc=kc, kci=kci, c=c, buf=buf,
                                     cur=cur:
                                     t_.matmul(
                                         p_a[:, c*512:(c+1)*512],
                                         cur[:, kc*TT + c*128:kc*TT + (c+1)*128],
                                         buf[:, kci*512:(kci+1)*512],
                                         start=(kc == 0), stop=(kc == NKC - 1)))
                        ring.done(P, sPE)
                    P.wait("vector", sPE, sPE.n)
                    P.op("vector", sDV, lambda v, fv=fv, dst=dst: v.tensor_copy(
                        out=bass.AP(dst, fv * 512,
                                    [[NCH * D, 128], [D, NCH], [1, 512]]),
                        in_=bass.AP(p_a, 0, [[2048, 128], [512, NCH], [1, 512]])))
            n_vev = sDV.n
            P.wait("sync", sDV, n_vev)
            n_spill_v = P.dma("sync", sSPv,
                              vtm_sp.ap()[:, t*NCH*D:(t+1)*NCH*D], v_tm[:])
            if t + 1 < NT:
                n_xn_t[t + 1], spx[(t + 1) % 2] = layer_norm(
                    t + 1, nb, n_xfm2, spx[(t + 1) % 2])
            P.wait("vector", sSPk, n_spill_k)
            chains = {}
            evk = {}
            for j in range(H):
                buf = rk.take(P)
                P.wait("tensor", sDV, evk.get(j - 2, n_vev))
                for kc in range(NKC):
                    P.op("tensor", sPE if kc == NKC - 1 else None,
                         lambda t_, j=j, kc=kc, buf=buf, cur=cur: t_.matmul(
                             p_a[:, (j % 2)*512:(j % 2)*512 + 512],
                             buf[:, kc*128:(kc+1)*128],
                             cur[:, kc*TT:(kc+1)*TT],
                             start=(kc == 0), stop=(kc == NKC - 1)))
                rk.done(P, sPE)
                P.wait("vector", sPE, sPE.n)
                P.op("vector", sDV, lambda v, j=j: v.tensor_copy(
                    out=bass.AP(k_fm, j * 128,
                                [[NCH * D, 128], [D, NCH], [1, 128]]),
                    in_=bass.AP(p_a, (j % 2)*512,
                                [[2048, 128], [128, NCH], [1, 128]])))
                evk[j] = sDV.n
                gci0 = t * NCH
                if j == 0:
                    chains[0] = chain(t, 0, st_c["mm_done"].get(gci0 - 2, 0))
                    chains[1] = chain(t, 1, st_c["mm_done"].get(gci0 - 1, 0))
                if j == 7:
                    scan_mms(t, 0, chains[0], n_vev)
                    chains[2] = chain(t, 2, st_c["mm_done"][gci0])
                if j == 11:
                    scan_mms(t, 1, chains[1], n_vev)
                    chains[3] = chain(t, 3, st_c["mm_done"][gci0 + 1])
                if j == 15:
                    scan_mms(t, 2, chains[2], n_vev)
            P.wait("sync", sDV, sDV.n)
            n_spill_k = P.dma("sync", sSPk,
                              kfm_sp.ap()[:, t*NCH*D:(t+1)*NCH*D], k_fm[:])
            prod_end[t] = sPE.n
            scan_mms(t, 3, chains[3], n_vev)
        P.wait("sync", sDV, sDV.n)
        n_stl = P.dma("sync", sSPs, st_loc.ap(), states[:])
        P.wait("gpsimd", sSPs, n_stl)
        Prog.TRACE.append(("gpsimd", "inc", "sAGst", 1))
        P.op("gpsimd", None, lambda g: g.collective_compute(
            "AllGather", ALU.bypass,
            replica_groups=[[0, 1], [2, 3], [4, 5], [6, 7]],
            ins=[st_loc.ap()], outs=[st_pair.ap()],
        ).then_inc(sAG["st"].h, 1))
        sAG["st"].n = 1
        P.run(nc, "phP1")

    if "p1" in phases:
        _phase_p1()

    def _phase_p2a():
      with ExitStack() as st2:
        xn_fm = sbt("p2_xnfm", [128, NKC * TT], BF16, st2)
        q_fm = sbt("p2_qfm", [128, NCH * D], BF16, st2)
        k_ch = [sbt("p2_kch%d" % i, [128, D], BF16, st2) for i in range(2)]
        v_ch = [sbt("p2_vch%d" % i, [128, D], BF16, st2) for i in range(2)]
        qpc = [sbt("p2_qpc%d" % i, [128, D], BF16, st2) for i in range(2)]
        qps = [sbt("p2_qps%d" % i, [128, D], BF16, st2) for i in range(2)]
        kpc = [sbt("p2_kpc%d" % i, [128, D], BF16, st2) for i in range(2)]
        kps = [sbt("p2_kps%d" % i, [128, D], BF16, st2) for i in range(2)]
        trc = [sbt("p2_trc%d" % i, [128, D], BF16, st2) for i in range(2)]
        trs = [sbt("p2_trs%d" % i, [128, D], BF16, st2) for i in range(2)]
        sttr = [sbt("p2_sttr%d" % i, [128, H * 256], BF16, st2)
                for i in range(2)]
        sTb = [sbt("p2_sT%d" % i, [128, 512], BF16, st2) for i in range(2)]
        ph_b = sbt("p2_ph", [128, D], F32, st2)
        rr_b = sbt("p2_rr", [128, D], F32, st2)
        qd_fm = sbt("p2_qdfm", [128, D], BF16, st2)
        ddt = sbt("p2_ddt", [128, D], BF16, st2)
        stin = sbt("p2_stin", [128, H * 256], BF16, st2)
        o_t = sbt("p2_ot", [128, H * TT], BF16, st2)
        pout = sbt("p2_pout", [128, 512], BF16, st2)
        tmp_o = sbt("p2_tmpo", [128, 512], BF16, st2)
        cM = sbt("p2_cM", [128, 1], F32, st2)
        cMn = sbt("p2_cMn", [128, 1], F32, st2)
        cI2P = sbt("p2_cI2P", [128, 1], F32, st2)
        cM4 = sbt("p2_cM4", [128, 1], F32, st2)
        p_q = pst("p2_pq", [128, 1024], F32, st2)
        p_s = pst("p2_ps", [128, 1024], F32, st2)
        p_o = pst("p2_po", [128, 1024], F32, st2)
        p_p = pst("p2_pp", [128, 1024], F32, st2)
        rq = Ring("p2_rq", "sq", 2048, st2, ring_sems[0])
        rp = Ring("p2_rp", "sp", 4096, st2, ring_sems[1])

        P = Prog()
        P.dma("gpsimd", sGQ, qd_fm[:], qd_fm_in.ap())
        P.dma("gpsimd", sGQ, ddt[:], ddt_in.ap())
        n_c2 = sGQ.n
        P.op("vector", sDV, lambda v: v.memset(cM[:], RND_M))
        P.op("vector", sDV, lambda v: v.memset(cMn[:], -RND_M))
        P.op("vector", sDV, lambda v: v.memset(cI2P[:], 1.0 / TWO_PI))
        P.op("vector", sDV, lambda v: v.memset(cM4[:], RND_M + 0.25))
        nc.const_aps.register(F32, RND_M, cM[:, 0:1])
        nc.const_aps.register(F32, -RND_M, cMn[:, 0:1])
        nc.const_aps.register(F32, 1.0 / TWO_PI, cI2P[:, 0:1])
        nc.const_aps.register(F32, RND_M + 0.25, cM4[:, 0:1])
        P.wait("gpsimd", sAG["st"], 1)
        nstin = P.dma("gpsimd", sSTP, stin[:], st_pair.ap()[0:128, :])
        P.wait("vector", sSTP, nstin)
        P.op("vector", sDV, lambda v: v.tensor_scalar(
            out=stin[:], in0=stin[:], scalar1=mask_col[:, 0:1], scalar2=None,
            op0=ALU.mult))
        P.wait("sync", sAG["sq"], 1)
        P.wait("sync", sAG["sp"], 1)
        rq.start(P, [j * 2048 for _ in range(NT) for j in range(H)])
        rp.start(P, [r * 4096 for _ in range(NT) for r in range(8)])

        def chain(ci, pe_done_before):
            t, c = divmod(ci, NCH)
            par = ci % 2
            gt0 = t * TT + c * CH
            P.wait("sync", sPE, pe_done_before)
            n_k = P.dma("sync", sLK[par], k_ch[par][:],
                        kfm_sp.ap()[:, ci*D:(ci+1)*D])
            n_v = P.dma("sync", sLV[par], v_ch[par][:],
                        vtm_sp.ap()[:, ci*D:(ci+1)*D])
            n_lo = P.dma("sync", sLO[par], sttr[par][:],
                         st_sp.ap()[:, ci*H*256:(ci+1)*H*256])
            P.wait("vector", sAC, sAC.n)
            P.wait("vector", sPO, sPO.n)
            P.op("vector", sDV, lambda v, gt0=gt0: v.tensor_scalar(
                out=rr_b[:], in0=iota_fm[:], scalar1=pos_col[:, 0:1],
                scalar2=float(gt0), op0=ALU.add, op1=ALU.add))
            P.wait("vector", sDV, sDV.n)
            th_rep = theta_t[:].unsqueeze(2).broadcast_to([128, H, 128])
            P.op("vector", sDV, lambda v, th_rep=th_rep: v.tensor_tensor(
                out=ph_b[:].rearrange("p (h t) -> p h t", h=H),
                in0=rr_b[:].rearrange("p (h t) -> p h t", h=H),
                in1=th_rep, op=ALU.mult))
            n_ph = sDV.n
            P.wait("scalar", sDV, n_ph)
            P.op("scalar", sAC, lambda s: s.activation(
                out=rr_b[:], in_=ph_b[:], func=AF.Copy,
                scale=1.0 / TWO_PI, bias=RND_M))
            P.op("scalar", sAC, lambda s: s.activation(
                out=rr_b[:], in_=rr_b[:], func=AF.Copy, bias=-RND_M))
            n_rr = sAC.n
            P.wait("vector", sAC, n_rr)
            P.op("vector", sDV, lambda v: v.scalar_tensor_tensor(
                out=rr_b[:], in0=rr_b[:], scalar=-TWO_PI, in1=ph_b[:],
                op0=ALU.mult, op1=ALU.add))
            P.wait("scalar", sDV, sDV.n)
            P.wait("scalar", sPE, pe_done_before)
            P.op("scalar", sAC, lambda s, par=par: s.activation(
                out=trs[par][:], in_=rr_b[:], func=AF.Sin))
            n_sin = sAC.n
            P.op("scalar", sAC, lambda s: s.activation(
                out=rr_b[:], in_=ph_b[:], func=AF.Copy,
                scale=1.0 / TWO_PI, bias=RND_M + 0.25))
            P.op("scalar", sAC, lambda s: s.activation(
                out=rr_b[:], in_=rr_b[:], func=AF.Copy, bias=-RND_M))
            n_rr2 = sAC.n
            P.wait("vector", sAC, n_rr2)
            P.op("vector", sDV, lambda v: v.scalar_tensor_tensor(
                out=rr_b[:], in0=rr_b[:], scalar=-TWO_PI, in1=ph_b[:],
                op0=ALU.mult, op1=ALU.add))
            P.wait("scalar", sDV, sDV.n)
            P.op("scalar", sAC, lambda s, par=par: s.activation(
                out=trc[par][:], in_=rr_b[:], func=AF.Sin,
                bias=half_pi[:, 0:1]))
            n_cos = sAC.n
            P.wait("gpsimd", sAC, n_cos)
            P.wait("gpsimd", sLK[par], n_k)
            P.wait("gpsimd", sPE, pe_done_before)
            P.op("gpsimd", sPO, lambda g, par=par: g.tensor_tensor(
                out=kpc[par][:], in0=k_ch[par][:], in1=trc[par][:],
                op=ALU.mult))
            P.op("gpsimd", sPO, lambda g, par=par: g.tensor_tensor(
                out=kps[par][:], in0=k_ch[par][:], in1=trs[par][:],
                op=ALU.mult))
            P.wait("vector", sLO[par], n_lo)
            P.wait("vector", sGQ, n_c2)
            P.wait("vector", sPE, pe_done_before)
            for h in range(H):
                P.op("vector", sDV if h == H - 1 else None,
                     lambda v, h=h, m=ci, par=par: v.scalar_tensor_tensor(
                         out=sttr[par][:, h*256:(h+1)*256],
                         in0=stin[:, h*256:(h+1)*256],
                         scalar=float(bdec[h] ** m),
                         in1=sttr[par][:, h*256:(h+1)*256],
                         op0=ALU.mult, op1=ALU.add))
            return [sPO.n, sDV.n, n_v]

        def chain_b(ci, deps, n_qev):
            t, c = divmod(ci, NCH)
            par = ci % 2
            P.wait("gpsimd", sDV, n_qev)
            qsl = bass.AP(q_fm, c * D, [[NCH * D, 128], [1, D]])
            P.op("gpsimd", sPO, lambda g, qsl=qsl, par=par: g.tensor_tensor(
                out=qpc[par][:], in0=qsl, in1=trc[par][:], op=ALU.mult))
            P.op("gpsimd", sPO, lambda g, qsl=qsl, par=par: g.tensor_tensor(
                out=qps[par][:], in0=qsl, in1=trs[par][:], op=ALU.mult))
            deps[0] = sPO.n
            if "qp_sp" in debug:
                P.wait("scalar", sPO, sPO.n)
                P.dma("scalar", sSC, qp_sp.ap()[:, ci*D:(ci+1)*D], qpc[par][:])
                P.dma("scalar", sSC, kp_sp.ap()[:, ci*D:(ci+1)*D], kpc[par][:])
                P.dma("scalar", sSC, trc_sp.ap()[:, ci*D:(ci+1)*D], trc[par][:])
                P.wait("gpsimd", sSC, sSC.n)
                P.wait("vector", sSC, sSC.n)
                P.wait("tensor", sSC, sSC.n)

        mst = {"mdv": 0}

        def mms(ci, deps):
            n_pool, n_sttr, n_v = deps
            t, c = divmod(ci, NCH)
            par = ci % 2
            evd = []
            for grp in range(4):
                gp = grp % 2
                P.wait("tensor", sPO, n_pool)
                P.wait("tensor", sDV, evd[grp - 2] if grp >= 2 else mst["mdv"])
                for hh in range(4):
                    h = grp * 4 + hh
                    P.op("tensor", None, lambda t_, h=h, hh=hh, gp=gp, par=par:
                         t_.matmul(
                             p_s[:, gp*512 + hh*128:gp*512 + (hh+1)*128],
                             kpc[par][:, h*128:(h+1)*128],
                             qpc[par][:, h*128:(h+1)*128],
                             start=(hh == 0), stop=False))
                    P.op("tensor", sPE if hh == 3 else None,
                         lambda t_, h=h, hh=hh, gp=gp, par=par: t_.matmul(
                             p_s[:, gp*512 + hh*128:gp*512 + (hh+1)*128],
                             kps[par][:, h*128:(h+1)*128],
                             qps[par][:, h*128:(h+1)*128],
                             start=False, stop=(hh == 3)))
                P.wait("vector", sPE, sPE.n)
                P.op("vector", sDV, lambda v, grp=grp, gp=gp, par=par:
                     v.tensor_tensor(
                         out=sTb[par][:],
                         in0=bass.AP(p_s, gp*512, [[1024, 128], [1, 512]]),
                         in1=ddt[:, grp*512:(grp+1)*512], op=ALU.mult))
                n_mask = sDV.n
                P.wait("tensor", sDV, n_mask)
                P.wait("tensor", sLV[par], n_v)
                for hh in range(4):
                    h = grp * 4 + hh
                    P.op("tensor", None,
                         lambda t_, h=h, hh=hh, gp=gp, par=par: t_.matmul(
                             p_o[:, gp*512 + hh*128:gp*512 + (hh+1)*128],
                             v_ch[par][:, h*128:(h+1)*128],
                             sTb[par][:, hh*128:(hh+1)*128],
                             start=(hh == 0), stop=False))
                    P.op("tensor", None, lambda t_, h=h, hh=hh, gp=gp, par=par:
                         t_.matmul(
                             p_q[:, gp*512 + hh*128:gp*512 + (hh+1)*128],
                             sttr[par][:, h*256:h*256 + 128],
                             qpc[par][:, h*128:(h+1)*128],
                             start=(hh == 0), stop=False))
                    P.op("tensor", sPE if hh == 3 else None,
                         lambda t_, h=h, hh=hh, gp=gp, par=par: t_.matmul(
                             p_q[:, gp*512 + hh*128:gp*512 + (hh+1)*128],
                             sttr[par][:, h*256 + 128:(h+1)*256],
                             qps[par][:, h*128:(h+1)*128],
                             start=False, stop=(hh == 3)))
                P.wait("vector", sPE, sPE.n)
                P.op("vector", sDV, lambda v, grp=grp, gp=gp: v.tensor_tensor(
                    out=tmp_o[:],
                    in0=bass.AP(p_q, gp*512, [[1024, 128], [1, 512]]),
                    in1=qd_fm[:, grp*512:(grp+1)*512], op=ALU.mult))
                P.wait("vector", sDV, sDV.n)
                P.op("vector", sDV, lambda v, grp=grp, gp=gp, c=c:
                     v.tensor_tensor(
                         out=bass.AP(o_t, grp*4*TT + c*128,
                                     [[H * TT, 128], [TT, 4], [1, 128]]),
                         in0=bass.AP(p_o, gp*512,
                                     [[1024, 128], [128, 4], [1, 128]]),
                         in1=tmp_o[:].rearrange("p (a b) -> p a b", a=4),
                         op=ALU.add))
                evd.append(sDV.n)
            mst["mdv"] = sDV.n

        pend = None
        pend_b = None
        pe_done = [0] * 40
        chained = {}
        qp_end = {-1: 0}
        for t in range(NT):
            P.wait("sync", sPE, qp_end[t - 1])
            P.wait("sync", sSPx, sSPx.n)
            n_xn = P.dma("sync", sXN, xn_fm[:],
                         xnfm_sp.ap()[:, t*NKC*TT:(t+1)*NKC*TT])
            if t == 0:
                chained[0] = chain(0, 0)
                pe_done[0] = sPE.n
            P.wait("tensor", sXN, n_xn)
            P.wait("vector", sPO, sPO.n)
            qev = {}
            for j in range(H):
                buf = rq.take(P)
                P.wait("tensor", sDV, qev.get(j - 2, mst["mdv"]))
                for kc in range(NKC):
                    P.op("tensor", sPE if kc == NKC - 1 else None,
                         lambda t_, j=j, kc=kc, buf=buf: t_.matmul(
                             p_q[:, (j % 2)*512:(j % 2)*512 + 512],
                             buf[:, kc*128:(kc+1)*128],
                             xn_fm[:, kc*TT:(kc+1)*TT],
                             start=(kc == 0), stop=(kc == NKC - 1)))
                rq.done(P, sPE)
                P.wait("vector", sPE, sPE.n)
                P.op("vector", sDV, lambda v, j=j: v.tensor_copy(
                    out=bass.AP(q_fm, j * 128,
                                [[NCH * D, 128], [D, NCH], [1, 128]]),
                    in_=bass.AP(p_q, (j % 2)*512,
                                [[1024, 128], [128, NCH], [1, 128]])))
                qev[j] = sDV.n
            mst["mdv"] = max(mst["mdv"], sDV.n)
            n_qev = sDV.n
            qp_end[t] = sPE.n
            chain_b(t * NCH, chained[t * NCH], n_qev)
            for c in range(NCH):
                ci = t * NCH + c
                nxt = ci + 1
                if nxt < NT * NCH:
                    chained[nxt] = chain(nxt, pe_done[nxt - 2] if nxt >= 2 else 0)
                    pe_done[nxt] = sPE.n
                    if nxt % NCH != 0:
                        chain_b(nxt, chained[nxt], n_qev)
                mms(ci, chained[ci])
                pe_done[ci] = sPE.n
            if "o_sp" in debug:
                P.wait("scalar", sDV, sDV.n)
                P.dma("scalar", sSC, o_sp.ap()[:, t*H*TT:(t+1)*H*TT], o_t[:])
                P.wait("tensor", sSC, sSC.n)
            pvd = {}
            for fb in range(NFB):
                bufs = [rp.take(P), rp.take(P)]
                for c in range(NCH):
                    k2 = fb * NCH + c
                    if k2 >= 2:
                        P.wait("tensor", sAC, pvd[k2 - 2])
                    else:
                        P.wait("tensor", sDV, mst["mdv"])
                    for h in range(H):
                        wbuf = bufs[h // 8]
                        P.op("tensor", sPE if h == H - 1 else None,
                             lambda t_, h=h, c=c, wbuf=wbuf: t_.matmul(
                                 p_p[:, (c % 2)*512:(c % 2)*512 + 512],
                                 o_t[:, h*TT + c*128:h*TT + (c+1)*128],
                                 wbuf[:, (h % 8)*512:((h % 8)+1)*512],
                                 start=(h == 0), stop=(h == H - 1)))
                    P.wait("scalar", sPE, sPE.n)
                    P.wait("scalar", sPW, sPW.n)
                    P.op("scalar", sAC, lambda s, c=c: s.copy(
                        out=pout[:],
                        in_=bass.AP(p_p, (c % 2)*512, [[1024, 128], [1, 512]])))
                    pvd[fb * NCH + c] = sAC.n
                    P.wait("scalar", sAC, sAC.n)
                    P.dma("scalar", sPW,
                          partial.ap()[t*TT + c*128:t*TT + (c+1)*128,
                                       fb*512:(fb+1)*512], pout[:])
                rp.done(P, sPE)
                rp.done(P, sPE)
        P.run(nc, "phP2a")

    if "p2a" in phases:
        _phase_p2a()

    def _phase_p2b():
      with ExitStack() as st3:
        xn_fm = sbt("p3_xnfm", [128, NKC * TT], BF16, st3)
        h_t = sbt("p3_h", [128, NJG * TT], BF16, st3)
        silu_g = sbt("p3_silu", [128, TT], BF16, st3)
        sg_sb = sbt("p3_sg", [128, NJG], F32, st3)
        su_sb = sbt("p3_su", [128, NJG], F32, st3)
        sd_mat = sbt("p3_sd", [128, D], F32, st3)
        par_ch = [sbt("p3_par%d" % i, [128, D], BF16, st3) for i in range(2)]
        x_ch = [sbt("p3_x%d" % i, [128, D], BF16, st3) for i in range(2)]
        o_st = [sbt("p3_o%d" % i, [128, D], F32, st3) for i in range(2)]
        xp_ch = [sbt("p3_xp%d" % i, [128, D], BF16, st3) for i in range(2)]
        tmp2 = [sbt("p3_tmp%d" % i, [128, D], BF16, st3) for i in range(2)]
        p8 = pst("p3_p8", [128, 4096], F32, st3)
        rg = Ring("p3_rg", "sg", 4096, st3, ring_sems[0], nbuf=3)
        ru = Ring("p3_ru", "su", 4096, st3, ring_sems[1], nbuf=3)
        rd = Ring("p3_rd", "sd", 4096, st3, ring_sems[2], nbuf=3)

        P = Prog()
        P.dma("sync", sSY, sg_sb[:], sg_t.ap())
        P.dma("sync", sSY, su_sb[:], su_t.ap())
        P.dma("sync", sSY, sd_mat[:], sd_row.ap().broadcast_to([128, D]))
        nconst3 = sSY.n
        P.wait("sync", sAG["sg"], 1)
        P.wait("sync", sAG["su"], 1)
        P.wait("sync", sAG["sd"], 1)
        seq_g = [r * 4096 for _ in range(NT) for r in range(22)]
        rg.start(P, seq_g)
        ru.start(P, seq_g)
        rd.start(P, [r * 4096 for _ in range(NT) for _ in range(2)
                     for r in range(22)])

        NRD = 22
        n_out = {0: 0, 1: 0}
        n_par = {0: 0, 1: 0}
        n_x = {0: 0, 1: 0}
        gu_end = {-1: 0}
        xp_rd = {0: 0, 1: 0}
        xp_wr = {0: 0, 1: 0}
        for t in range(NT):
            P.wait("sync", sPE, gu_end[t - 1])
            n_xn = P.dma("sync", sXN, xn_fm[:],
                         xnfm_sp.ap()[:, t*NKC*TT:(t+1)*NKC*TT])
            P.wait("tensor", sXN, n_xn)
            P.wait("scalar", sSY, nconst3)
            P.wait("vector", sPE, sPE.n)
            h_done = {}
            dn_ev = sDV.n
            for j in range(NJG):
                if j % 2 == 0:
                    gbuf = rg.take(P)
                    ubuf = ru.take(P)
                gp = j % 2
                P.wait("tensor", sDV, h_done.get(j - 2, dn_ev))
                for kc in range(NKC):
                    P.op("tensor", None, lambda t_, j=j, kc=kc, gbuf=gbuf, gp=gp:
                         t_.matmul(p8[:, gp*512:gp*512 + 512],
                                   gbuf[:, (gp*NKC + kc)*128:(gp*NKC + kc + 1)*128],
                                   xn_fm[:, kc*TT:(kc+1)*TT],
                                   start=(kc == 0), stop=(kc == NKC - 1)))
                for kc in range(NKC):
                    P.op("tensor", sPE if kc == NKC - 1 else None,
                         lambda t_, j=j, kc=kc, ubuf=ubuf, gp=gp:
                         t_.matmul(p8[:, 1024 + gp*512:1024 + gp*512 + 512],
                                   ubuf[:, (gp*NKC + kc)*128:(gp*NKC + kc + 1)*128],
                                   xn_fm[:, kc*TT:(kc+1)*TT],
                                   start=(kc == 0), stop=(kc == NKC - 1)))
                P.wait("scalar", sPE, sPE.n)
                P.wait("scalar", sDV, h_done.get(j - 1, 0))
                P.op("scalar", sAC, lambda s, j=j, gp=gp: s.activation(
                    out=silu_g[:],
                    in_=bass.AP(p8, gp*512, [[4096, 128], [1, 512]]),
                    func=AF.Silu, scale=sg_sb[:, j:j+1]))
                P.wait("vector", sAC, sAC.n)
                P.op("vector", sDV, lambda v, j=j, gp=gp: v.scalar_tensor_tensor(
                    out=h_t[:, j*TT:(j+1)*TT],
                    in0=bass.AP(p8, 1024 + gp*512, [[4096, 128], [1, 512]]),
                    scalar=su_sb[:, j:j+1], in1=silu_g[:],
                    op0=ALU.mult, op1=ALU.mult))
                h_done[j] = sDV.n
                if j % 2 == 1 or j == NJG - 1:
                    rg.done(P, sPE)
                    ru.done(P, sPE)
            n_h = sDV.n
            gu_end[t] = sPE.n
            for half in range(2):
                cc0 = half * 2
                for cc in range(2):
                    c = cc0 + cc
                    par = c % 2
                    P.wait("scalar", sDV, sDV.n)
                    n_par[par] = P.dma(
                        "scalar", sRP[par], par_ch[par][:],
                        partial.ap()[t*TT + c*128:t*TT + (c+1)*128, :])
                    n_x[par] = P.dma(
                        "gpsimd", sRX[par], x_ch[par][:],
                        x_in.ap()[t*TT + c*128:t*TT + (c+1)*128, :])
                P.wait("tensor", sDV, sDV.n)
                P.wait("tensor", sDV, n_h)
                for r in range(NRD):
                    dbuf = rd.take(P)
                    for ki in range(2):
                        kc = r * 2 + ki
                        if kc >= NKD:
                            continue
                        for cc in range(2):
                            c = cc0 + cc
                            for fb in range(NFB):
                                last = ((ki == 1 or kc == NKD - 1)
                                        and fb == NFB - 1)
                                P.op("tensor",
                                     sPE if (last and cc == 1) else None,
                                     lambda t_, kc=kc, fb=fb, c=c, cc=cc,
                                     ki=ki, dbuf=dbuf:
                                     t_.matmul(
                                         p8[:, (cc*4 + fb)*512:(cc*4 + fb + 1)*512],
                                         h_t[:, kc*TT + c*128:kc*TT + (c+1)*128],
                                         dbuf[:, (ki*4 + fb)*512:(ki*4 + fb + 1)*512],
                                         start=(kc == 0), stop=(kc == NKD - 1)))
                    rd.done(P, sPE)
                n_dn = sPE.n
                for cc in range(2):
                    c = cc0 + cc
                    par = c % 2
                    P.wait("gpsimd", sRX[par], n_x[par])
                    P.wait("gpsimd", sRP[par], n_par[par])
                    P.wait("gpsimd", sPO, xp_rd[par])
                    P.op("gpsimd", sPO, lambda g, par=par: g.tensor_tensor(
                        out=xp_ch[par][:], in0=x_ch[par][:],
                        in1=par_ch[par][:], op=ALU.add))
                    xp_wr[par] = sPO.n
                P.wait("vector", sPE, n_dn)
                for cc in range(2):
                    c = cc0 + cc
                    par = c % 2
                    P.wait("vector", sOU[par], n_out[par])
                    for fb in range(NFB):
                        P.op("vector", sDV, lambda v, fb=fb, cc=cc, par=par:
                             v.tensor_tensor(
                                 out=tmp2[par][:, fb*512:(fb+1)*512],
                                 in0=p8[:, (cc*4 + fb)*512:(cc*4 + fb + 1)*512],
                                 in1=sd_mat[:, fb*512:(fb+1)*512], op=ALU.mult))
                    nmul = sDV.n
                    P.wait("gpsimd", sDV, nmul)
                    P.wait("gpsimd", sPO, xp_wr[par])
                    P.op("gpsimd", sPO, lambda g, par=par: g.tensor_tensor(
                        out=o_st[par][:], in0=tmp2[par][:],
                        in1=xp_ch[par][:], op=ALU.add))
                    xp_rd[par] = sPO.n
                    P.wait("scalar", sPO, sPO.n)
                    n_out[par] = P.dma(
                        "scalar", sOU[par],
                        out_t.ap()[t*TT + c*128:t*TT + (c+1)*128, :],
                        o_st[par][:])
        P.wait("scalar", sOU[0], sOU[0].n)
        P.wait("scalar", sOU[1], sOU[1].n)
        P.run(nc, "phP2b")

    if "p2b" in phases:
        _phase_p2b()

    Pd = Prog()
    alldma = [sSY, sSC, sGQ, sPE, sDV, sAC, sPO, sXB, sXC, sXN, sST,
              sSPx, sSPk, sSPv, sSPs, sSTP, sPW]
    alldma += [s for pr in ring_sems for s in pr]
    alldma += sLK + sLV + sLO + sRP + sRX + sOU
    for s in alldma:
        Pd.wait("sync", s)
    for k in sAG:
        Pd.wait("sync", sAG[k], sAG[k].n)
    for dout, din in dbg_pairs:
        Pd.dma("sync", sSY, dout.ap(), din.ap())
    Pd.wait("sync", sSY)
    Pd.run(nc, "phDrain")

    es.close()
    return nc


def _host_consts():
    s = _slopes().astype(np.float64)
    i = np.arange(CH, dtype=np.float64)
    qd = np.exp(-s[:, None] * (i[None, :] + 1.0))
    kd = np.exp(-s[None, :] * (CH - 1.0 - i[:, None]))
    diff = i[None, :] - i[:, None]
    dd = np.where(diff >= 0,
                  np.exp(-s[:, None, None] * diff[None, :, :]), 0.0)
    return dict(
        ident=np.eye(128),
        ones_col=np.ones((128, 1)),
        ones_row=np.ones((1, 128), dtype=np.float32),
        iota_fm=np.tile(np.arange(CH, dtype=np.float32)[None, :], (128, H)),
        iota_col=np.arange(CH, dtype=np.float32)[:, None].copy(),
        qd_fm=np.tile(qd.reshape(1, H * CH), (128, 1)).astype(np.float32),
        kd_tm=np.repeat(kd, E, axis=1).astype(np.float32),
        ddt=dd.transpose(1, 0, 2).reshape(CH, H * CH).astype(np.float32),
    )


def make_inputs(core, consts, L0_attn_x, L0_attn_w_qkv, L0_attn_w_proj,
                L0_attn_theta, L0_ffn_wq_gate, L0_ffn_s_gate, L0_ffn_wq_up,
                L0_ffn_s_up, L0_ffn_wq_down, L0_ffn_s_down):
    import ml_dtypes
    bf = ml_dtypes.bfloat16
    b, half = core // 2, core % 2
    c = core
    xq = np.asarray(L0_attn_x)
    wqkv = np.asarray(L0_attn_w_qkv)
    wproj = np.asarray(L0_attn_w_proj)
    theta = np.asarray(L0_attn_theta).reshape(H, E)
    wg = np.asarray(L0_ffn_wq_gate)
    wu = np.asarray(L0_ffn_wq_up)
    wd = np.asarray(L0_ffn_wq_down)
    wq = wqkv[0:D]
    wk = wqkv[D:2*D]
    wv = wqkv[2*D:3*D]
    fv, kh = c // 2, c % 2
    gpad = np.zeros((NJGP * 128, D), np.int32)
    gpad[:HID] = wg
    upad = np.zeros((NJGP * 128, D), np.int32)
    upad[:HID] = wu
    dpad = np.zeros((D, NKDP * 128), np.int32)
    dpad[:, :HID] = wd
    return {
        "x": np.ascontiguousarray(xq[b, half*TPC:(half+1)*TPC, :],
                                  dtype=np.float32),
        "wq_sh": np.ascontiguousarray(wq[c*256:(c+1)*256], dtype=np.float32),
        "wk_sh": np.ascontiguousarray(wk[c*256:(c+1)*256], dtype=np.float32),
        "wv_sh": np.ascontiguousarray(
            wv[fv*512:(fv+1)*512, kh*1024:(kh+1)*1024], dtype=np.float32),
        "wkt_sh": np.ascontiguousarray(
            wk[fv*512:(fv+1)*512, kh*1024:(kh+1)*1024], dtype=np.float32),
        "wp_sh": np.ascontiguousarray(
            wproj[fv*512:(fv+1)*512, kh*1024:(kh+1)*1024], dtype=np.float32),
        "wg_sh": np.ascontiguousarray(gpad[c*768:(c+1)*768]),
        "wu_sh": np.ascontiguousarray(upad[c*768:(c+1)*768]),
        "wd_sh": np.ascontiguousarray(dpad[:, c*768:(c+1)*768]),
        "sg_t": np.ascontiguousarray(
            np.asarray(L0_ffn_s_gate).reshape(NJG, 128).T, dtype=np.float32),
        "su_t": np.ascontiguousarray(
            np.asarray(L0_ffn_s_up).reshape(NJG, 128).T, dtype=np.float32),
        "sd_row": np.ascontiguousarray(
            np.asarray(L0_ffn_s_down).reshape(1, D), dtype=np.float32),
        "theta_tm": np.ascontiguousarray(theta.reshape(1, D),
                                         dtype=np.float32),
        "theta_t": np.ascontiguousarray(theta.T, dtype=np.float32),
        "ident": consts["ident"].astype(bf),
        "ones_col": consts["ones_col"].astype(bf),
        "ones_row": consts["ones_row"],
        "iota_fm": consts["iota_fm"],
        "iota_col": consts["iota_col"],
        "pos_col": np.full((128, 1), float(half * TPC), np.float32),
        "mask_col": np.full((128, 1), float(half), np.float32),
        "qd_fm": consts["qd_fm"],
        "kd_tm": consts["kd_tm"],
        "ddt": consts["ddt"],
    }


DEV_PHASES = ("w", "p1", "p2a", "p2b")
DEV_DEBUG = ()


def kernel(**inputs):
    consts = _host_consts()
    in_maps = [make_inputs(c, consts, **inputs) for c in range(8)]
    nc = build_program(debug=DEV_DEBUG, phases=DEV_PHASES)
    res = run_bass_kernel_spmd(nc, in_maps, core_ids=list(range(8)))
    out = np.empty((B, T, D), np.float32)
    for c in range(8):
        b, half = c // 2, c % 2
        out[b, half*TPC:(half+1)*TPC, :] = res.results[c]["out"]
    return out
